# version 09
# baseline (speedup 1.0000x reference)
import numpy as np
from contextlib import ExitStack

import concourse.bass as bass
import concourse.mybir as mybir
from concourse.bass_utils import run_bass_kernel_spmd

F32 = mybir.dt.float32
BF16 = mybir.dt.bfloat16
AF = mybir.ActivationFunctionType
ALU = mybir.AluOpType
AX = mybir.AxisListType

ENGINES = ("sync", "tensor", "vector", "scalar", "gpsimd")

FULL_CFG = dict(D=1024, H=2048, F=4096, E=8, S=4096, Tc=256, NQ=4)


class Sem:
    def __init__(self, stk, nc, name):
        self.h = stk.enter_context(nc.semaphore(name))
        self.n = 0
        self.name = name


class Ctx:

    def __init__(self, nc, cfg):
        self.nc = nc
        self.cfg = cfg
        self.q = {e: [] for e in ENGINES}
        self.last_wait = {e: {} for e in ENGINES}

    def op(self, eng, fn):
        self.q[eng].append(fn)

    def opi(self, eng, fn, sem, k=1):
        sem.n += k
        def run(e, fn=fn, s=sem.h, k=k):
            return fn(e).then_inc(s, k)
        self.q[eng].append(run)
        return sem.n

    def wait(self, eng, sem, val):
        if val <= 0:
            return
        lw = self.last_wait[eng]
        if lw.get(sem.name, -1) >= val:
            return
        lw[sem.name] = val
        self.q[eng].append(lambda e, s=sem.h, v=val: e.wait_ge(s, v))


def build_program(cfg):
    nc = bass.Bass("TRN2", target_bir_lowering=False, detect_race_conditions=False)
    D, H, F, E, S, Tc, NQ = (cfg[k] for k in ("D", "H", "F", "E", "S", "Tc", "NQ"))
    DC, CC, FC = D // 128, H // 128, F // 128
    G = Tc // 128
    NCH = S // Tc
    OWN0 = NCH // 2
    NOWN = NCH - OWN0
    SH = S // 2
    FQ = F // NQ
    FQC = FQ // 128
    NTT = SH // Tc
    assert NTT == NOWN

    dt_in = lambda n, shp: nc.dram_tensor(n, list(shp), F32, kind="ExternalInput").ap()
    x_seq = dt_in("x_seq", (S, D))
    mask16 = dt_in("mask16", (128, NCH))
    cvecT = dt_in("cvecT", (128, DC))
    rms_row = dt_in("rms_row", (1, D))
    split_row = dt_in("split_row", (1, D))
    bg_row = dt_in("bg_row", (1, E))
    wmod = dt_in("wmod", (D, 6 * D))
    wk = dt_in("wk", (D, H))
    wugg = dt_in("wugg", (D, 3 * H))
    wout = dt_in("wout", (H, D))
    wg = dt_in("wg", (D, E))
    w1 = dt_in("w1", (E, D, F))
    b1T = dt_in("b1T", (E, 128, FC))
    w2 = dt_in("w2", (E, F, D))
    b2 = dt_in("b2", (E, D))

    out = nc.dram_tensor("out", [SH, D], F32, kind="ExternalOutput").ap()

    hin_b = nc.dram_tensor("hin_b", [SH, D], F32, kind="Internal").ap()
    x1_b = nc.dram_tensor("x1_b", [SH, D], F32, kind="Internal").ap()
    xnT_b = nc.dram_tensor("xnT_b", [NOWN, 128, DC, Tc], BF16, kind="Internal").ap()
    S_b = nc.dram_tensor("S_b", [NOWN, 128, CC, Tc], BF16, kind="Internal").ap()
    mod_d = nc.dram_tensor("mod_d", [1, 6 * D], F32, kind="Internal").ap()
    w_d = nc.dram_tensor("w_d", [E, SH], F32, kind="Internal").ap()

    ctx = Ctx(nc, cfg)
    top = ExitStack()
    sb = lambda n, shp, dt=F32: top.enter_context(nc.sbuf_tensor(n, list(shp), dt))
    ps = lambda n, shp, dt=F32: top.enter_context(nc.psum_tensor(n, list(shp), dt))

    ident_f = sb("ident_f", (128, 128))
    ident_b = sb("ident_b", (128, 128), BF16)
    wa = sb("wa", (128, DC * H), BF16)
    wb = sb("wb", (128, 2 * DC * H), BF16)
    bc3 = sb("bc3", (128, 3, D))
    bg_bc = sb("bg_bc", (128, E))
    mask_sb = sb("mask_sb", (128, NCH))
    cvec_sb = sb("cvec_sb", (128, DC))
    siluc_bf = sb("siluc_bf", (128, DC), BF16)
    eps5 = sb("eps5", (128, 1))
    stat = sb("stat", (128, 16))
    tailK = sb("tailK", (128, CC))
    carry = sb("carry", (128, CC))

    wk_v = wa[:].rearrange("p (c h) -> p c h", c=DC)
    h2T = wa[:].rearrange("p (c t) -> p c t", c=DC)
    wb_flat = wb[:]
    wmod_v = wb_flat.rearrange("p (c f) -> p c f", c=DC)
    ugg_v = wb_flat.rearrange("p (c f) -> p c f", c=DC)
    gout_v = wb_flat[:, 0:DC * H].rearrange("p (c h) -> p c h", c=DC)
    wout_v = wb_flat[:, DC * H:2 * DC * H].rearrange("p (c d) -> p c d", c=CC)
    QW = DC * FQ
    assert QW == FQC * D
    w1q_v = [wb_flat[:, i * QW:(i + 1) * QW].rearrange("p (c f) -> p c f", c=DC) for i in range(2)]
    w2q_v = [wb_flat[:, (2 + i) * QW:(3 + i) * QW].rearrange("p (c d) -> p c d", c=FQC) for i in range(2)]

    ps_y = ps("ps_y", (128, 8, Tc))
    ps_h = ps("ps_h", (128, 2, 512))
    ps_tf = ps("ps_tf", (128, 512))
    ps_t = ps("ps_t", (128, 1024), BF16)
    ps_g = ps_y[:, 0, 0:E]

    sstk = ExitStack()
    sems = {}
    for name in ("sT", "sV", "sS", "sG", "sod"):
        sems[name] = Sem(sstk, nc, name)
    sT, sV, sS, sG, sod = (sems[k] for k in ("sT", "sV", "sS", "sG", "sod"))
    sxp = [Sem(sstk, nc, "sx%d" % i) for i in range(2)]
    sbop = [Sem(sstk, nc, "sbo%d" % i) for i in range(2)]
    soutp = [Sem(sstk, nc, "sout%d" % i) for i in range(2)]

    op, opi, wait = ctx.op, ctx.opi, ctx.wait

    class GQ:
        def __init__(self, n=4):
            self.sems = [Sem(sstk, nc, "sgd%d" % i) for i in range(n)]
            self.i = 0

        def dma(self, fn):
            s = self.sems[self.i % len(self.sems)]
            self.i += 1
            if s.n > 0:
                wait("gpsimd", s, s.n)
            val = opi("gpsimd", fn, s, 16)
            return (s, val)

    gq = GQ()

    def wait_tok(eng, tok):
        wait(eng, tok[0], tok[1])

    op("gpsimd", lambda g: g.memset(ident_f[:], 0.0))
    e_ms = opi("gpsimd", lambda g: g.memset(ident_b[:], 0.0), sG)
    wait("gpsimd", sG, e_ms)
    op("gpsimd", lambda g: g.affine_select(out=ident_f[:], in_=ident_f[:],
        compare_op=ALU.not_equal, fill=1.0, base=0, pattern=[[-1, 128]], channel_multiplier=1))
    op("gpsimd", lambda g: g.affine_select(out=ident_b[:], in_=ident_b[:],
        compare_op=ALU.not_equal, fill=1.0, base=0, pattern=[[-1, 128]], channel_multiplier=1))
    op("gpsimd", lambda g: g.memset(eps5[:], 1e-5))
    op("gpsimd", lambda g: g.memset(tailK[:], 0.0))
    op("gpsimd", lambda g: g.memset(carry[:], 0.0))
    e_const = opi("gpsimd", lambda g: g.engine_nop(), sG)

    opi("sync", lambda e: e.dma_start(out=mask_sb[:], in_=mask16), sxp[0], 16)
    ld0 = (sxp[0], opi("sync", lambda e: e.dma_start(out=cvec_sb[:], in_=cvecT), sxp[0], 16))
    e_wk = gq.dma(lambda g: g.dma_start(
        out=wk_v, in_=wk.rearrange("(c p) h -> p c h", p=128)))
    HF = 3 * D
    e_wm0 = gq.dma(lambda g: g.dma_start(
        out=wmod_v[:, :, 0:HF], in_=wmod[:, 0:HF].rearrange("(c p) m -> p c m", p=128)))

    wait_tok("scalar", ld0)
    e_silu = opi("scalar", lambda s: s.activation(out=cvec_sb[:], in_=cvec_sb[:], func=AF.Silu), sS)
    wait("vector", sS, e_silu)
    e_silub = opi("vector", lambda v: v.tensor_copy(out=siluc_bf[:], in_=cvec_sb[:]), sV)

    NGH = (HF + 511) // 512
    GW = HF // NGH
    assert GW * NGH == HF and GW <= 512
    wait_tok("tensor", e_wm0)
    wait("tensor", sV, e_silub)
    mm_done = []
    for half in range(2):
        if half == 1:
            wait("gpsimd", sT, mm_done[-1])
            e_wm1 = gq.dma(lambda g: g.dma_start(
                out=wmod_v[:, :, 0:HF],
                in_=wmod[:, HF:6 * D].rearrange("(c p) m -> p c m", p=128)))
            wait_tok("tensor", e_wm1)
        for j in range(NGH):
            if mm_done:
                wait("tensor", sV, e_row)
            def mmrow(t, j=j):
                mm = None
                for dc in range(DC):
                    mm = t.matmul(ps_tf[0:1, 0:GW], siluc_bf[:, dc:dc + 1],
                                  wmod_v[:, dc, j * GW:(j + 1) * GW],
                                  start=(dc == 0), stop=(dc == DC - 1))
                return mm
            ev = opi("tensor", mmrow, sT)
            mm_done.append(ev)
            col0 = half * HF + j * GW
            wait("vector", sT, ev)
            if sod.n:
                wait("vector", sod, sod.n)
            e_row = opi("vector", lambda v: v.tensor_copy(
                out=mod_sb[0:1, 0:GW], in_=ps_tf[0:1, 0:GW]), sV)
            wait("sync", sV, e_row)
            e_dd = opi("sync", lambda e, c0=col0: e.dma_start(
                out=mod_d[0:1, c0:c0 + GW], in_=mod_sb[0:1, 0:GW]), sod, 16)
    e_modd = (sod, e_dd)

    def bcast(dst_ap, src_t, src_off, n):
        src = bass.AP(src_t.tensor, src_t.offset + src_off, [[0, 128], [1, n]])
        return gq.dma(lambda g, d=dst_ap, s=src: g.dma_start(out=d, in_=s))

    wait_tok("gpsimd", e_modd)
    e_ca = bcast(bc3[:, 0, :], mod_d, 1 * D, D)
    e_sa = bcast(bc3[:, 1, :], mod_d, 0 * D, D)
    e_rm = bcast(bc3[:, 2, :], rms_row, 0, D)
    e_bg = bcast(bg_bc[:], bg_row, 0, E)

    stkA = ExitStack()
    sbA = lambda n, shp, dt=F32: stkA.enter_context(nc.sbuf_tensor(n, list(shp), dt))
    x_tok = [sbA("x_tok%d" % i, (128, G, D)) for i in range(2)]
    scrap = [sbA("scrap%d" % i, (128, G, D)) for i in range(2)]
    hin_t = sbA("hin_t", (128, G, D))
    hin = [hin_t, hin_t]
    xg_tok = [sbA("xg_tok%d" % i, (128, G, D), BF16) for i in range(2)]
    xgT = [sbA("xgT%d" % i, (128, DC, Tc), BF16) for i in range(2)]
    Gibuf = [sbA("Gibuf%d" % i, (128, Tc)) for i in range(2)]
    S_chunk_t = sbA("S_chunk", (128, CC, Tc), BF16)
    S_chunk = [S_chunk_t, S_chunk_t]
    wg_bf = sbA("wg_bf", (128, DC, E), BF16)
    wfin = sbA("wfin", (128, E))
    srt8 = sbA("srt8", (128, E))
    gz = sbA("gz", (128, E))
    wT_sb = sbA("wT_sb", (E, SH))
    stkP1 = ExitStack()
    sbP1 = lambda n, shp, dt=F32: stkP1.enter_context(nc.sbuf_tensor(n, list(shp), dt))
    mod_sb = sbP1("mod_sb", (1, 512))
    Kbuf = [sbP1("Kbuf%d" % i, (128, Tc + 1)) for i in range(2)]
    T2buf = [sbP1("T2buf%d" % i, (128, Tc)) for i in range(2)]
    Ubuf = [sbP1("Ubuf%d" % i, (128, Tc)) for i in range(2)]
    Sbuf = [sbP1("Sbuf%d" % i, (128, Tc)) for i in range(2)]

    e_sp = bcast(x_tok[0][:, 0, :], split_row, 0, D)
    for tok in (e_ca, e_sa, e_rm, e_sp):
        wait_tok("vector", tok)
    e_rsss = opi("vector", lambda v: v.tensor_mul(out=bc3[:, 2, :], in0=bc3[:, 2, :], in1=x_tok[0][:, 0, :]), sV)
    e_ca1 = opi("vector", lambda v: v.tensor_scalar(out=bc3[:, 0, :], in0=bc3[:, 0, :],
        scalar1=1.0, scalar2=None, op0=ALU.add), sV)
    P1_STATICS_V = e_ca1

    wait("gpsimd", sT, mm_done[-1])
    e_ug = gq.dma(lambda g: g.dma_start(
        out=ugg_v, in_=wugg[:, 0:2 * H].rearrange("(c p) m -> p c m", p=128)))

    chV = [0] * NCH
    chS = [0] * NCH
    chT = [0] * NCH
    chBO = [None] * (NCH + 1)
    hinBO = [None] * (NCH + 1)
    statV = [0] * NCH
    hinS = [0] * NCH
    lnV = [0] * NCH
    lnS = [0] * NCH
    psV = {}
    psS = {}
    ccV = {}
    ccG = {}
    xload = [0] * NCH
    LN = {}

    def ln_part(t):
        par = t % 2
        if t < 2:
            wait("sync", sV, P1_STATICS_V)
        if t >= 2:
            wait("sync", sV, lnV[t - 2])
            wait("sync", sS, lnS[t - 2])
        xs = x_seq[t * Tc:(t + 1) * Tc, :].rearrange("(g p) d -> p g d", p=128)
        xload[t] = (sxp[par], opi("sync", lambda e, d=x_tok[par], s=xs: e.dma_start(out=d[:], in_=s), sxp[par], 16))
        yield
        wait_tok("scalar", xload[t])
        if t >= 1:
            wait("scalar", sV, statV[t - 1])
        if t >= 2:
            wait("scalar", sV, lnV[t - 2])
        for g in range(G):
            e_sq = opi("scalar", lambda s, g=g, p=par: s.activation(
                out=scrap[p][:, g, :], in_=x_tok[p][:, g, :], func=AF.Square,
                accum_out=stat[:, 2 + g:3 + g]), sS)
        wait_tok("vector", xload[t])
        wait("vector", sV, P1_STATICS_V)
        for g in range(G):
            e_s1 = opi("vector", lambda v, g=g, p=par: v.reduce_sum(
                stat[:, 0 + g:1 + g], x_tok[p][:, g, :], axis=AX.X), sV)
        yield
        wait("vector", sV, e_s1)
        wait("vector", sS, e_sq)
        for g in range(G):
            op("vector", lambda v, g=g: v.tensor_scalar(out=stat[:, 4 + g:5 + g],
                in0=stat[:, 0 + g:1 + g], scalar1=1.0 / D, scalar2=None, op0=ALU.mult))
        for g in range(G):
            op("vector", lambda v, g=g: v.tensor_scalar(out=stat[:, 6 + g:7 + g],
                in0=stat[:, 2 + g:3 + g], scalar1=1.0 / D, scalar2=None, op0=ALU.mult))
        for g in range(G):
            op("vector", lambda v, g=g: v.tensor_mul(out=stat[:, 8 + g:9 + g],
                in0=stat[:, 4 + g:5 + g], in1=stat[:, 4 + g:5 + g]))
        for g in range(G):
            e_var = opi("vector", lambda v, g=g: v.tensor_sub(out=stat[:, 6 + g:7 + g],
                in0=stat[:, 6 + g:7 + g], in1=stat[:, 8 + g:9 + g]), sV)
        statV[t] = e_var
        wait("scalar", sV, e_var)
        for g in range(G):
            e_std = opi("scalar", lambda s, g=g: s.activation(out=stat[:, 8 + g:9 + g],
                in_=stat[:, 6 + g:7 + g], func=AF.Sqrt, bias=eps5[:]), sS)
        yield
        wait("vector", sS, e_std)
        for g in range(G):
            e_rst = opi("vector", lambda v, g=g: v.reciprocal(out=stat[:, 10 + g:11 + g],
                in_=stat[:, 8 + g:9 + g]), sV)
        wait("vector", sV, e_rst)
        for g in range(G):
            op("vector", lambda v, g=g, p=par: v.tensor_scalar(out=scrap[p][:, g, :],
                in0=x_tok[p][:, g, :], scalar1=stat[:, 4 + g:5 + g], scalar2=stat[:, 10 + g:11 + g],
                op0=ALU.subtract, op1=ALU.mult))
        yield
        if t >= 1:
            if hinBO[t - 1]:
                wait_tok("vector", hinBO[t - 1])
            wait("vector", sS, hinS[t - 1])
        for g in range(G):
            op("vector", lambda v, g=g, p=par: v.tensor_mul(out=hin[p][:, g, :],
                in0=scrap[p][:, g, :], in1=bc3[:, 0, :]))
        for g in range(G):
            e_hin = opi("vector", lambda v, g=g, p=par: v.tensor_add(out=hin[p][:, g, :],
                in0=hin[p][:, g, :], in1=bc3[:, 1, :]), sV)
        yield
        if t >= OWN0:
            oc = t - OWN0
            wait("sync", sV, e_hin)
            hinBO[t] = (sbop[par], opi("sync", lambda e, p=par, oc=oc: e.dma_start(
                out=hin_b[oc * Tc:(oc + 1) * Tc, :].rearrange("(g p) d -> p g d", p=128),
                in_=hin[p][:]), sbop[par], 16))
        wait("scalar", sV, e_hin)
        for g in range(G):
            e_rsq = opi("scalar", lambda s, g=g, p=par: s.activation(
                out=scrap[p][:, g, :], in_=hin[p][:, g, :], func=AF.Square,
                accum_out=stat[:, 12 + g:13 + g]), sS)
        hinS[t] = e_rsq
        wait("scalar", sS, e_rsq)
        for g in range(G):
            e_nrm = opi("scalar", lambda s, g=g: s.activation(out=stat[:, 14 + g:15 + g],
                in_=stat[:, 12 + g:13 + g], func=AF.Sqrt, scale=1.0 / D), sS)
        yield
        wait("vector", sS, e_nrm)
        for g in range(G):
            op("vector", lambda v, g=g: v.tensor_scalar(out=stat[:, 12 + g:13 + g],
                in0=stat[:, 14 + g:15 + g], scalar1=1e-8, scalar2=None, op0=ALU.add))
        for g in range(G):
            e_rr = opi("vector", lambda v, g=g: v.reciprocal(out=stat[:, 14 + g:15 + g],
                in_=stat[:, 12 + g:13 + g]), sV)
        wait("vector", sV, e_rr)
        for g in range(G):
            op("vector", lambda v, g=g, p=par: v.tensor_scalar(out=scrap[p][:, g, :],
                in0=hin[p][:, g, :], scalar1=stat[:, 14 + g:15 + g], scalar2=None, op0=ALU.mult))
        for g in range(G):
            e_xn = opi("vector", lambda v, g=g, p=par: v.tensor_mul(out=xg_tok[p][:, g, :],
                in0=scrap[p][:, g, :], in1=bc3[:, 2, :]), sV)
        LN[t] = dict(e_hin=e_hin, e_xn=e_xn)
        lnV[t] = sV.n
        lnS[t] = sS.n

    def ln_transpose(t):
        par = t % 2
        e_xn = LN[t]["e_xn"]
        wait("tensor", sV, e_xn)
        if t >= 2:
            if chBO[t - 2]:
                wait_tok("tensor", chBO[t - 2])
                wait_tok("vector", chBO[t - 2])
            wait("vector", sT, chT[t - 2])
        n_tp = G * DC
        e_xnT = None
        for b0 in range(0, n_tp, 4):
            for k in range(4):
                idx = b0 + k
                g, dc = idx // DC, idx % DC
                e_tp = opi("tensor", lambda tt, g=g, dc=dc, k=k, p=par: tt.transpose(
                    ps_t[:, k * 128:(k + 1) * 128], xg_tok[p][:, g, dc * 128:(dc + 1) * 128],
                    ident_b[:]), sT)
            wait("vector", sT, e_tp)
            g0, dc0 = b0 // DC, b0 % DC
            if dc0 + 4 <= DC:
                e_xnT = opi("vector", lambda v, g=g0, dc=dc0, p=par: v.tensor_copy(
                    out=xgT[p][:, dc:dc + 4, g * 128:(g + 1) * 128],
                    in_=ps_t[:, 0:512].rearrange("p (k t) -> p k t", k=4)), sV)
            else:
                for k in range(4):
                    idx = b0 + k
                    g, dc = idx // DC, idx % DC
                    e_xnT = opi("vector", lambda v, g=g, dc=dc, k=k, p=par: v.tensor_copy(
                        out=xgT[p][:, dc, g * 128:(g + 1) * 128],
                        in_=ps_t[:, k * 128:(k + 1) * 128]), sV)
            if b0 + 4 < n_tp:
                wait("tensor", sV, e_xnT)
        LN[t]["e_xnT"] = e_xnT
        if t >= OWN0:
            oc = t - OWN0
            wait("sync", sV, e_xnT)
            chBO[t] = (sbop[par], opi("sync", lambda e, p=par, oc=oc: e.dma_start(
                out=xnT_b[oc], in_=xgT[p][:]), sbop[par], 16))

    def cc_part(t):
        par = t % 2
        own = t >= OWN0
        e_xnT = LN[t]["e_xnT"]
        e_hin = LN[t]["e_hin"]
        wait_tok("tensor", e_ug)
        wait_tok("tensor", e_wk)
        for cc in range(CC):
            cp = cc % 2
            slot_k, slot_u, slot_gi = 4 * cp, 4 * cp + 1, 4 * cp + 2
            if (t, cc) > (0, 1):
                pse = psS.get(cp)
                if pse:
                    wait("tensor", sS, pse)
            wait("tensor", sV, e_xnT)
            def proj(tt, p=par, cc=cc, sk=slot_k, su=slot_u, sgi=slot_gi):
                mm = None
                for dc in range(DC):
                    mm = tt.matmul(ps_y[:, sk, :], wk_v[:, dc, cc * 128:(cc + 1) * 128],
                                   xgT[p][:, dc, :], start=(dc == 0), stop=(dc == DC - 1))
                for dc in range(DC):
                    mm = tt.matmul(ps_y[:, su, :], ugg_v[:, dc, cc * 128:(cc + 1) * 128],
                                   xgT[p][:, dc, :], start=(dc == 0), stop=(dc == DC - 1))
                for dc in range(DC):
                    mm = tt.matmul(ps_y[:, sgi, :], ugg_v[:, dc, H + cc * 128:H + (cc + 1) * 128],
                                   xgT[p][:, dc, :], start=(dc == 0), stop=(dc == DC - 1))
                return mm
            e_mm = opi("tensor", proj, sT)
            wait("scalar", sT, e_mm)
            if cp in ccV:
                wait("scalar", sV, ccV[cp])
            if cp in ccG:
                wait("scalar", sG, ccG[cp])
            e_sk = opi("scalar", lambda s, cp=cp, sk=slot_k: s.activation(
                out=Kbuf[cp][:, 1:Tc + 1], in_=ps_y[:, sk, :], func=AF.Sigmoid), sS)
            e_um = opi("scalar", lambda s, cp=cp, su=slot_u, t=t: s.activation(
                out=T2buf[cp][:], in_=ps_y[:, su, :], func=AF.Copy,
                scale=mask_sb[:, t:t + 1]), sS)
            e_sg = opi("scalar", lambda s, cp=cp, sgi=slot_gi: s.activation(
                out=Gibuf[cp][:], in_=ps_y[:, sgi, :], func=AF.Sigmoid), sS)
            e_km = opi("scalar", lambda s, cp=cp, sk=slot_k: s.activation(
                out=Ubuf[cp][:], in_=ps_y[:, sk, :], func=AF.Sigmoid,
                scale=-1.0), sS)
            psS[cp] = e_km
            wait("gpsimd", sS, e_km)
            op("gpsimd", lambda g, cp=cp: g.tensor_tensor(out=T2buf[cp][:],
                in0=T2buf[cp][:], in1=Gibuf[cp][:], op=ALU.mult))
            e_ut = opi("gpsimd", lambda g, cp=cp: g.tensor_tensor(out=T2buf[cp][:],
                in0=T2buf[cp][:], in1=Ubuf[cp][:], op=ALU.mult), sG)
            ccG[cp] = e_ut
            wait("vector", sS, e_sk)
            op("vector", lambda v, cp=cp, cc=cc: v.tensor_copy(out=Kbuf[cp][:, 0:1],
                in_=tailK[:, cc:cc + 1]))
            e_tl = opi("vector", lambda v, cp=cp, cc=cc: v.tensor_copy(
                out=tailK[:, cc:cc + 1], in_=Kbuf[cp][:, Tc:Tc + 1]), sV)
            wait("vector", sG, e_ut)
            e_scan = opi("vector", lambda v, cp=cp, cc=cc: v.tensor_tensor_scan(
                out=Sbuf[cp][:], data0=Kbuf[cp][:, 0:Tc], data1=T2buf[cp][:],
                initial=carry[:, cc:cc + 1], op0=ALU.mult, op1=ALU.add), sV)
            wait("vector", sV, e_scan)
            e_cr = opi("vector", lambda v, cp=cp, cc=cc: v.tensor_copy(
                out=carry[:, cc:cc + 1], in_=Sbuf[cp][:, Tc - 1:Tc]), sV)
            if own:
                if cc == 0 and t - 1 >= OWN0 and chBO[t - 1]:
                    wait_tok("vector", chBO[t - 1])
                e_sS = opi("vector", lambda v, cp=cp, cc=cc, p=par: v.tensor_copy(
                    out=S_chunk[p][:, cc, :], in_=Sbuf[cp][:]), sV)
            ccV[cp] = sV.n
            yield
        chV[t] = sV.n
        chS[t] = sS.n
        chT[t] = sT.n
        if own:
            oc = t - OWN0
            wait("sync", sV, sV.n)
            bo = opi("sync", lambda e, p=par, oc=oc: e.dma_start(
                out=S_b[oc], in_=S_chunk[p][:]), sbop[par], 16)
            chBO[t] = (sbop[par], bo)

    for _ in ln_part(0):
        pass
    ln_transpose(0)
    for _ in ln_part(1):
        pass
    for t in range(NCH):
        gc = cc_part(t)
        gl = ln_part(t + 2) if t + 2 < NCH else None
        for _ in gc:
            if gl is not None:
                try:
                    next(gl)
                except StopIteration:
                    gl = None
        while gl is not None:
            try:
                next(gl)
            except StopIteration:
                gl = None
        if t + 1 < NCH:
            ln_transpose(t + 1)

    P1_T_END = sT.n
    P1_V_END = sV.n
    P1_S_END = sS.n

    wait("gpsimd", sT, P1_T_END)
    e_go = gq.dma(lambda g: g.dma_start(
        out=gout_v, in_=wugg[:, 2 * H:3 * H].rearrange("(c p) m -> p c m", p=128)))
    e_wo = gq.dma(lambda g: g.dma_start(
        out=wout_v, in_=wout.rearrange("(c p) d -> p c d", p=128)))
    e_wg = gq.dma(lambda g: g.dma_start(
        out=wg_bf[:], in_=wg.rearrange("(c p) e -> p c e", p=128)))
    e_cb = bcast(bc3[:, 0, :], mod_d, 4 * D, D)
    e_sb2 = bcast(bc3[:, 1, :], mod_d, 3 * D, D)
    e_ga = bcast(bc3[:, 2, :], mod_d, 2 * D, D)
    wait("vector", sV, P1_V_END)
    wait("vector", sS, P1_S_END)
    wait("vector", sT, P1_T_END)
    for tok in (e_cb, e_sb2, e_ga):
        wait_tok("vector", tok)
    e_cb1 = opi("vector", lambda v: v.tensor_scalar(out=bc3[:, 0, :], in0=bc3[:, 0, :],
        scalar1=1.0, scalar2=None, op0=ALU.add), sV)
    op("vector", lambda v: v.tensor_scalar(out=scrap[0][:, 0, :], in0=bc3[:, 2, :],
        scalar1=0.0, scalar2=None, op0=ALU.is_ge))
    op("vector", lambda v: v.tensor_scalar(out=scrap[0][:, 0, :], in0=scrap[0][:, 0, :],
        scalar1=2.0, scalar2=None, op0=ALU.mult))
    op("vector", lambda v: v.tensor_scalar(out=scrap[0][:, 0, :], in0=scrap[0][:, 0, :],
        scalar1=-1.0, scalar2=None, op0=ALU.add))
    op("vector", lambda v: v.tensor_mul(out=bc3[:, 2, :], in0=bc3[:, 2, :],
        in1=scrap[0][:, 0, :]))
    op("vector", lambda v: v.tensor_scalar_max(bc3[:, 2, :], bc3[:, 2, :], 0.25))
    e_gap = opi("vector", lambda v: v.tensor_mul(out=bc3[:, 2, :], in0=bc3[:, 2, :],
        in1=scrap[0][:, 0, :]), sV)
    P2_STATICS_V = e_gap

    stkP1.close()
    y_sb = sbA("y_sb", (128, DC, Tc))
    x1_t = sbA("x1_t", (128, G, D))
    x1 = [x1_t, x1_t]
    s_all = sbA("s_all", (128, CC, Tc), BF16)

    p2V = [0] * NOWN
    p2T = [0] * NOWN
    p2S = [0] * NOWN
    p2BO = [0] * NOWN
    p2load = [0] * NOWN
    for oc in range(NOWN):
        par = oc % 2
        if oc >= 2:
            wait("sync", sV, p2V[oc - 2])
            wait("sync", sT, p2T[oc - 2])
            wait("sync", sS, p2S[oc - 2])
        else:
            wait("sync", sV, P1_V_END)
            wait("sync", sT, P1_T_END)
            wait("sync", sS, P1_S_END)
            for tok in (chBO[NCH - 1], chBO[NCH - 2]):
                if tok:
                    wait_tok("sync", tok)
        opi("sync", lambda e, p=par, oc=oc: e.dma_start(out=xgT[p][:], in_=xnT_b[oc]), sxp[par], 16)
        if oc >= 1:
            wait("sync", sV, p2Sdone[oc - 1])
            wait("sync", sV, p2hin[oc - 1])
            wait("sync", sS, p2S[oc - 1])
        opi("sync", lambda e, p=par, oc=oc: e.dma_start(out=S_chunk[p][:], in_=S_b[oc]), sxp[par], 16)
        opi("sync", lambda e, p=par, oc=oc: e.dma_start(
            out=hin[p][:], in_=hin_b[oc * Tc:(oc + 1) * Tc, :].rearrange("(g p) d -> p g d", p=128)), sxp[par], 16)
        ln = opi("sync", lambda e, p=par, oc=oc: e.dma_start(
            out=x_tok[p][:], in_=x_seq[(OWN0 + oc) * Tc:(OWN0 + oc + 1) * Tc, :]
            .rearrange("(g p) d -> p g d", p=128)), sxp[par], 16)
        p2load[oc] = (sxp[par], ln)

        wait_tok("tensor", e_wo)
        wait_tok("tensor", e_go)
        wait_tok("tensor", p2load[oc])
        if oc >= 1:
            wait("vector", sT, p2sT[oc - 1])
        for cc in range(CC):
            cp = cc % 2
            if cc >= 2:
                wait("tensor", sS, e_go_evac[cc - 2])
            def gproj(tt, p=par, cc=cc, cp=cp):
                mm = None
                for dc in range(DC):
                    mm = tt.matmul(ps_h[:, cp, 0:Tc], gout_v[:, dc, cc * 128:(cc + 1) * 128],
                                   xgT[p][:, dc, :], start=(dc == 0), stop=(dc == DC - 1))
                return mm
            e_gmm = opi("tensor", gproj, sT)
            wait("scalar", sT, e_gmm)
            if cc >= 2:
                wait("scalar", sV, e_s_done[cc - 2])
            if cc == 0:
                e_go_evac, e_s_done = {}, {}
            e_go_evac[cc] = opi("scalar", lambda s, cp=cp: s.activation(
                out=Gibuf[cp][:], in_=ps_h[:, cp, 0:Tc], func=AF.Sigmoid), sS)
            wait("vector", sS, e_go_evac[cc])
            e_s_done[cc] = opi("vector", lambda v, cp=cp, cc=cc, p=par: v.tensor_tensor(
                out=s_all[:, cc, :], in0=S_chunk[p][:, cc, :], in1=Gibuf[cp][:],
                op=ALU.mult), sV)
        if oc == 0:
            p2Sdone, p2hin = {}, {}
        p2Sdone[oc] = e_s_done[CC - 1]
        wait("tensor", sV, e_s_done[CC - 1])
        if oc >= 1:
            wait("tensor", sV, p2V[oc - 1])
        def ymm(tt):
            mm = None
            for dd in range(DC):
                for cc in range(CC):
                    mm = tt.matmul(ps_y[:, dd, :], wout_v[:, cc, dd * 128:(dd + 1) * 128],
                                   s_all[:, cc, :], start=(cc == 0), stop=(cc == CC - 1))
            return mm
        e_ymm = opi("tensor", ymm, sT)
        if oc == 0:
            p2sT = {}
        p2sT[oc] = e_ymm

        wait("vector", sT, e_ymm)
        for dd in range(DC):
            e_ye = opi("vector", lambda v, dd=dd: v.tensor_copy(
                out=y_sb[:, dd, :], in_=ps_y[:, dd, :]), sV)
        wait("tensor", sV, e_ye)
        wait("vector", sV, P2_STATICS_V)
        if oc >= 1:
            wait_tok("vector", p2BO[oc - 1])
        e_x1 = None
        BW = min(4, DC)
        W = BW * 128
        for g in range(G):
            for hdc in range(0, DC, BW):
                for k in range(BW):
                    dd = hdc + k
                    e_yt = opi("tensor", lambda tt, dd=dd, g=g, k=k: tt.transpose(
                        ps_tf[:, k * 128:(k + 1) * 128], y_sb[:, dd, g * 128:(g + 1) * 128],
                        ident_f[:]), sT)
                wait("vector", sT, e_yt)
                c0 = hdc * 128
                op("vector", lambda v, g=g, c0=c0, W=W, p=par: v.tensor_add(
                    out=scrap[p][:, g, c0:c0 + W], in0=ps_tf[:, 0:W],
                    in1=hin[p][:, g, c0:c0 + W]))
                op("vector", lambda v, g=g, c0=c0, W=W, p=par: v.tensor_mul(
                    out=scrap[p][:, g, c0:c0 + W], in0=scrap[p][:, g, c0:c0 + W],
                    in1=bc3[:, 2, c0:c0 + W]))
                e_x1 = opi("vector", lambda v, g=g, c0=c0, W=W, p=par: v.tensor_add(
                    out=x1[p][:, g, c0:c0 + W], in0=scrap[p][:, g, c0:c0 + W],
                    in1=x_tok[p][:, g, c0:c0 + W]), sV)
                if g * DC * 128 + c0 + W < G * D:
                    wait("tensor", sV, e_x1)

        p2hin[oc] = e_x1
        wait("scalar", sV, e_x1)
        for g in range(G):
            e_sq = opi("scalar", lambda s, g=g, p=par: s.activation(
                out=scrap[p][:, g, :], in_=x1[p][:, g, :], func=AF.Square,
                accum_out=stat[:, 2 + g:3 + g]), sS)
        for g in range(G):
            e_s1 = opi("vector", lambda v, g=g, p=par: v.reduce_sum(
                stat[:, 0 + g:1 + g], x1[p][:, g, :], axis=AX.X), sV)
        wait("vector", sV, e_s1)
        wait("vector", sS, e_sq)
        for g in range(G):
            op("vector", lambda v, g=g: v.tensor_scalar(out=stat[:, 4 + g:5 + g],
                in0=stat[:, 0 + g:1 + g], scalar1=1.0 / D, scalar2=None, op0=ALU.mult))
        for g in range(G):
            op("vector", lambda v, g=g: v.tensor_scalar(out=stat[:, 6 + g:7 + g],
                in0=stat[:, 2 + g:3 + g], scalar1=1.0 / D, scalar2=None, op0=ALU.mult))
        for g in range(G):
            op("vector", lambda v, g=g: v.tensor_mul(out=stat[:, 8 + g:9 + g],
                in0=stat[:, 4 + g:5 + g], in1=stat[:, 4 + g:5 + g]))
        for g in range(G):
            e_var = opi("vector", lambda v, g=g: v.tensor_sub(out=stat[:, 6 + g:7 + g],
                in0=stat[:, 6 + g:7 + g], in1=stat[:, 8 + g:9 + g]), sV)
        wait("scalar", sV, e_var)
        for g in range(G):
            e_std = opi("scalar", lambda s, g=g: s.activation(out=stat[:, 8 + g:9 + g],
                in_=stat[:, 6 + g:7 + g], func=AF.Sqrt, bias=eps5[:]), sS)
        wait("vector", sS, e_std)
        for g in range(G):
            e_rst = opi("vector", lambda v, g=g: v.reciprocal(out=stat[:, 10 + g:11 + g],
                in_=stat[:, 8 + g:9 + g]), sV)
        wait("vector", sV, e_rst)
        for g in range(G):
            op("vector", lambda v, g=g, p=par: v.tensor_scalar(out=scrap[p][:, g, :],
                in0=x1[p][:, g, :], scalar1=stat[:, 4 + g:5 + g], scalar2=stat[:, 10 + g:11 + g],
                op0=ALU.subtract, op1=ALU.mult))
        for g in range(G):
            op("vector", lambda v, g=g, p=par: v.tensor_mul(out=scrap[p][:, g, :],
                in0=scrap[p][:, g, :], in1=bc3[:, 0, :]))
        for g in range(G):
            e_h2 = opi("vector", lambda v, g=g, p=par: v.tensor_add(out=xg_tok[p][:, g, :],
                in0=scrap[p][:, g, :], in1=bc3[:, 1, :]), sV)

        wait("tensor", sV, e_h2)
        for b0 in range(0, G * DC, 4):
            for k in range(4):
                idx = b0 + k
                g, dc = idx // DC, idx % DC
                e_tp = opi("tensor", lambda tt, g=g, dc=dc, k=k, p=par: tt.transpose(
                    ps_t[:, k * 128:(k + 1) * 128], xg_tok[p][:, g, dc * 128:(dc + 1) * 128],
                    ident_b[:]), sT)
            wait("vector", sT, e_tp)
            for k in range(4):
                idx = b0 + k
                g, dc = idx // DC, idx % DC
                col = oc * Tc + g * 128
                e_h2T = opi("vector", lambda v, g=g, dc=dc, k=k, col=col: v.tensor_copy(
                    out=h2T[:, dc, col:col + 128], in_=ps_t[:, k * 128:(k + 1) * 128]), sV)
            if b0 + 4 < G * DC:
                wait("tensor", sV, e_h2T)

        wait_tok("tensor", e_wg)
        for g in range(G):
            col = oc * Tc + g * 128
            def gatemm(tt, col=col):
                mm = None
                for dc in range(DC):
                    mm = tt.matmul(ps_g, h2T[:, dc, col:col + 128], wg_bf[:, dc, :],
                                   start=(dc == 0), stop=(dc == DC - 1))
                return mm
            wait("tensor", sV, e_h2T)
            if g > 0 or oc > 0:
                wait("tensor", sV, e_wT_evac)
            e_gl = opi("tensor", gatemm, sT)
            wait("vector", sT, e_gl)
            e_z = opi("vector", lambda v: v.tensor_add(out=gz[:], in0=ps_g, in1=bg_bc[:]), sV)
            wait("vector", sV, e_z)
            e_nm = opi("vector", lambda v: v.reduce_max(stat[:, 0:1], gz[:], axis=AX.X, negate=True), sV)
            wait("scalar", sV, e_nm)
            e_ex = opi("scalar", lambda s: s.activation(out=srt8[:], in_=gz[:], func=AF.Exp,
                bias=stat[:, 0:1], accum_out=stat[:, 1:2]), sS)
            wait("vector", sS, e_ex)
            e_rc = opi("vector", lambda v: v.reciprocal(out=stat[:, 2:3], in_=stat[:, 1:2]), sV)
            wait("vector", sV, e_rc)
            e_gt = opi("vector", lambda v: v.tensor_scalar(out=gz[:], in0=srt8[:],
                scalar1=stat[:, 2:3], scalar2=None, op0=ALU.mult), sV)
            wait("vector", sV, e_gt)
            e_m8 = opi("vector", lambda v: v.max(out=srt8[:], in_=gz[:]), sV)
            wait("vector", sV, e_m8)
            op("vector", lambda v: v.tensor_scalar(out=wfin[:], in0=gz[:],
                scalar1=srt8[:, 1:2], scalar2=None, op0=ALU.subtract))
            op("vector", lambda v: v.tensor_add(out=stat[:, 3:4], in0=srt8[:, 0:1], in1=srt8[:, 1:2]))
            op("vector", lambda v: v.tensor_scalar(out=wfin[:], in0=wfin[:],
                scalar1=0.0, scalar2=None, op0=ALU.is_ge))
            op("vector", lambda v: v.tensor_scalar(out=stat[:, 3:4], in0=stat[:, 3:4],
                scalar1=1e-8, scalar2=None, op0=ALU.add))
            op("vector", lambda v: v.tensor_mul(out=wfin[:], in0=wfin[:], in1=gz[:]))
            e_dn = opi("vector", lambda v: v.reciprocal(out=stat[:, 4:5], in_=stat[:, 3:4]), sV)
            wait("vector", sV, e_dn)
            e_wf = opi("vector", lambda v: v.tensor_scalar(out=wfin[:], in0=wfin[:],
                scalar1=stat[:, 4:5], scalar2=None, op0=ALU.mult), sV)
            wait("tensor", sV, e_wf)
            e_wt = opi("tensor", lambda tt: tt.transpose(ps_tf[0:E, 0:128], wfin[:], ident_f[:]), sT)
            wait("vector", sT, e_wt)
            e_wT_evac = opi("vector", lambda v, col=col: v.tensor_copy(
                out=wT_sb[:, col:col + 128], in_=ps_tf[0:E, 0:128]), sV)

        p2V[oc] = sV.n
        p2T[oc] = sT.n
        p2S[oc] = sS.n
        wait("sync", sV, e_x1)
        p2BO[oc] = (sbop[par], opi("sync", lambda e, p=par, oc=oc: e.dma_start(
            out=x1_b[oc * Tc:(oc + 1) * Tc, :].rearrange("(g p) d -> p g d", p=128),
            in_=x1[p][:]), sbop[par], 16))

    P2_V_END = sV.n
    P2_T_END = sT.n
    wait("sync", sV, P2_V_END)
    e_wd = (sod, opi("sync", lambda e: e.dma_start(out=w_d, in_=wT_sb[:]), sod, 16))

    stkA.close()
    acc = sb("acc", (128, DC, SH))
    p3stk = ExitStack()
    sb3 = lambda n, shp, dt=F32: p3stk.enter_context(nc.sbuf_tensor(n, list(shp), dt))
    w_bc = sb3("w_bc", (128, 2, NTT, Tc), BF16)
    H_all = sb3("H_all", (128, 2, FQC, Tc), BF16)
    tq_sb = [sb3("tq_sb%d" % i, (128, Tc)) for i in range(2)]
    b1_sb = [sb3("b1_sb%d" % i, (128, FC)) for i in range(2)]
    b2_bf = sb3("b2_bf", (E, D), BF16)
    wT_bf = sb3("wT_bf", (E, SH), BF16)
    x1p4_t = sb3("x1p4", (128, G, D))
    x1p4 = [x1p4_t, x1p4_t]
    p4tmp = sb3("p4tmp", (128, 512))

    wait_tok("gpsimd", e_wd)
    e_b2 = gq.dma(lambda g: g.dma_start(out=b2_bf[:], in_=b2))
    e_wtb = gq.dma(lambda g: g.dma_start(out=wT_bf[:], in_=w_d))

    wait_tok("tensor", e_wtb)
    wait_tok("tensor", e_b2)
    wait("tensor", sT, P2_T_END)
    wait("tensor", sV, P2_V_END)
    acc_init_v = 0
    for tt in range(NTT):
        def b2mm(ttn, tt=tt):
            mm = None
            for dd in range(DC):
                mm = ttn.matmul(ps_y[:, dd, :], b2_bf[0:E, dd * 128:(dd + 1) * 128],
                                wT_bf[0:E, tt * Tc:(tt + 1) * Tc],
                                start=True, stop=True)
            return mm
        e_b2mm = opi("tensor", b2mm, sT)
        wait("vector", sT, e_b2mm)
        for dd in range(DC):
            acc_init_v = opi("vector", lambda v, dd=dd, tt=tt: v.tensor_copy(
                out=acc[:, dd, tt * Tc:(tt + 1) * Tc], in_=ps_y[:, dd, :]), sV)
        if tt + 1 < NTT:
            wait("tensor", sV, acc_init_v)

    pairs = [(e, q) for e in range(E) for q in range(NQ)]
    loaded = {}
    w2T_hist = {}
    prev_accum_T = {}
    expV = {}
    b1_used = {}
    ebc = {}
    eb1t = {}
    wait("tensor", sV, acc_init_v)

    def issue_pair_loads(idx):
        e_i, q = pairs[idx]
        wpar = idx % 2
        pt = prev_accum_T.get(wpar)
        wait("gpsimd", sT, pt if pt else P2_T_END)
        lw1 = gq.dma(lambda g, e_i=e_i, q=q, wpar=wpar: g.dma_start(
            out=w1q_v[wpar], in_=w1[e_i][:, q * FQ:(q + 1) * FQ]
            .rearrange("(c p) f -> p c f", p=128)))
        lw2 = gq.dma(lambda g, e_i=e_i, q=q, wpar=wpar: g.dma_start(
            out=w2q_v[wpar], in_=w2[e_i][q * FQ:(q + 1) * FQ, :]
            .rearrange("(c p) d -> p c d", p=128)))
        loaded[idx] = (lw1, lw2)

    def issue_expert_smalls(e_i):
        ep = e_i % 2
        if e_i >= 2:
            wait("gpsimd", sV, expV[e_i - 2])
            wait("gpsimd", sS, b1_used[e_i - 2])
        ebc[e_i] = gq.dma(lambda g, e_i=e_i, ep=ep: g.dma_start(
            out=w_bc[:, ep].rearrange("p a t -> p (a t)"),
            in_=bass.AP(w_d.tensor, e_i * SH, [[0, 128], [1, SH]])))
        eb1t[e_i] = gq.dma(lambda g, e_i=e_i, ep=ep: g.dma_start(
            out=b1_sb[ep][:], in_=b1T[e_i]))

    issue_expert_smalls(0)
    issue_pair_loads(0)
    for idx, (e_i, q) in enumerate(pairs):
        wpar = idx % 2
        ep = e_i % 2
        if idx + 1 < len(pairs):
            if pairs[idx + 1][1] == 0:
                issue_expert_smalls(e_i + 1)
            issue_pair_loads(idx + 1)
        lw1, lw2 = loaded[idx]
        wait_tok("tensor", lw1)
        wait_tok("tensor", lw2)
        if q == 0:
            wait_tok("scalar", eb1t[e_i])
            wait_tok("vector", ebc[e_i])
        for tt in range(NTT):
            tpar = (idx * NTT + tt) % 2
            if idx * NTT + tt >= 2:
                wait("scalar", sT, w2T_hist[(idx * NTT + tt) - 2])
            for fc in range(FQC):
                hp = fc % 2
                if fc >= 2:
                    wait("tensor", sS, e_gelu[fc - 2])
                def w1mm(ttn, fc=fc, hp=hp, wpar=wpar, tt=tt, e_i=e_i):
                    mm = None
                    for dc in range(DC):
                        mm = ttn.matmul(ps_h[:, hp, 0:Tc], w1q_v[wpar][:, dc, fc * 128:(fc + 1) * 128],
                                        h2T[:, dc, tt * Tc:(tt + 1) * Tc],
                                        start=(dc == 0), stop=(dc == DC - 1))
                    return mm
                e_w1 = opi("tensor", w1mm, sT)
                wait("scalar", sT, e_w1)
                if fc == 0:
                    e_gelu = {}
                bcol = q * FQC + fc
                e_gelu[fc] = opi("scalar", lambda s, hp=hp, fc=fc, bcol=bcol, ep=ep, tpar=tpar: s.activation(
                    out=H_all[:, tpar, fc, :], in_=ps_h[:, hp, 0:Tc], func=AF.Gelu_apprx_tanh,
                    bias=b1_sb[ep][:, bcol:bcol + 1]), sS)
                b1_used[e_i] = e_gelu[fc]
            wait("tensor", sS, e_gelu[FQC - 1])
            if idx or tt:
                wait("tensor", sV, prev_tq)
            def w2mm(ttn, wpar=wpar, tpar=tpar):
                mm = None
                for dd in range(DC):
                    for fc in range(FQC):
                        mm = ttn.matmul(ps_y[:, dd, :], w2q_v[wpar][:, fc, dd * 128:(dd + 1) * 128],
                                        H_all[:, tpar, fc, :], start=(fc == 0), stop=(fc == FQC - 1))
                return mm
            e_w2mm = opi("tensor", w2mm, sT)
            w2T_hist[idx * NTT + tt] = e_w2mm
            wait("vector", sT, e_w2mm)
            wait("vector", sV, acc_init_v)
            for dd in range(DC):
                dp = dd % 2
                if dd >= 2:
                    wait("vector", sG, e_acc[dd - 2])
                if dd == 0:
                    e_acc = {}
                e_tq = opi("vector", lambda v, dd=dd, dp=dp, tt=tt, ep=ep: v.tensor_mul(
                    out=tq_sb[dp][:], in0=ps_y[:, dd, :], in1=w_bc[:, ep, tt, :]), sV)
                wait("gpsimd", sV, e_tq)
                e_acc[dd] = opi("gpsimd", lambda g, dd=dd, dp=dp, tt=tt: g.tensor_tensor(
                    out=acc[:, dd, tt * Tc:(tt + 1) * Tc], in0=acc[:, dd, tt * Tc:(tt + 1) * Tc],
                    in1=tq_sb[dp][:], op=ALU.add), sG)
            prev_tq = e_tq
        prev_accum_T[wpar] = sT.n
        if q == NQ - 1:
            expV[e_i] = sV.n
    P3_G_END = sG.n
    P3_V_END = sV.n

    e_gb = bcast(bc3[:, 0, :], mod_d, 5 * D, D)
    wait_tok("vector", e_gb)
    wait("vector", sV, P3_V_END)
    e_gbp = 0
    W4 = min(512, D)
    for c0 in range(0, D, W4):
        op("vector", lambda v, c0=c0: v.tensor_scalar(out=p4tmp[:, 0:W4], in0=bc3[:, 0, c0:c0 + W4],
            scalar1=0.0, scalar2=None, op0=ALU.is_ge))
        op("vector", lambda v, c0=c0: v.tensor_scalar(out=p4tmp[:, 0:W4], in0=p4tmp[:, 0:W4],
            scalar1=2.0, scalar2=None, op0=ALU.mult))
        op("vector", lambda v, c0=c0: v.tensor_scalar(out=p4tmp[:, 0:W4], in0=p4tmp[:, 0:W4],
            scalar1=-1.0, scalar2=None, op0=ALU.add))
        op("vector", lambda v, c0=c0: v.tensor_mul(out=bc3[:, 0, c0:c0 + W4],
            in0=bc3[:, 0, c0:c0 + W4], in1=p4tmp[:, 0:W4]))
        op("vector", lambda v, c0=c0: v.tensor_scalar_max(bc3[:, 0, c0:c0 + W4], bc3[:, 0, c0:c0 + W4], 0.25))
        e_gbp = opi("vector", lambda v, c0=c0: v.tensor_mul(out=bc3[:, 0, c0:c0 + W4],
            in0=bc3[:, 0, c0:c0 + W4], in1=p4tmp[:, 0:W4]), sV)

    p4ld = [0] * NTT
    p4st = [0] * NTT
    for tt in range(NTT):
        par = tt % 2
        if tt >= 1:
            wait_tok("sync", p4st[tt - 1])
            wait("sync", sV, p4V[tt - 1])
        elif tt == 0:
            wait("sync", sV, P3_V_END)
            for oc_ in range(max(0, NOWN - 2), NOWN):
                wait_tok("sync", p2BO[oc_])
        ln = opi("sync", lambda e, p=par, tt=tt: e.dma_start(
            out=x1p4[p][:], in_=x1_b[tt * Tc:(tt + 1) * Tc, :].rearrange("(g p) d -> p g d", p=128)), sxp[par], 16)
        p4ld[tt] = (sxp[par], ln)
        wait("tensor", sG, P3_G_END)
        wait_tok("vector", p4ld[tt])
        wait("vector", sV, e_gbp)
        wait("vector", sG, P3_G_END)
        e_cmb = None
        BW = min(4, DC)
        HW = BW * 64
        for g in range(G):
            for hdc in range(0, DC, BW):
                for k in range(BW):
                    dd = hdc + k
                    e_at = opi("tensor", lambda ttn, dd=dd, g=g, k=k, tt=tt: ttn.transpose(
                        ps_tf[:, k * 128:(k + 1) * 128],
                        acc[:, dd, tt * Tc + g * 128:tt * Tc + (g + 1) * 128], ident_f[:]), sT)
                wait("vector", sT, e_at)
                c0 = hdc * 128
                for h in range(2):
                    op("vector", lambda v, g=g, c0=c0, h=h, HW=HW, p=par: v.tensor_mul(
                        out=p4tmp[:, h * HW:(h + 1) * HW],
                        in0=ps_tf[:, h * HW:(h + 1) * HW],
                        in1=bc3[:, 0, c0 + h * HW:c0 + (h + 1) * HW]))
                for h in range(2):
                    e_cmb = opi("vector", lambda v, g=g, c0=c0, h=h, HW=HW, p=par: v.tensor_add(
                        out=x1p4[p][:, g, c0 + h * HW:c0 + (h + 1) * HW],
                        in0=x1p4[p][:, g, c0 + h * HW:c0 + (h + 1) * HW],
                        in1=p4tmp[:, h * HW:(h + 1) * HW]), sV)
                wait("tensor", sV, e_cmb)
        p4V_cur = sV.n
        if tt == 0:
            p4V = [0] * NTT
        p4V[tt] = p4V_cur
        wait("sync", sV, e_cmb)
        p4st[tt] = (soutp[par], opi("sync", lambda e, p=par, tt=tt: e.dma_start(
            out=out[tt * Tc:(tt + 1) * Tc, :].rearrange("(g p) d -> p g d", p=128),
            in_=x1p4[p][:]), soutp[par], 16))

    for sp in soutp:
        if sp.n:
            wait("gpsimd", sp, sp.n)

    with nc.Block() as block:
        def runner(lst):
            def run(e):
                for fn in lst:
                    fn(e)
            return run
        block.sync(runner(ctx.q["sync"]))
        block.tensor(runner(ctx.q["tensor"]))
        block.vector(runner(ctx.q["vector"]))
        block.scalar(runner(ctx.q["scalar"]))
        block.gpsimd(runner(ctx.q["gpsimd"]))

    p3stk.close()
    sstk.close()
    top.close()
    return nc


def make_in_maps(cfg, inputs):
    D, H, F, E, S, Tc = (cfg[k] for k in ("D", "H", "F", "E", "S", "Tc"))
    NCH = S // Tc
    FC = F // 128
    x = np.ascontiguousarray(inputs["L0_attn_x"], dtype=np.float32)
    B = x.shape[0]
    SH = S // 2
    c = np.ascontiguousarray(inputs["L0_c"], dtype=np.float32)
    shared = {
        "rms_row": np.ascontiguousarray(inputs["L0_attn_rms_scale"].reshape(1, D)),
        "split_row": np.ascontiguousarray(inputs["L0_attn_split_scale"].reshape(1, D)),
        "bg_row": np.ascontiguousarray(inputs["L0_ffn_bg"].reshape(1, E)),
        "wmod": np.ascontiguousarray(inputs["L0_wmod"]),
        "wk": np.ascontiguousarray(inputs["L0_attn_W_K"]),
        "wugg": np.ascontiguousarray(inputs["L0_attn_W_ugg"]),
        "wout": np.ascontiguousarray(inputs["L0_attn_W_out"]),
        "wg": np.ascontiguousarray(inputs["L0_ffn_Wg"]),
        "w1": np.ascontiguousarray(inputs["L0_ffn_W1"]),
        "b1T": np.ascontiguousarray(
            inputs["L0_ffn_b1"].reshape(E, FC, 128).transpose(0, 2, 1)),
        "w2": np.ascontiguousarray(inputs["L0_ffn_W2"]),
        "b2": np.ascontiguousarray(inputs["L0_ffn_b2"]),
    }
    ones = np.ones((128, NCH), np.float32)
    mask_j0 = ones.copy()
    mask_j0[:, :NCH // 2] = 0.0
    in_maps = []
    for core in range(2 * B):
        b, j = core // 2, core % 2
        if j == 0:
            xs = np.concatenate([x[b, 0:SH], x[b, 0:SH]], axis=0)
            mk = mask_j0
        else:
            xs = np.concatenate([x[b, 0:SH], x[b, SH:S]], axis=0)
            mk = ones
        m = dict(shared)
        m["x_seq"] = np.ascontiguousarray(xs)
        m["mask16"] = np.ascontiguousarray(mk)
        m["cvecT"] = np.ascontiguousarray(c[b].reshape(D // 128, 128).T)
        in_maps.append(m)
    return in_maps


def kernel(**inputs):
    cfg = FULL_CFG
    nc = build_program(cfg)
    in_maps = make_in_maps(cfg, inputs)
    res = run_bass_kernel_spmd(nc, in_maps, core_ids=list(range(len(in_maps))))
    B = inputs["L0_attn_x"].shape[0]
    S, D = cfg["S"], cfg["D"]
    out = np.empty((B, S, D), dtype=np.float32)
    for core in range(2 * B):
        b, j = core // 2, core % 2
        out[b, j * (S // 2):(j + 1) * (S // 2)] = res.results[core]["out"]
    return out


# version 10
# speedup vs baseline: 1.0494x; 1.0494x over previous
import numpy as np
from contextlib import ExitStack

import concourse.bass as bass
import concourse.mybir as mybir
from concourse.bass_utils import run_bass_kernel_spmd

F32 = mybir.dt.float32
BF16 = mybir.dt.bfloat16
AF = mybir.ActivationFunctionType
ALU = mybir.AluOpType
AX = mybir.AxisListType

ENGINES = ("sync", "tensor", "vector", "scalar", "gpsimd")

FULL_CFG = dict(D=1024, H=2048, F=4096, E=8, S=4096, Tc=256, NQ=4)


class Sem:
    def __init__(self, stk, nc, name):
        self.h = stk.enter_context(nc.semaphore(name))
        self.n = 0
        self.name = name


class Ctx:

    def __init__(self, nc, cfg):
        self.nc = nc
        self.cfg = cfg
        self.q = {e: [] for e in ENGINES}
        self.last_wait = {e: {} for e in ENGINES}

    def op(self, eng, fn):
        self.q[eng].append(fn)

    def opi(self, eng, fn, sem, k=1):
        sem.n += k
        def run(e, fn=fn, s=sem.h, k=k):
            return fn(e).then_inc(s, k)
        self.q[eng].append(run)
        return sem.n

    def wait(self, eng, sem, val):
        if val <= 0:
            return
        lw = self.last_wait[eng]
        if lw.get(sem.name, -1) >= val:
            return
        lw[sem.name] = val
        self.q[eng].append(lambda e, s=sem.h, v=val: e.wait_ge(s, v))


def build_program(cfg):
    nc = bass.Bass("TRN2", target_bir_lowering=False, detect_race_conditions=False)
    D, H, F, E, S, Tc, NQ = (cfg[k] for k in ("D", "H", "F", "E", "S", "Tc", "NQ"))
    DC, CC, FC = D // 128, H // 128, F // 128
    G = Tc // 128
    NCH = S // Tc
    OWN0 = NCH // 2
    NOWN = NCH - OWN0
    SH = S // 2
    FQ = F // NQ
    FQC = FQ // 128
    NTT = SH // Tc
    assert NTT == NOWN

    dt_in = lambda n, shp: nc.dram_tensor(n, list(shp), F32, kind="ExternalInput").ap()
    x_seq = dt_in("x_seq", (S, D))
    mask16 = dt_in("mask16", (128, NCH))
    cvecT = dt_in("cvecT", (128, DC))
    rms_row = dt_in("rms_row", (1, D))
    split_row = dt_in("split_row", (1, D))
    bg_row = dt_in("bg_row", (1, E))
    wmod = dt_in("wmod", (D, 6 * D))
    wk = dt_in("wk", (D, H))
    wugg = dt_in("wugg", (D, 3 * H))
    wout = dt_in("wout", (H, D))
    wg = dt_in("wg", (D, E))
    w1 = dt_in("w1", (E, D, F))
    b1T = dt_in("b1T", (E, 128, FC))
    w2 = dt_in("w2", (E, F, D))
    b2 = dt_in("b2", (E, D))

    out = nc.dram_tensor("out", [SH, D], F32, kind="ExternalOutput").ap()

    hin_b = nc.dram_tensor("hin_b", [SH, D], F32, kind="Internal").ap()
    x1_b = nc.dram_tensor("x1_b", [SH, D], F32, kind="Internal").ap()
    xnT_b = nc.dram_tensor("xnT_b", [NOWN, 128, DC, Tc], BF16, kind="Internal").ap()
    S_b = nc.dram_tensor("S_b", [NOWN, 128, CC, Tc], BF16, kind="Internal").ap()
    mod_d = nc.dram_tensor("mod_d", [1, 6 * D], F32, kind="Internal").ap()
    w_d = nc.dram_tensor("w_d", [E, SH], F32, kind="Internal").ap()

    ctx = Ctx(nc, cfg)
    top = ExitStack()
    sb = lambda n, shp, dt=F32: top.enter_context(nc.sbuf_tensor(n, list(shp), dt))
    ps = lambda n, shp, dt=F32: top.enter_context(nc.psum_tensor(n, list(shp), dt))

    ident_f = sb("ident_f", (128, 128))
    ident_b = sb("ident_b", (128, 128), BF16)
    wa = sb("wa", (128, DC * H), BF16)
    wb = sb("wb", (128, 2 * DC * H), BF16)
    bc3 = sb("bc3", (128, 3, D))
    bg_bc = sb("bg_bc", (128, E))
    mask_sb = sb("mask_sb", (128, NCH))
    cvec_sb = sb("cvec_sb", (128, DC))
    siluc_bf = sb("siluc_bf", (128, DC), BF16)
    eps5 = sb("eps5", (128, 1))
    stat = sb("stat", (128, 16))
    tailK = sb("tailK", (128, CC))
    carry = sb("carry", (128, CC))

    wk_v = wa[:].rearrange("p (c h) -> p c h", c=DC)
    h2T = wa[:].rearrange("p (c t) -> p c t", c=DC)
    wb_flat = wb[:]
    wmod_v = wb_flat.rearrange("p (c f) -> p c f", c=DC)
    ugg_v = wb_flat.rearrange("p (c f) -> p c f", c=DC)
    gout_v = wb_flat[:, 0:DC * H].rearrange("p (c h) -> p c h", c=DC)
    wout_v = wb_flat[:, DC * H:2 * DC * H].rearrange("p (c d) -> p c d", c=CC)
    QW = DC * FQ
    assert QW == FQC * D
    w1q_v = [wb_flat[:, i * QW:(i + 1) * QW].rearrange("p (c f) -> p c f", c=DC) for i in range(2)]
    w2q_v = [wb_flat[:, (2 + i) * QW:(3 + i) * QW].rearrange("p (c d) -> p c d", c=FQC) for i in range(2)]

    ps_y = ps("ps_y", (128, 8, Tc))
    ps_h = ps("ps_h", (128, 2, 512))
    ps_tf = ps("ps_tf", (128, 512))
    ps_t = ps("ps_t", (128, 1024), BF16)
    ps_g = ps_tf[:, 256:256 + E]

    sstk = ExitStack()
    sems = {}
    for name in ("sT", "sV", "sS", "sG", "sod"):
        sems[name] = Sem(sstk, nc, name)
    sT, sV, sS, sG, sod = (sems[k] for k in ("sT", "sV", "sS", "sG", "sod"))
    sxp = [Sem(sstk, nc, "sx%d" % i) for i in range(2)]
    sbop = [Sem(sstk, nc, "sbo%d" % i) for i in range(2)]
    soutp = [Sem(sstk, nc, "sout%d" % i) for i in range(2)]

    op, opi, wait = ctx.op, ctx.opi, ctx.wait

    class GQ:
        def __init__(self, n=4):
            self.sems = [Sem(sstk, nc, "sgd%d" % i) for i in range(n)]
            self.i = 0

        def dma(self, fn):
            s = self.sems[self.i % len(self.sems)]
            self.i += 1
            if s.n > 0:
                wait("gpsimd", s, s.n)
            val = opi("gpsimd", fn, s, 16)
            return (s, val)

    gq = GQ()

    def wait_tok(eng, tok):
        wait(eng, tok[0], tok[1])

    op("gpsimd", lambda g: g.memset(ident_f[:], 0.0))
    e_ms = opi("gpsimd", lambda g: g.memset(ident_b[:], 0.0), sG)
    wait("gpsimd", sG, e_ms)
    op("gpsimd", lambda g: g.affine_select(out=ident_f[:], in_=ident_f[:],
        compare_op=ALU.not_equal, fill=1.0, base=0, pattern=[[-1, 128]], channel_multiplier=1))
    op("gpsimd", lambda g: g.affine_select(out=ident_b[:], in_=ident_b[:],
        compare_op=ALU.not_equal, fill=1.0, base=0, pattern=[[-1, 128]], channel_multiplier=1))
    op("gpsimd", lambda g: g.memset(eps5[:], 1e-5))
    op("gpsimd", lambda g: g.memset(tailK[:], 0.0))
    op("gpsimd", lambda g: g.memset(carry[:], 0.0))
    e_const = opi("gpsimd", lambda g: g.engine_nop(), sG)

    opi("sync", lambda e: e.dma_start(out=mask_sb[:], in_=mask16), sxp[0], 16)
    ld0 = (sxp[0], opi("sync", lambda e: e.dma_start(out=cvec_sb[:], in_=cvecT), sxp[0], 16))
    e_wk = gq.dma(lambda g: g.dma_start(
        out=wk_v, in_=wk.rearrange("(c p) h -> p c h", p=128)))
    HF = 3 * D
    e_wm0 = gq.dma(lambda g: g.dma_start(
        out=wmod_v[:, :, 0:HF], in_=wmod[:, 0:HF].rearrange("(c p) m -> p c m", p=128)))

    wait_tok("scalar", ld0)
    e_silu = opi("scalar", lambda s: s.activation(out=cvec_sb[:], in_=cvec_sb[:], func=AF.Silu), sS)
    wait("vector", sS, e_silu)
    e_silub = opi("vector", lambda v: v.tensor_copy(out=siluc_bf[:], in_=cvec_sb[:]), sV)

    NGH = (HF + 511) // 512
    GW = HF // NGH
    assert GW * NGH == HF and GW <= 512
    wait_tok("tensor", e_wm0)
    wait("tensor", sV, e_silub)
    mm_done = []
    for half in range(2):
        if half == 1:
            wait("gpsimd", sT, mm_done[-1])
            e_wm1 = gq.dma(lambda g: g.dma_start(
                out=wmod_v[:, :, 0:HF],
                in_=wmod[:, HF:6 * D].rearrange("(c p) m -> p c m", p=128)))
            wait_tok("tensor", e_wm1)
        for j in range(NGH):
            if mm_done:
                wait("tensor", sV, e_row)
            def mmrow(t, j=j):
                mm = None
                for dc in range(DC):
                    mm = t.matmul(ps_tf[0:1, 0:GW], siluc_bf[:, dc:dc + 1],
                                  wmod_v[:, dc, j * GW:(j + 1) * GW],
                                  start=(dc == 0), stop=(dc == DC - 1))
                return mm
            ev = opi("tensor", mmrow, sT)
            mm_done.append(ev)
            col0 = half * HF + j * GW
            wait("vector", sT, ev)
            if sod.n:
                wait("vector", sod, sod.n)
            e_row = opi("vector", lambda v: v.tensor_copy(
                out=mod_sb[0:1, 0:GW], in_=ps_tf[0:1, 0:GW]), sV)
            wait("sync", sV, e_row)
            e_dd = opi("sync", lambda e, c0=col0: e.dma_start(
                out=mod_d[0:1, c0:c0 + GW], in_=mod_sb[0:1, 0:GW]), sod, 16)
    e_modd = (sod, e_dd)

    def bcast(dst_ap, src_t, src_off, n):
        src = bass.AP(src_t.tensor, src_t.offset + src_off, [[0, 128], [1, n]])
        return gq.dma(lambda g, d=dst_ap, s=src: g.dma_start(out=d, in_=s))

    wait_tok("gpsimd", e_modd)
    e_ca = bcast(bc3[:, 0, :], mod_d, 1 * D, D)
    e_sa = bcast(bc3[:, 1, :], mod_d, 0 * D, D)
    e_rm = bcast(bc3[:, 2, :], rms_row, 0, D)
    e_bg = bcast(bg_bc[:], bg_row, 0, E)

    stkA = ExitStack()
    sbA = lambda n, shp, dt=F32: stkA.enter_context(nc.sbuf_tensor(n, list(shp), dt))
    x_tok = [sbA("x_tok%d" % i, (128, G, D)) for i in range(2)]
    scrap = [sbA("scrap%d" % i, (128, G, D)) for i in range(2)]
    hin_t = sbA("hin_t", (128, G, D))
    hin = [hin_t, hin_t]
    xg_tok = [sbA("xg_tok%d" % i, (128, G, D), BF16) for i in range(2)]
    xgT = [sbA("xgT%d" % i, (128, DC, Tc), BF16) for i in range(2)]
    Gibuf = [sbA("Gibuf%d" % i, (128, Tc)) for i in range(2)]
    S_chunk_t = sbA("S_chunk", (128, CC, Tc), BF16)
    S_chunk = [S_chunk_t, S_chunk_t]
    wg_bf = sbA("wg_bf", (128, DC, E), BF16)
    wfin = sbA("wfin", (128, E))
    srt8 = sbA("srt8", (128, E))
    gz = sbA("gz", (128, E))
    wT_sb = sbA("wT_sb", (E, SH))
    stkP1 = ExitStack()
    sbP1 = lambda n, shp, dt=F32: stkP1.enter_context(nc.sbuf_tensor(n, list(shp), dt))
    mod_sb = sbP1("mod_sb", (1, 512))
    Kbuf = [sbP1("Kbuf%d" % i, (128, Tc + 1)) for i in range(2)]
    T2buf = [sbP1("T2buf%d" % i, (128, Tc)) for i in range(2)]
    Ubuf = [sbP1("Ubuf%d" % i, (128, Tc)) for i in range(2)]
    Sbuf = [sbP1("Sbuf%d" % i, (128, Tc)) for i in range(2)]

    e_sp = bcast(x_tok[0][:, 0, :], split_row, 0, D)
    for tok in (e_ca, e_sa, e_rm, e_sp):
        wait_tok("vector", tok)
    e_rsss = opi("vector", lambda v: v.tensor_mul(out=bc3[:, 2, :], in0=bc3[:, 2, :], in1=x_tok[0][:, 0, :]), sV)
    e_ca1 = opi("vector", lambda v: v.tensor_scalar(out=bc3[:, 0, :], in0=bc3[:, 0, :],
        scalar1=1.0, scalar2=None, op0=ALU.add), sV)
    P1_STATICS_V = e_ca1

    wait("gpsimd", sT, mm_done[-1])
    e_ug = gq.dma(lambda g: g.dma_start(
        out=ugg_v, in_=wugg[:, 0:2 * H].rearrange("(c p) m -> p c m", p=128)))

    chV = [0] * NCH
    chS = [0] * NCH
    chT = [0] * NCH
    chBO = [None] * (NCH + 1)
    hinBO = [None] * (NCH + 1)
    statV = [0] * NCH
    hinS = [0] * NCH
    lnV = [0] * NCH
    lnS = [0] * NCH
    psV = {}
    psS = {}
    ccV = {}
    ccG = {}
    xload = [0] * NCH
    LN = {}

    def ln_part(t):
        par = t % 2
        if t < 2:
            wait("sync", sV, P1_STATICS_V)
        if t >= 2:
            wait("sync", sV, lnV[t - 2])
            wait("sync", sS, lnS[t - 2])
        xs = x_seq[t * Tc:(t + 1) * Tc, :].rearrange("(g p) d -> p g d", p=128)
        xload[t] = (sxp[par], opi("sync", lambda e, d=x_tok[par], s=xs: e.dma_start(out=d[:], in_=s), sxp[par], 16))
        yield
        wait_tok("scalar", xload[t])
        if t >= 1:
            wait("scalar", sV, statV[t - 1])
        if t >= 2:
            wait("scalar", sV, lnV[t - 2])
        for g in range(G):
            e_sq = opi("scalar", lambda s, g=g, p=par: s.activation(
                out=scrap[p][:, g, :], in_=x_tok[p][:, g, :], func=AF.Square,
                accum_out=stat[:, 2 + g:3 + g]), sS)
        wait_tok("vector", xload[t])
        wait("vector", sV, P1_STATICS_V)
        for g in range(G):
            e_s1 = opi("vector", lambda v, g=g, p=par: v.reduce_sum(
                stat[:, 0 + g:1 + g], x_tok[p][:, g, :], axis=AX.X), sV)
        yield
        wait("vector", sV, e_s1)
        wait("vector", sS, e_sq)
        for g in range(G):
            op("vector", lambda v, g=g: v.tensor_scalar(out=stat[:, 4 + g:5 + g],
                in0=stat[:, 0 + g:1 + g], scalar1=1.0 / D, scalar2=None, op0=ALU.mult))
        for g in range(G):
            op("vector", lambda v, g=g: v.tensor_scalar(out=stat[:, 6 + g:7 + g],
                in0=stat[:, 2 + g:3 + g], scalar1=1.0 / D, scalar2=None, op0=ALU.mult))
        for g in range(G):
            op("vector", lambda v, g=g: v.tensor_mul(out=stat[:, 8 + g:9 + g],
                in0=stat[:, 4 + g:5 + g], in1=stat[:, 4 + g:5 + g]))
        for g in range(G):
            e_var = opi("vector", lambda v, g=g: v.tensor_sub(out=stat[:, 6 + g:7 + g],
                in0=stat[:, 6 + g:7 + g], in1=stat[:, 8 + g:9 + g]), sV)
        statV[t] = e_var
        wait("scalar", sV, e_var)
        for g in range(G):
            e_std = opi("scalar", lambda s, g=g: s.activation(out=stat[:, 8 + g:9 + g],
                in_=stat[:, 6 + g:7 + g], func=AF.Sqrt, bias=eps5[:]), sS)
        yield
        wait("vector", sS, e_std)
        for g in range(G):
            e_rst = opi("vector", lambda v, g=g: v.reciprocal(out=stat[:, 10 + g:11 + g],
                in_=stat[:, 8 + g:9 + g]), sV)
        wait("vector", sV, e_rst)
        for g in range(G):
            op("vector", lambda v, g=g, p=par: v.tensor_scalar(out=scrap[p][:, g, :],
                in0=x_tok[p][:, g, :], scalar1=stat[:, 4 + g:5 + g], scalar2=stat[:, 10 + g:11 + g],
                op0=ALU.subtract, op1=ALU.mult))
        yield
        if t >= 1:
            if hinBO[t - 1]:
                wait_tok("vector", hinBO[t - 1])
            wait("vector", sS, hinS[t - 1])
        for g in range(G):
            op("vector", lambda v, g=g, p=par: v.tensor_mul(out=hin[p][:, g, :],
                in0=scrap[p][:, g, :], in1=bc3[:, 0, :]))
        for g in range(G):
            e_hin = opi("vector", lambda v, g=g, p=par: v.tensor_add(out=hin[p][:, g, :],
                in0=hin[p][:, g, :], in1=bc3[:, 1, :]), sV)
        yield
        if t >= OWN0:
            oc = t - OWN0
            wait("sync", sV, e_hin)
            hinBO[t] = (sbop[par], opi("sync", lambda e, p=par, oc=oc: e.dma_start(
                out=hin_b[oc * Tc:(oc + 1) * Tc, :].rearrange("(g p) d -> p g d", p=128),
                in_=hin[p][:]), sbop[par], 16))
        wait("scalar", sV, e_hin)
        for g in range(G):
            e_rsq = opi("scalar", lambda s, g=g, p=par: s.activation(
                out=scrap[p][:, g, :], in_=hin[p][:, g, :], func=AF.Square,
                accum_out=stat[:, 12 + g:13 + g]), sS)
        hinS[t] = e_rsq
        wait("scalar", sS, e_rsq)
        for g in range(G):
            e_nrm = opi("scalar", lambda s, g=g: s.activation(out=stat[:, 14 + g:15 + g],
                in_=stat[:, 12 + g:13 + g], func=AF.Sqrt, scale=1.0 / D), sS)
        yield
        wait("vector", sS, e_nrm)
        for g in range(G):
            op("vector", lambda v, g=g: v.tensor_scalar(out=stat[:, 12 + g:13 + g],
                in0=stat[:, 14 + g:15 + g], scalar1=1e-8, scalar2=None, op0=ALU.add))
        for g in range(G):
            e_rr = opi("vector", lambda v, g=g: v.reciprocal(out=stat[:, 14 + g:15 + g],
                in_=stat[:, 12 + g:13 + g]), sV)
        wait("vector", sV, e_rr)
        for g in range(G):
            op("vector", lambda v, g=g, p=par: v.tensor_scalar(out=scrap[p][:, g, :],
                in0=hin[p][:, g, :], scalar1=stat[:, 14 + g:15 + g], scalar2=None, op0=ALU.mult))
        for g in range(G):
            e_xn = opi("vector", lambda v, g=g, p=par: v.tensor_mul(out=xg_tok[p][:, g, :],
                in0=scrap[p][:, g, :], in1=bc3[:, 2, :]), sV)
        LN[t] = dict(e_hin=e_hin, e_xn=e_xn)
        lnV[t] = sV.n
        lnS[t] = sS.n

    def ln_transpose(t):
        par = t % 2
        e_xn = LN[t]["e_xn"]
        wait("tensor", sV, e_xn)
        if t >= 2:
            if chBO[t - 2]:
                wait_tok("tensor", chBO[t - 2])
                wait_tok("vector", chBO[t - 2])
            wait("vector", sT, chT[t - 2])
        n_tp = G * DC
        e_xnT = None
        for b0 in range(0, n_tp, 4):
            for k in range(4):
                idx = b0 + k
                g, dc = idx // DC, idx % DC
                e_tp = opi("tensor", lambda tt, g=g, dc=dc, k=k, p=par: tt.transpose(
                    ps_t[:, k * 128:(k + 1) * 128], xg_tok[p][:, g, dc * 128:(dc + 1) * 128],
                    ident_b[:]), sT)
            wait("vector", sT, e_tp)
            g0, dc0 = b0 // DC, b0 % DC
            if dc0 + 4 <= DC:
                e_xnT = opi("vector", lambda v, g=g0, dc=dc0, p=par: v.tensor_copy(
                    out=xgT[p][:, dc:dc + 4, g * 128:(g + 1) * 128],
                    in_=ps_t[:, 0:512].rearrange("p (k t) -> p k t", k=4)), sV)
            else:
                for k in range(4):
                    idx = b0 + k
                    g, dc = idx // DC, idx % DC
                    e_xnT = opi("vector", lambda v, g=g, dc=dc, k=k, p=par: v.tensor_copy(
                        out=xgT[p][:, dc, g * 128:(g + 1) * 128],
                        in_=ps_t[:, k * 128:(k + 1) * 128]), sV)
            if b0 + 4 < n_tp:
                wait("tensor", sV, e_xnT)
        LN[t]["e_xnT"] = e_xnT
        if t >= OWN0:
            oc = t - OWN0
            wait("sync", sV, e_xnT)
            chBO[t] = (sbop[par], opi("sync", lambda e, p=par, oc=oc: e.dma_start(
                out=xnT_b[oc], in_=xgT[p][:]), sbop[par], 16))

    def cc_part(t):
        par = t % 2
        own = t >= OWN0
        e_xnT = LN[t]["e_xnT"]
        e_hin = LN[t]["e_hin"]
        wait_tok("tensor", e_ug)
        wait_tok("tensor", e_wk)
        for cc in range(CC):
            cp = cc % 2
            slot_k, slot_u, slot_gi = 4 * cp, 4 * cp + 1, 4 * cp + 2
            if (t, cc) > (0, 1):
                pse = psS.get(cp)
                if pse:
                    wait("tensor", sS, pse)
            wait("tensor", sV, e_xnT)
            def proj(tt, p=par, cc=cc, sk=slot_k, su=slot_u, sgi=slot_gi):
                mm = None
                for dc in range(DC):
                    mm = tt.matmul(ps_y[:, sk, :], wk_v[:, dc, cc * 128:(cc + 1) * 128],
                                   xgT[p][:, dc, :], start=(dc == 0), stop=(dc == DC - 1))
                for dc in range(DC):
                    mm = tt.matmul(ps_y[:, su, :], ugg_v[:, dc, cc * 128:(cc + 1) * 128],
                                   xgT[p][:, dc, :], start=(dc == 0), stop=(dc == DC - 1))
                for dc in range(DC):
                    mm = tt.matmul(ps_y[:, sgi, :], ugg_v[:, dc, H + cc * 128:H + (cc + 1) * 128],
                                   xgT[p][:, dc, :], start=(dc == 0), stop=(dc == DC - 1))
                return mm
            e_mm = opi("tensor", proj, sT)
            wait("scalar", sT, e_mm)
            if cp in ccV:
                wait("scalar", sV, ccV[cp])
            if cp in ccG:
                wait("scalar", sG, ccG[cp])
            e_sk = opi("scalar", lambda s, cp=cp, sk=slot_k: s.activation(
                out=Kbuf[cp][:, 1:Tc + 1], in_=ps_y[:, sk, :], func=AF.Sigmoid), sS)
            e_um = opi("scalar", lambda s, cp=cp, su=slot_u, t=t: s.activation(
                out=T2buf[cp][:], in_=ps_y[:, su, :], func=AF.Copy,
                scale=mask_sb[:, t:t + 1]), sS)
            e_sg = opi("scalar", lambda s, cp=cp, sgi=slot_gi: s.activation(
                out=Gibuf[cp][:], in_=ps_y[:, sgi, :], func=AF.Sigmoid), sS)
            e_km = opi("scalar", lambda s, cp=cp, sk=slot_k: s.activation(
                out=Ubuf[cp][:], in_=ps_y[:, sk, :], func=AF.Sigmoid,
                scale=-1.0), sS)
            psS[cp] = e_km
            wait("gpsimd", sS, e_km)
            op("gpsimd", lambda g, cp=cp: g.tensor_tensor(out=T2buf[cp][:],
                in0=T2buf[cp][:], in1=Gibuf[cp][:], op=ALU.mult))
            e_ut = opi("gpsimd", lambda g, cp=cp: g.tensor_tensor(out=T2buf[cp][:],
                in0=T2buf[cp][:], in1=Ubuf[cp][:], op=ALU.mult), sG)
            ccG[cp] = e_ut
            wait("vector", sS, e_sk)
            op("vector", lambda v, cp=cp, cc=cc: v.tensor_copy(out=Kbuf[cp][:, 0:1],
                in_=tailK[:, cc:cc + 1]))
            e_tl = opi("vector", lambda v, cp=cp, cc=cc: v.tensor_copy(
                out=tailK[:, cc:cc + 1], in_=Kbuf[cp][:, Tc:Tc + 1]), sV)
            wait("vector", sG, e_ut)
            e_scan = opi("vector", lambda v, cp=cp, cc=cc: v.tensor_tensor_scan(
                out=Sbuf[cp][:], data0=Kbuf[cp][:, 0:Tc], data1=T2buf[cp][:],
                initial=carry[:, cc:cc + 1], op0=ALU.mult, op1=ALU.add), sV)
            wait("vector", sV, e_scan)
            e_cr = opi("vector", lambda v, cp=cp, cc=cc: v.tensor_copy(
                out=carry[:, cc:cc + 1], in_=Sbuf[cp][:, Tc - 1:Tc]), sV)
            if own:
                if cc == 0 and t - 1 >= OWN0 and chBO[t - 1]:
                    wait_tok("vector", chBO[t - 1])
                e_sS = opi("vector", lambda v, cp=cp, cc=cc, p=par: v.tensor_copy(
                    out=S_chunk[p][:, cc, :], in_=Sbuf[cp][:]), sV)
            ccV[cp] = sV.n
            yield
        chV[t] = sV.n
        chS[t] = sS.n
        chT[t] = sT.n
        if own:
            oc = t - OWN0
            wait("sync", sV, sV.n)
            bo = opi("sync", lambda e, p=par, oc=oc: e.dma_start(
                out=S_b[oc], in_=S_chunk[p][:]), sbop[par], 16)
            chBO[t] = (sbop[par], bo)

    for _ in ln_part(0):
        pass
    ln_transpose(0)
    for _ in ln_part(1):
        pass
    for t in range(NCH):
        gc = cc_part(t)
        gl = ln_part(t + 2) if t + 2 < NCH else None
        for _ in gc:
            if gl is not None:
                try:
                    next(gl)
                except StopIteration:
                    gl = None
        while gl is not None:
            try:
                next(gl)
            except StopIteration:
                gl = None
        if t + 1 < NCH:
            ln_transpose(t + 1)

    P1_T_END = sT.n
    P1_V_END = sV.n
    P1_S_END = sS.n

    wait("gpsimd", sT, P1_T_END)
    e_go = gq.dma(lambda g: g.dma_start(
        out=gout_v, in_=wugg[:, 2 * H:3 * H].rearrange("(c p) m -> p c m", p=128)))
    e_wo = gq.dma(lambda g: g.dma_start(
        out=wout_v, in_=wout.rearrange("(c p) d -> p c d", p=128)))
    e_wg = gq.dma(lambda g: g.dma_start(
        out=wg_bf[:], in_=wg.rearrange("(c p) e -> p c e", p=128)))
    e_cb = bcast(bc3[:, 0, :], mod_d, 4 * D, D)
    e_sb2 = bcast(bc3[:, 1, :], mod_d, 3 * D, D)
    e_ga = bcast(bc3[:, 2, :], mod_d, 2 * D, D)
    wait("vector", sV, P1_V_END)
    wait("vector", sS, P1_S_END)
    wait("vector", sT, P1_T_END)
    for tok in (e_cb, e_sb2, e_ga):
        wait_tok("vector", tok)
    e_cb1 = opi("vector", lambda v: v.tensor_scalar(out=bc3[:, 0, :], in0=bc3[:, 0, :],
        scalar1=1.0, scalar2=None, op0=ALU.add), sV)
    op("vector", lambda v: v.tensor_scalar(out=scrap[0][:, 0, :], in0=bc3[:, 2, :],
        scalar1=0.0, scalar2=None, op0=ALU.is_ge))
    op("vector", lambda v: v.tensor_scalar(out=scrap[0][:, 0, :], in0=scrap[0][:, 0, :],
        scalar1=2.0, scalar2=None, op0=ALU.mult))
    op("vector", lambda v: v.tensor_scalar(out=scrap[0][:, 0, :], in0=scrap[0][:, 0, :],
        scalar1=-1.0, scalar2=None, op0=ALU.add))
    op("vector", lambda v: v.tensor_mul(out=bc3[:, 2, :], in0=bc3[:, 2, :],
        in1=scrap[0][:, 0, :]))
    op("vector", lambda v: v.tensor_scalar_max(bc3[:, 2, :], bc3[:, 2, :], 0.25))
    e_gap = opi("vector", lambda v: v.tensor_mul(out=bc3[:, 2, :], in0=bc3[:, 2, :],
        in1=scrap[0][:, 0, :]), sV)
    P2_STATICS_V = e_gap

    stkP1.close()
    y_sb = sbA("y_sb", (128, DC, Tc))
    x1_t = sbA("x1_t", (128, G, D))
    x1 = [x1_t, x1_t]
    s_all = sbA("s_all", (128, CC, Tc), BF16)

    p2Sdone = {}
    p2hin = {}
    p2sT = {}
    aT = {}
    bV = {}
    gateZ = {}
    yeV = {}
    x1BO = {}
    GB = {}

    wait_tok("tensor", e_wo)
    wait_tok("tensor", e_go)
    wait_tok("tensor", e_wg)

    def genA(oc):
        par = oc % 2
        if oc >= 2:
            wait("sync", sV, bV[oc - 2])
            wait("sync", sT, aT[oc - 2])
        else:
            wait("sync", sV, P1_V_END)
            wait("sync", sT, P1_T_END)
            wait("sync", sS, P1_S_END)
            for tok in (chBO[NCH - 1], chBO[NCH - 2]):
                if tok:
                    wait_tok("sync", tok)
        opi("sync", lambda e, p=par, oc=oc: e.dma_start(out=xgT[p][:], in_=xnT_b[oc]), sxp[par], 16)
        if oc >= 1:
            wait("sync", sV, p2Sdone[oc - 1])
            wait("sync", sV, p2hin[oc - 1])
            wait("sync", sS, GB.get(oc - 1, {}).get("bS", 0))
        opi("sync", lambda e, p=par, oc=oc: e.dma_start(out=S_chunk[p][:], in_=S_b[oc]), sxp[par], 16)
        opi("sync", lambda e, p=par, oc=oc: e.dma_start(
            out=hin[p][:], in_=hin_b[oc * Tc:(oc + 1) * Tc, :].rearrange("(g p) d -> p g d", p=128)), sxp[par], 16)
        ln = opi("sync", lambda e, p=par, oc=oc: e.dma_start(
            out=x_tok[p][:], in_=x_seq[(OWN0 + oc) * Tc:(OWN0 + oc + 1) * Tc, :]
            .rearrange("(g p) d -> p g d", p=128)), sxp[par], 16)
        yield
        wait("tensor", sxp[par], ln)
        if oc >= 1:
            wait("vector", sT, p2sT[oc - 1])
        e_go_evac, e_s_done = {}, {}
        for cc in range(CC):
            cp = cc % 2
            if cc >= 2:
                wait("tensor", sS, e_go_evac[cc - 2])
            def gproj(tt, p=par, cc=cc, cp=cp):
                mm = None
                for dc in range(DC):
                    mm = tt.matmul(ps_h[:, cp, 0:Tc], gout_v[:, dc, cc * 128:(cc + 1) * 128],
                                   xgT[p][:, dc, :], start=(dc == 0), stop=(dc == DC - 1))
                return mm
            e_gmm = opi("tensor", gproj, sT)
            wait("scalar", sT, e_gmm)
            if cc >= 2:
                wait("scalar", sV, e_s_done[cc - 2])
            e_go_evac[cc] = opi("scalar", lambda s, cp=cp: s.activation(
                out=Gibuf[cp][:], in_=ps_h[:, cp, 0:Tc], func=AF.Sigmoid), sS)
            wait("vector", sS, e_go_evac[cc])
            e_s_done[cc] = opi("vector", lambda v, cp=cp, cc=cc, p=par: v.tensor_tensor(
                out=s_all[:, cc, :], in0=S_chunk[p][:, cc, :], in1=Gibuf[cp][:],
                op=ALU.mult), sV)
            yield
        p2Sdone[oc] = e_s_done[CC - 1]
        wait("tensor", sV, e_s_done[CC - 1])
        if oc >= 1:
            wait("tensor", sV, yeV[oc - 1])
        def ymm(tt):
            mm = None
            for dd in range(DC):
                for cc in range(CC):
                    mm = tt.matmul(ps_y[:, dd, :], wout_v[:, cc, dd * 128:(dd + 1) * 128],
                                   s_all[:, cc, :], start=(cc == 0), stop=(cc == CC - 1))
            return mm
        e_ymm = opi("tensor", ymm, sT)
        p2sT[oc] = e_ymm
        aT[oc] = sT.n

    def genB(oc):
        par = oc % 2
        e_ymm = p2sT[oc]
        wait("vector", sT, e_ymm)
        for dd in range(DC):
            e_ye = opi("vector", lambda v, dd=dd: v.tensor_copy(
                out=y_sb[:, dd, :], in_=ps_y[:, dd, :]), sV)
        yeV[oc] = e_ye
        wait("tensor", sV, e_ye)
        wait("vector", sV, P2_STATICS_V)
        if oc >= 1:
            wait_tok("vector", x1BO[oc - 1])
        e_x1 = None
        BW = min(4, DC)
        W = BW * 128
        for g in range(G):
            for hdc in range(0, DC, BW):
                for k in range(BW):
                    dd = hdc + k
                    e_yt = opi("tensor", lambda tt, dd=dd, g=g, k=k: tt.transpose(
                        ps_tf[:, k * 128:(k + 1) * 128], y_sb[:, dd, g * 128:(g + 1) * 128],
                        ident_f[:]), sT)
                wait("vector", sT, e_yt)
                c0 = hdc * 128
                op("vector", lambda v, g=g, c0=c0, W=W, p=par: v.tensor_add(
                    out=scrap[p][:, g, c0:c0 + W], in0=ps_tf[:, 0:W],
                    in1=hin[p][:, g, c0:c0 + W]))
                op("vector", lambda v, g=g, c0=c0, W=W, p=par: v.tensor_mul(
                    out=scrap[p][:, g, c0:c0 + W], in0=scrap[p][:, g, c0:c0 + W],
                    in1=bc3[:, 2, c0:c0 + W]))
                e_x1 = opi("vector", lambda v, g=g, c0=c0, W=W, p=par: v.tensor_add(
                    out=x1[p][:, g, c0:c0 + W], in0=scrap[p][:, g, c0:c0 + W],
                    in1=x_tok[p][:, g, c0:c0 + W]), sV)
                if g * DC * 128 + c0 + W < G * D:
                    wait("tensor", sV, e_x1)
        p2hin[oc] = e_x1
        wait("sync", sV, e_x1)
        x1BO[oc] = (sbop[par], opi("sync", lambda e, p=par, oc=oc: e.dma_start(
            out=x1_b[oc * Tc:(oc + 1) * Tc, :].rearrange("(g p) d -> p g d", p=128),
            in_=x1[p][:]), sbop[par], 16))
        yield "x1"
        wait("scalar", sV, e_x1)
        if oc >= 1:
            wait("scalar", sV, GB[oc - 1]["statV"])
        for g in range(G):
            e_sq = opi("scalar", lambda s, g=g, p=par: s.activation(
                out=scrap[p][:, g, :], in_=x1[p][:, g, :], func=AF.Square,
                accum_out=stat[:, 2 + g:3 + g]), sS)
        for g in range(G):
            e_s1 = opi("vector", lambda v, g=g, p=par: v.reduce_sum(
                stat[:, 0 + g:1 + g], x1[p][:, g, :], axis=AX.X), sV)
        yield "ln"
        wait("vector", sV, e_s1)
        wait("vector", sS, e_sq)
        for g in range(G):
            op("vector", lambda v, g=g: v.tensor_scalar(out=stat[:, 4 + g:5 + g],
                in0=stat[:, 0 + g:1 + g], scalar1=1.0 / D, scalar2=None, op0=ALU.mult))
        for g in range(G):
            op("vector", lambda v, g=g: v.tensor_scalar(out=stat[:, 6 + g:7 + g],
                in0=stat[:, 2 + g:3 + g], scalar1=1.0 / D, scalar2=None, op0=ALU.mult))
        for g in range(G):
            op("vector", lambda v, g=g: v.tensor_mul(out=stat[:, 8 + g:9 + g],
                in0=stat[:, 4 + g:5 + g], in1=stat[:, 4 + g:5 + g]))
        for g in range(G):
            e_var = opi("vector", lambda v, g=g: v.tensor_sub(out=stat[:, 6 + g:7 + g],
                in0=stat[:, 6 + g:7 + g], in1=stat[:, 8 + g:9 + g]), sV)
        GB.setdefault(oc, {})["statV"] = e_var
        wait("scalar", sV, e_var)
        for g in range(G):
            e_std = opi("scalar", lambda s, g=g: s.activation(out=stat[:, 8 + g:9 + g],
                in_=stat[:, 6 + g:7 + g], func=AF.Sqrt, bias=eps5[:]), sS)
        yield "std"
        wait("vector", sS, e_std)
        for g in range(G):
            e_rst = opi("vector", lambda v, g=g: v.reciprocal(out=stat[:, 10 + g:11 + g],
                in_=stat[:, 8 + g:9 + g]), sV)
        wait("vector", sV, e_rst)
        for g in range(G):
            op("vector", lambda v, g=g, p=par: v.tensor_scalar(out=scrap[p][:, g, :],
                in0=x1[p][:, g, :], scalar1=stat[:, 4 + g:5 + g], scalar2=stat[:, 10 + g:11 + g],
                op0=ALU.subtract, op1=ALU.mult))
        for g in range(G):
            op("vector", lambda v, g=g, p=par: v.tensor_mul(out=scrap[p][:, g, :],
                in0=scrap[p][:, g, :], in1=bc3[:, 0, :]))
        for g in range(G):
            e_h2 = opi("vector", lambda v, g=g, p=par: v.tensor_add(out=xg_tok[p][:, g, :],
                in0=scrap[p][:, g, :], in1=bc3[:, 1, :]), sV)
        GB[oc]["bS"] = sS.n
        yield "h2"
        wait("tensor", sV, e_h2)
        for b0 in range(0, G * DC, 4):
            for k in range(4):
                idx = b0 + k
                g, dc = idx // DC, idx % DC
                e_tp = opi("tensor", lambda tt, g=g, dc=dc, k=k, p=par: tt.transpose(
                    ps_t[:, k * 128:(k + 1) * 128], xg_tok[p][:, g, dc * 128:(dc + 1) * 128],
                    ident_b[:]), sT)
            wait("vector", sT, e_tp)
            for k in range(4):
                idx = b0 + k
                g, dc = idx // DC, idx % DC
                col = oc * Tc + g * 128
                e_h2T = opi("vector", lambda v, g=g, dc=dc, k=k, col=col: v.tensor_copy(
                    out=h2T[:, dc, col:col + 128], in_=ps_t[:, k * 128:(k + 1) * 128]), sV)
            if b0 + 4 < G * DC:
                wait("tensor", sV, e_h2T)
            yield "tp"
        for g in range(G):
            col = oc * Tc + g * 128
            def gatemm(tt, col=col):
                mm = None
                for dc in range(DC):
                    mm = tt.matmul(ps_g, h2T[:, dc, col:col + 128], wg_bf[:, dc, :],
                                   start=(dc == 0), stop=(dc == DC - 1))
                return mm
            wait("tensor", sV, e_h2T)
            if e_wT_evac:
                wait("tensor", sV, e_wT_evac[-1])
            e_gl = opi("tensor", gatemm, sT)
            wait("vector", sT, e_gl)
            e_z = opi("vector", lambda v: v.tensor_add(out=gz[:], in0=ps_g, in1=bg_bc[:]), sV)
            gateZ[oc] = e_z
            wait("vector", sV, e_z)
            e_nm = opi("vector", lambda v: v.reduce_max(stat[:, 0:1], gz[:], axis=AX.X, negate=True), sV)
            wait("scalar", sV, e_nm)
            e_ex = opi("scalar", lambda s: s.activation(out=srt8[:], in_=gz[:], func=AF.Exp,
                bias=stat[:, 0:1], accum_out=stat[:, 1:2]), sS)
            wait("vector", sS, e_ex)
            e_rc = opi("vector", lambda v: v.reciprocal(out=stat[:, 2:3], in_=stat[:, 1:2]), sV)
            wait("vector", sV, e_rc)
            e_gt = opi("vector", lambda v: v.tensor_scalar(out=gz[:], in0=srt8[:],
                scalar1=stat[:, 2:3], scalar2=None, op0=ALU.mult), sV)
            wait("vector", sV, e_gt)
            e_m8 = opi("vector", lambda v: v.max(out=srt8[:], in_=gz[:]), sV)
            wait("vector", sV, e_m8)
            op("vector", lambda v: v.tensor_scalar(out=wfin[:], in0=gz[:],
                scalar1=srt8[:, 1:2], scalar2=None, op0=ALU.subtract))
            op("vector", lambda v: v.tensor_add(out=stat[:, 3:4], in0=srt8[:, 0:1], in1=srt8[:, 1:2]))
            op("vector", lambda v: v.tensor_scalar(out=wfin[:], in0=wfin[:],
                scalar1=0.0, scalar2=None, op0=ALU.is_ge))
            op("vector", lambda v: v.tensor_scalar(out=stat[:, 3:4], in0=stat[:, 3:4],
                scalar1=1e-8, scalar2=None, op0=ALU.add))
            op("vector", lambda v: v.tensor_mul(out=wfin[:], in0=wfin[:], in1=gz[:]))
            e_dn = opi("vector", lambda v: v.reciprocal(out=stat[:, 4:5], in_=stat[:, 3:4]), sV)
            wait("vector", sV, e_dn)
            e_wf = opi("vector", lambda v: v.tensor_scalar(out=wfin[:], in0=wfin[:],
                scalar1=stat[:, 4:5], scalar2=None, op0=ALU.mult), sV)
            wait("tensor", sV, e_wf)
            e_wt = opi("tensor", lambda tt: tt.transpose(ps_tf[0:E, 0:128], wfin[:], ident_f[:]), sT)
            wait("vector", sT, e_wt)
            e_wT_evac.append(opi("vector", lambda v, col=col: v.tensor_copy(
                out=wT_sb[:, col:col + 128], in_=ps_tf[0:E, 0:128]), sV))
            yield "gate"
        GB[oc]["wT"] = True
        bV[oc] = sV.n

    e_wT_evac = []
    import os
    OVL_STAGES = os.environ.get("P2_OVL_STAGES", "ln,std,h2,tp,gate").split(",")
    gB = None
    for oc in range(NOWN):
        gA = genA(oc)
        for _ in gA:
            if gB is not None:
                try:
                    nt = next(gB)
                    if nt not in OVL_STAGES:
                        while True:
                            next(gB)
                except StopIteration:
                    gB = None
        while gB is not None:
            try:
                next(gB)
            except StopIteration:
                gB = None
        gB = genB(oc)
        while True:
            tag = next(gB, None)
            if tag is None:
                gB = None
                break
            if tag == "x1":
                break
    while gB is not None:
        try:
            next(gB)
        except StopIteration:
            gB = None

    P2_V_END = sV.n
    P2_T_END = sT.n
    wait("sync", sV, P2_V_END)
    e_wd = (sod, opi("sync", lambda e: e.dma_start(out=w_d, in_=wT_sb[:]), sod, 16))

    stkA.close()
    acc = sb("acc", (128, DC, SH))
    p3stk = ExitStack()
    sb3 = lambda n, shp, dt=F32: p3stk.enter_context(nc.sbuf_tensor(n, list(shp), dt))
    w_bc = sb3("w_bc", (128, 2, NTT, Tc), BF16)
    H_all = sb3("H_all", (128, 2, FQC, Tc), BF16)
    tq_sb = [sb3("tq_sb%d" % i, (128, Tc)) for i in range(2)]
    b1_sb = [sb3("b1_sb%d" % i, (128, FC)) for i in range(2)]
    b2_bf = sb3("b2_bf", (E, D), BF16)
    wT_bf = sb3("wT_bf", (E, SH), BF16)
    x1p4_t = sb3("x1p4", (128, G, D))
    x1p4 = [x1p4_t, x1p4_t]
    p4tmp = sb3("p4tmp", (128, 512))

    wait_tok("gpsimd", e_wd)
    e_b2 = gq.dma(lambda g: g.dma_start(out=b2_bf[:], in_=b2))
    e_wtb = gq.dma(lambda g: g.dma_start(out=wT_bf[:], in_=w_d))

    wait_tok("tensor", e_wtb)
    wait_tok("tensor", e_b2)
    wait("tensor", sT, P2_T_END)
    wait("tensor", sV, P2_V_END)
    acc_init_v = 0
    for tt in range(NTT):
        def b2mm(ttn, tt=tt):
            mm = None
            for dd in range(DC):
                mm = ttn.matmul(ps_y[:, dd, :], b2_bf[0:E, dd * 128:(dd + 1) * 128],
                                wT_bf[0:E, tt * Tc:(tt + 1) * Tc],
                                start=True, stop=True)
            return mm
        e_b2mm = opi("tensor", b2mm, sT)
        wait("vector", sT, e_b2mm)
        for dd in range(DC):
            acc_init_v = opi("vector", lambda v, dd=dd, tt=tt: v.tensor_copy(
                out=acc[:, dd, tt * Tc:(tt + 1) * Tc], in_=ps_y[:, dd, :]), sV)
        if tt + 1 < NTT:
            wait("tensor", sV, acc_init_v)

    pairs = [(e, q) for e in range(E) for q in range(NQ)]
    loaded = {}
    w2T_hist = {}
    prev_accum_T = {}
    expV = {}
    b1_used = {}
    ebc = {}
    eb1t = {}
    wait("tensor", sV, acc_init_v)

    def issue_pair_loads(idx):
        e_i, q = pairs[idx]
        wpar = idx % 2
        pt = prev_accum_T.get(wpar)
        wait("gpsimd", sT, pt if pt else P2_T_END)
        lw1 = gq.dma(lambda g, e_i=e_i, q=q, wpar=wpar: g.dma_start(
            out=w1q_v[wpar], in_=w1[e_i][:, q * FQ:(q + 1) * FQ]
            .rearrange("(c p) f -> p c f", p=128)))
        lw2 = gq.dma(lambda g, e_i=e_i, q=q, wpar=wpar: g.dma_start(
            out=w2q_v[wpar], in_=w2[e_i][q * FQ:(q + 1) * FQ, :]
            .rearrange("(c p) d -> p c d", p=128)))
        loaded[idx] = (lw1, lw2)

    def issue_expert_smalls(e_i):
        ep = e_i % 2
        if e_i >= 2:
            wait("gpsimd", sV, expV[e_i - 2])
            wait("gpsimd", sS, b1_used[e_i - 2])
        ebc[e_i] = gq.dma(lambda g, e_i=e_i, ep=ep: g.dma_start(
            out=w_bc[:, ep].rearrange("p a t -> p (a t)"),
            in_=bass.AP(w_d.tensor, e_i * SH, [[0, 128], [1, SH]])))
        eb1t[e_i] = gq.dma(lambda g, e_i=e_i, ep=ep: g.dma_start(
            out=b1_sb[ep][:], in_=b1T[e_i]))

    issue_expert_smalls(0)
    issue_pair_loads(0)
    for idx, (e_i, q) in enumerate(pairs):
        wpar = idx % 2
        ep = e_i % 2
        if idx + 1 < len(pairs):
            if pairs[idx + 1][1] == 0:
                issue_expert_smalls(e_i + 1)
            issue_pair_loads(idx + 1)
        lw1, lw2 = loaded[idx]
        wait_tok("tensor", lw1)
        wait_tok("tensor", lw2)
        if q == 0:
            wait_tok("scalar", eb1t[e_i])
            wait_tok("vector", ebc[e_i])
        for tt in range(NTT):
            tpar = (idx * NTT + tt) % 2
            if idx * NTT + tt >= 2:
                wait("scalar", sT, w2T_hist[(idx * NTT + tt) - 2])
            for fc in range(FQC):
                hp = fc % 2
                if fc >= 2:
                    wait("tensor", sS, e_gelu[fc - 2])
                def w1mm(ttn, fc=fc, hp=hp, wpar=wpar, tt=tt, e_i=e_i):
                    mm = None
                    for dc in range(DC):
                        mm = ttn.matmul(ps_h[:, hp, 0:Tc], w1q_v[wpar][:, dc, fc * 128:(fc + 1) * 128],
                                        h2T[:, dc, tt * Tc:(tt + 1) * Tc],
                                        start=(dc == 0), stop=(dc == DC - 1))
                    return mm
                e_w1 = opi("tensor", w1mm, sT)
                wait("scalar", sT, e_w1)
                if fc == 0:
                    e_gelu = {}
                bcol = q * FQC + fc
                e_gelu[fc] = opi("scalar", lambda s, hp=hp, fc=fc, bcol=bcol, ep=ep, tpar=tpar: s.activation(
                    out=H_all[:, tpar, fc, :], in_=ps_h[:, hp, 0:Tc], func=AF.Gelu_apprx_tanh,
                    bias=b1_sb[ep][:, bcol:bcol + 1]), sS)
                b1_used[e_i] = e_gelu[fc]
            wait("tensor", sS, e_gelu[FQC - 1])
            if idx or tt:
                wait("tensor", sV, prev_tq)
            def w2mm(ttn, wpar=wpar, tpar=tpar):
                mm = None
                for dd in range(DC):
                    for fc in range(FQC):
                        mm = ttn.matmul(ps_y[:, dd, :], w2q_v[wpar][:, fc, dd * 128:(dd + 1) * 128],
                                        H_all[:, tpar, fc, :], start=(fc == 0), stop=(fc == FQC - 1))
                return mm
            e_w2mm = opi("tensor", w2mm, sT)
            w2T_hist[idx * NTT + tt] = e_w2mm
            wait("vector", sT, e_w2mm)
            wait("vector", sV, acc_init_v)
            for dd in range(DC):
                dp = dd % 2
                if dd >= 2:
                    wait("vector", sG, e_acc[dd - 2])
                if dd == 0:
                    e_acc = {}
                e_tq = opi("vector", lambda v, dd=dd, dp=dp, tt=tt, ep=ep: v.tensor_mul(
                    out=tq_sb[dp][:], in0=ps_y[:, dd, :], in1=w_bc[:, ep, tt, :]), sV)
                wait("gpsimd", sV, e_tq)
                e_acc[dd] = opi("gpsimd", lambda g, dd=dd, dp=dp, tt=tt: g.tensor_tensor(
                    out=acc[:, dd, tt * Tc:(tt + 1) * Tc], in0=acc[:, dd, tt * Tc:(tt + 1) * Tc],
                    in1=tq_sb[dp][:], op=ALU.add), sG)
            prev_tq = e_tq
        prev_accum_T[wpar] = sT.n
        if q == NQ - 1:
            expV[e_i] = sV.n
    P3_G_END = sG.n
    P3_V_END = sV.n

    e_gb = bcast(bc3[:, 0, :], mod_d, 5 * D, D)
    wait_tok("vector", e_gb)
    wait("vector", sV, P3_V_END)
    e_gbp = 0
    W4 = min(512, D)
    for c0 in range(0, D, W4):
        op("vector", lambda v, c0=c0: v.tensor_scalar(out=p4tmp[:, 0:W4], in0=bc3[:, 0, c0:c0 + W4],
            scalar1=0.0, scalar2=None, op0=ALU.is_ge))
        op("vector", lambda v, c0=c0: v.tensor_scalar(out=p4tmp[:, 0:W4], in0=p4tmp[:, 0:W4],
            scalar1=2.0, scalar2=None, op0=ALU.mult))
        op("vector", lambda v, c0=c0: v.tensor_scalar(out=p4tmp[:, 0:W4], in0=p4tmp[:, 0:W4],
            scalar1=-1.0, scalar2=None, op0=ALU.add))
        op("vector", lambda v, c0=c0: v.tensor_mul(out=bc3[:, 0, c0:c0 + W4],
            in0=bc3[:, 0, c0:c0 + W4], in1=p4tmp[:, 0:W4]))
        op("vector", lambda v, c0=c0: v.tensor_scalar_max(bc3[:, 0, c0:c0 + W4], bc3[:, 0, c0:c0 + W4], 0.25))
        e_gbp = opi("vector", lambda v, c0=c0: v.tensor_mul(out=bc3[:, 0, c0:c0 + W4],
            in0=bc3[:, 0, c0:c0 + W4], in1=p4tmp[:, 0:W4]), sV)

    p4ld = [0] * NTT
    p4st = [0] * NTT
    for tt in range(NTT):
        par = tt % 2
        if tt >= 1:
            wait_tok("sync", p4st[tt - 1])
            wait("sync", sV, p4V[tt - 1])
        elif tt == 0:
            wait("sync", sV, P3_V_END)
            for oc_ in range(max(0, NOWN - 2), NOWN):
                wait_tok("sync", x1BO[oc_])
        ln = opi("sync", lambda e, p=par, tt=tt: e.dma_start(
            out=x1p4[p][:], in_=x1_b[tt * Tc:(tt + 1) * Tc, :].rearrange("(g p) d -> p g d", p=128)), sxp[par], 16)
        p4ld[tt] = (sxp[par], ln)
        wait("tensor", sG, P3_G_END)
        wait_tok("vector", p4ld[tt])
        wait("vector", sV, e_gbp)
        wait("vector", sG, P3_G_END)
        e_cmb = None
        BW = min(4, DC)
        HW = BW * 64
        for g in range(G):
            for hdc in range(0, DC, BW):
                for k in range(BW):
                    dd = hdc + k
                    e_at = opi("tensor", lambda ttn, dd=dd, g=g, k=k, tt=tt: ttn.transpose(
                        ps_tf[:, k * 128:(k + 1) * 128],
                        acc[:, dd, tt * Tc + g * 128:tt * Tc + (g + 1) * 128], ident_f[:]), sT)
                wait("vector", sT, e_at)
                c0 = hdc * 128
                for h in range(2):
                    op("vector", lambda v, g=g, c0=c0, h=h, HW=HW, p=par: v.tensor_mul(
                        out=p4tmp[:, h * HW:(h + 1) * HW],
                        in0=ps_tf[:, h * HW:(h + 1) * HW],
                        in1=bc3[:, 0, c0 + h * HW:c0 + (h + 1) * HW]))
                for h in range(2):
                    e_cmb = opi("vector", lambda v, g=g, c0=c0, h=h, HW=HW, p=par: v.tensor_add(
                        out=x1p4[p][:, g, c0 + h * HW:c0 + (h + 1) * HW],
                        in0=x1p4[p][:, g, c0 + h * HW:c0 + (h + 1) * HW],
                        in1=p4tmp[:, h * HW:(h + 1) * HW]), sV)
                wait("tensor", sV, e_cmb)
        p4V_cur = sV.n
        if tt == 0:
            p4V = [0] * NTT
        p4V[tt] = p4V_cur
        wait("sync", sV, e_cmb)
        p4st[tt] = (soutp[par], opi("sync", lambda e, p=par, tt=tt: e.dma_start(
            out=out[tt * Tc:(tt + 1) * Tc, :].rearrange("(g p) d -> p g d", p=128),
            in_=x1p4[p][:]), soutp[par], 16))

    for sp in soutp:
        if sp.n:
            wait("gpsimd", sp, sp.n)

    with nc.Block() as block:
        def runner(lst):
            def run(e):
                for fn in lst:
                    fn(e)
            return run
        block.sync(runner(ctx.q["sync"]))
        block.tensor(runner(ctx.q["tensor"]))
        block.vector(runner(ctx.q["vector"]))
        block.scalar(runner(ctx.q["scalar"]))
        block.gpsimd(runner(ctx.q["gpsimd"]))

    p3stk.close()
    sstk.close()
    top.close()
    return nc


def make_in_maps(cfg, inputs):
    D, H, F, E, S, Tc = (cfg[k] for k in ("D", "H", "F", "E", "S", "Tc"))
    NCH = S // Tc
    FC = F // 128
    x = np.ascontiguousarray(inputs["L0_attn_x"], dtype=np.float32)
    B = x.shape[0]
    SH = S // 2
    c = np.ascontiguousarray(inputs["L0_c"], dtype=np.float32)
    shared = {
        "rms_row": np.ascontiguousarray(inputs["L0_attn_rms_scale"].reshape(1, D)),
        "split_row": np.ascontiguousarray(inputs["L0_attn_split_scale"].reshape(1, D)),
        "bg_row": np.ascontiguousarray(inputs["L0_ffn_bg"].reshape(1, E)),
        "wmod": np.ascontiguousarray(inputs["L0_wmod"]),
        "wk": np.ascontiguousarray(inputs["L0_attn_W_K"]),
        "wugg": np.ascontiguousarray(inputs["L0_attn_W_ugg"]),
        "wout": np.ascontiguousarray(inputs["L0_attn_W_out"]),
        "wg": np.ascontiguousarray(inputs["L0_ffn_Wg"]),
        "w1": np.ascontiguousarray(inputs["L0_ffn_W1"]),
        "b1T": np.ascontiguousarray(
            inputs["L0_ffn_b1"].reshape(E, FC, 128).transpose(0, 2, 1)),
        "w2": np.ascontiguousarray(inputs["L0_ffn_W2"]),
        "b2": np.ascontiguousarray(inputs["L0_ffn_b2"]),
    }
    ones = np.ones((128, NCH), np.float32)
    mask_j0 = ones.copy()
    mask_j0[:, :NCH // 2] = 0.0
    in_maps = []
    for core in range(2 * B):
        b, j = core // 2, core % 2
        if j == 0:
            xs = np.concatenate([x[b, 0:SH], x[b, 0:SH]], axis=0)
            mk = mask_j0
        else:
            xs = np.concatenate([x[b, 0:SH], x[b, SH:S]], axis=0)
            mk = ones
        m = dict(shared)
        m["x_seq"] = np.ascontiguousarray(xs)
        m["mask16"] = np.ascontiguousarray(mk)
        m["cvecT"] = np.ascontiguousarray(c[b].reshape(D // 128, 128).T)
        in_maps.append(m)
    return in_maps


def kernel(**inputs):
    cfg = FULL_CFG
    nc = build_program(cfg)
    in_maps = make_in_maps(cfg, inputs)
    res = run_bass_kernel_spmd(nc, in_maps, core_ids=list(range(len(in_maps))))
    B = inputs["L0_attn_x"].shape[0]
    S, D = cfg["S"], cfg["D"]
    out = np.empty((B, S, D), dtype=np.float32)
    for core in range(2 * B):
        b, j = core // 2, core % 2
        out[b, j * (S // 2):(j + 1) * (S // 2)] = res.results[core]["out"]
    return out


# version 11
# speedup vs baseline: 1.0495x; 1.0001x over previous
import numpy as np
from contextlib import ExitStack

import concourse.bass as bass
import concourse.mybir as mybir
from concourse.bass_utils import run_bass_kernel_spmd

F32 = mybir.dt.float32
BF16 = mybir.dt.bfloat16
AF = mybir.ActivationFunctionType
ALU = mybir.AluOpType
AX = mybir.AxisListType

ENGINES = ("sync", "tensor", "vector", "scalar", "gpsimd")

FULL_CFG = dict(D=1024, H=2048, F=4096, E=8, S=4096, Tc=256, NQ=4)


class Sem:
    def __init__(self, stk, nc, name):
        self.h = stk.enter_context(nc.semaphore(name))
        self.n = 0
        self.name = name


class Ctx:

    def __init__(self, nc, cfg):
        self.nc = nc
        self.cfg = cfg
        self.q = {e: [] for e in ENGINES}
        self.last_wait = {e: {} for e in ENGINES}

    def op(self, eng, fn):
        self.q[eng].append(fn)

    def opi(self, eng, fn, sem, k=1):
        sem.n += k
        def run(e, fn=fn, s=sem.h, k=k):
            return fn(e).then_inc(s, k)
        self.q[eng].append(run)
        return sem.n

    def wait(self, eng, sem, val):
        if val <= 0:
            return
        lw = self.last_wait[eng]
        if lw.get(sem.name, -1) >= val:
            return
        lw[sem.name] = val
        self.q[eng].append(lambda e, s=sem.h, v=val: e.wait_ge(s, v))


def build_program(cfg):
    nc = bass.Bass("TRN2", target_bir_lowering=False, detect_race_conditions=False)
    D, H, F, E, S, Tc, NQ = (cfg[k] for k in ("D", "H", "F", "E", "S", "Tc", "NQ"))
    DC, CC, FC = D // 128, H // 128, F // 128
    G = Tc // 128
    NCH = S // Tc
    OWN0 = NCH // 2
    NOWN = NCH - OWN0
    SH = S // 2
    FQ = F // NQ
    FQC = FQ // 128
    NTT = SH // Tc
    assert NTT == NOWN

    dt_in = lambda n, shp: nc.dram_tensor(n, list(shp), F32, kind="ExternalInput").ap()
    x_seq = dt_in("x_seq", (S, D))
    mask16 = dt_in("mask16", (128, NCH))
    cvecT = dt_in("cvecT", (128, DC))
    rms_row = dt_in("rms_row", (1, D))
    split_row = dt_in("split_row", (1, D))
    bg_row = dt_in("bg_row", (1, E))
    wmod = dt_in("wmod", (D, 6 * D))
    wk = dt_in("wk", (D, H))
    wugg = dt_in("wugg", (D, 3 * H))
    wout = dt_in("wout", (H, D))
    wg = dt_in("wg", (D, E))
    w1 = dt_in("w1", (E, D, F))
    b1T = dt_in("b1T", (E, 128, FC))
    w2 = dt_in("w2", (E, F, D))
    b2 = dt_in("b2", (E, D))

    out = nc.dram_tensor("out", [SH, D], F32, kind="ExternalOutput").ap()

    hin_b = nc.dram_tensor("hin_b", [SH, D], F32, kind="Internal").ap()
    x1_b = nc.dram_tensor("x1_b", [SH, D], F32, kind="Internal").ap()
    xnT_b = nc.dram_tensor("xnT_b", [NOWN, 128, DC, Tc], BF16, kind="Internal").ap()
    S_b = nc.dram_tensor("S_b", [NOWN, 128, CC, Tc], BF16, kind="Internal").ap()
    mod_d = nc.dram_tensor("mod_d", [1, 6 * D], F32, kind="Internal").ap()
    w_d = nc.dram_tensor("w_d", [E, SH], F32, kind="Internal").ap()

    ctx = Ctx(nc, cfg)
    top = ExitStack()
    sb = lambda n, shp, dt=F32: top.enter_context(nc.sbuf_tensor(n, list(shp), dt))
    ps = lambda n, shp, dt=F32: top.enter_context(nc.psum_tensor(n, list(shp), dt))

    ident_f = sb("ident_f", (128, 128))
    ident_b = sb("ident_b", (128, 128), BF16)
    wa = sb("wa", (128, DC * H), BF16)
    wb = sb("wb", (128, 2 * DC * H), BF16)
    bc3 = sb("bc3", (128, 3, D))
    bg_bc = sb("bg_bc", (128, E))
    mask_sb = sb("mask_sb", (128, NCH))
    cvec_sb = sb("cvec_sb", (128, DC))
    siluc_bf = sb("siluc_bf", (128, DC), BF16)
    eps5 = sb("eps5", (128, 1))
    stat = sb("stat", (128, 16))
    tailK = sb("tailK", (128, CC))
    carry = sb("carry", (128, CC))

    wk_v = wa[:].rearrange("p (c h) -> p c h", c=DC)
    h2T = wa[:].rearrange("p (c t) -> p c t", c=DC)
    wb_flat = wb[:]
    wmod_v = wb_flat.rearrange("p (c f) -> p c f", c=DC)
    ugg_v = wb_flat.rearrange("p (c f) -> p c f", c=DC)
    gout_v = wb_flat[:, 0:DC * H].rearrange("p (c h) -> p c h", c=DC)
    wout_v = wb_flat[:, DC * H:2 * DC * H].rearrange("p (c d) -> p c d", c=CC)
    QW = DC * FQ
    assert QW == FQC * D
    w1q_v = [wb_flat[:, i * QW:(i + 1) * QW].rearrange("p (c f) -> p c f", c=DC) for i in range(2)]
    w2q_v = [wb_flat[:, (2 + i) * QW:(3 + i) * QW].rearrange("p (c d) -> p c d", c=FQC) for i in range(2)]

    ps_y = ps("ps_y", (128, 8, Tc))
    ps_h = ps("ps_h", (128, 2, 512))
    ps_tf = ps("ps_tf", (128, 512))
    ps_t = ps("ps_t", (128, 1024), BF16)
    ps_g = ps_tf[:, 256:256 + E]

    sstk = ExitStack()
    sems = {}
    for name in ("sT", "sV", "sS", "sG", "sod"):
        sems[name] = Sem(sstk, nc, name)
    sT, sV, sS, sG, sod = (sems[k] for k in ("sT", "sV", "sS", "sG", "sod"))
    sxp = [Sem(sstk, nc, "sx%d" % i) for i in range(2)]
    sbop = [Sem(sstk, nc, "sbo%d" % i) for i in range(2)]
    soutp = [Sem(sstk, nc, "sout%d" % i) for i in range(2)]

    op, opi, wait = ctx.op, ctx.opi, ctx.wait

    class GQ:
        def __init__(self, n=4):
            self.sems = [Sem(sstk, nc, "sgd%d" % i) for i in range(n)]
            self.i = 0

        def dma(self, fn):
            s = self.sems[self.i % len(self.sems)]
            self.i += 1
            if s.n > 0:
                wait("gpsimd", s, s.n)
            val = opi("gpsimd", fn, s, 16)
            return (s, val)

    gq = GQ()

    def wait_tok(eng, tok):
        wait(eng, tok[0], tok[1])

    op("gpsimd", lambda g: g.memset(ident_f[:], 0.0))
    e_ms = opi("gpsimd", lambda g: g.memset(ident_b[:], 0.0), sG)
    wait("gpsimd", sG, e_ms)
    op("gpsimd", lambda g: g.affine_select(out=ident_f[:], in_=ident_f[:],
        compare_op=ALU.not_equal, fill=1.0, base=0, pattern=[[-1, 128]], channel_multiplier=1))
    op("gpsimd", lambda g: g.affine_select(out=ident_b[:], in_=ident_b[:],
        compare_op=ALU.not_equal, fill=1.0, base=0, pattern=[[-1, 128]], channel_multiplier=1))
    op("gpsimd", lambda g: g.memset(eps5[:], 1e-5))
    op("gpsimd", lambda g: g.memset(tailK[:], 0.0))
    op("gpsimd", lambda g: g.memset(carry[:], 0.0))
    e_const = opi("gpsimd", lambda g: g.engine_nop(), sG)

    opi("sync", lambda e: e.dma_start(out=mask_sb[:], in_=mask16), sxp[0], 16)
    ld0 = (sxp[0], opi("sync", lambda e: e.dma_start(out=cvec_sb[:], in_=cvecT), sxp[0], 16))
    e_wk = gq.dma(lambda g: g.dma_start(
        out=wk_v, in_=wk.rearrange("(c p) h -> p c h", p=128)))
    HF = 3 * D
    e_wm0 = gq.dma(lambda g: g.dma_start(
        out=wmod_v[:, :, 0:HF], in_=wmod[:, 0:HF].rearrange("(c p) m -> p c m", p=128)))

    wait_tok("scalar", ld0)
    e_silu = opi("scalar", lambda s: s.activation(out=cvec_sb[:], in_=cvec_sb[:], func=AF.Silu), sS)
    wait("vector", sS, e_silu)
    e_silub = opi("vector", lambda v: v.tensor_copy(out=siluc_bf[:], in_=cvec_sb[:]), sV)

    NGH = (HF + 511) // 512
    GW = HF // NGH
    assert GW * NGH == HF and GW <= 512
    wait_tok("tensor", e_wm0)
    wait("tensor", sV, e_silub)
    mm_done = []
    for half in range(2):
        if half == 1:
            wait("gpsimd", sT, mm_done[-1])
            e_wm1 = gq.dma(lambda g: g.dma_start(
                out=wmod_v[:, :, 0:HF],
                in_=wmod[:, HF:6 * D].rearrange("(c p) m -> p c m", p=128)))
            wait_tok("tensor", e_wm1)
        for j in range(NGH):
            if mm_done:
                wait("tensor", sV, e_row)
            def mmrow(t, j=j):
                mm = None
                for dc in range(DC):
                    mm = t.matmul(ps_tf[0:1, 0:GW], siluc_bf[:, dc:dc + 1],
                                  wmod_v[:, dc, j * GW:(j + 1) * GW],
                                  start=(dc == 0), stop=(dc == DC - 1))
                return mm
            ev = opi("tensor", mmrow, sT)
            mm_done.append(ev)
            col0 = half * HF + j * GW
            wait("vector", sT, ev)
            if sod.n:
                wait("vector", sod, sod.n)
            e_row = opi("vector", lambda v: v.tensor_copy(
                out=mod_sb[0:1, 0:GW], in_=ps_tf[0:1, 0:GW]), sV)
            wait("sync", sV, e_row)
            e_dd = opi("sync", lambda e, c0=col0: e.dma_start(
                out=mod_d[0:1, c0:c0 + GW], in_=mod_sb[0:1, 0:GW]), sod, 16)
    e_modd = (sod, e_dd)

    def bcast(dst_ap, src_t, src_off, n):
        src = bass.AP(src_t.tensor, src_t.offset + src_off, [[0, 128], [1, n]])
        return gq.dma(lambda g, d=dst_ap, s=src: g.dma_start(out=d, in_=s))

    wait_tok("gpsimd", e_modd)
    e_ca = bcast(bc3[:, 0, :], mod_d, 1 * D, D)
    e_sa = bcast(bc3[:, 1, :], mod_d, 0 * D, D)
    e_rm = bcast(bc3[:, 2, :], rms_row, 0, D)
    e_bg = bcast(bg_bc[:], bg_row, 0, E)

    stkA = ExitStack()
    sbA = lambda n, shp, dt=F32: stkA.enter_context(nc.sbuf_tensor(n, list(shp), dt))
    x_tok = [sbA("x_tok%d" % i, (128, G, D)) for i in range(2)]
    scrap = [sbA("scrap%d" % i, (128, G, D)) for i in range(2)]
    hin_t = sbA("hin_t", (128, G, D))
    hin = [hin_t, hin_t]
    xg_tok = [sbA("xg_tok%d" % i, (128, G, D), BF16) for i in range(2)]
    xgT = [sbA("xgT%d" % i, (128, DC, Tc), BF16) for i in range(2)]
    Gibuf = [sbA("Gibuf%d" % i, (128, Tc)) for i in range(2)]
    S_chunk_t = sbA("S_chunk", (128, CC, Tc), BF16)
    S_chunk = [S_chunk_t, S_chunk_t]
    wg_bf = sbA("wg_bf", (128, DC, E), BF16)
    wfin = sbA("wfin", (128, E))
    srt8 = sbA("srt8", (128, E))
    gz = sbA("gz", (128, E))
    wT_sb = sbA("wT_sb", (E, SH))
    stkP1 = ExitStack()
    sbP1 = lambda n, shp, dt=F32: stkP1.enter_context(nc.sbuf_tensor(n, list(shp), dt))
    mod_sb = sbP1("mod_sb", (1, 512))
    Kbuf = [sbP1("Kbuf%d" % i, (128, Tc + 1)) for i in range(2)]
    T2buf = [sbP1("T2buf%d" % i, (128, Tc)) for i in range(2)]
    Ubuf = [sbP1("Ubuf%d" % i, (128, Tc)) for i in range(2)]
    Sbuf = [sbP1("Sbuf%d" % i, (128, Tc)) for i in range(2)]

    e_sp = bcast(x_tok[0][:, 0, :], split_row, 0, D)
    for tok in (e_ca, e_sa, e_rm, e_sp):
        wait_tok("vector", tok)
    e_rsss = opi("vector", lambda v: v.tensor_mul(out=bc3[:, 2, :], in0=bc3[:, 2, :], in1=x_tok[0][:, 0, :]), sV)
    e_ca1 = opi("vector", lambda v: v.tensor_scalar(out=bc3[:, 0, :], in0=bc3[:, 0, :],
        scalar1=1.0, scalar2=None, op0=ALU.add), sV)
    P1_STATICS_V = e_ca1

    wait("gpsimd", sT, mm_done[-1])
    e_ug = gq.dma(lambda g: g.dma_start(
        out=ugg_v, in_=wugg[:, 0:2 * H].rearrange("(c p) m -> p c m", p=128)))

    chV = [0] * NCH
    chS = [0] * NCH
    chT = [0] * NCH
    chBO = [None] * (NCH + 1)
    hinBO = [None] * (NCH + 1)
    statV = [0] * NCH
    hinS = [0] * NCH
    lnV = [0] * NCH
    lnS = [0] * NCH
    psV = {}
    psS = {}
    ccV = {}
    ccG = {}
    xload = [0] * NCH
    LN = {}

    def ln_part(t):
        par = t % 2
        if t < 2:
            wait("sync", sV, P1_STATICS_V)
        if t >= 2:
            wait("sync", sV, lnV[t - 2])
            wait("sync", sS, lnS[t - 2])
        xs = x_seq[t * Tc:(t + 1) * Tc, :].rearrange("(g p) d -> p g d", p=128)
        xload[t] = (sxp[par], opi("sync", lambda e, d=x_tok[par], s=xs: e.dma_start(out=d[:], in_=s), sxp[par], 16))
        yield
        wait_tok("scalar", xload[t])
        if t >= 1:
            wait("scalar", sV, statV[t - 1])
        if t >= 2:
            wait("scalar", sV, lnV[t - 2])
        for g in range(G):
            e_sq = opi("scalar", lambda s, g=g, p=par: s.activation(
                out=scrap[p][:, g, :], in_=x_tok[p][:, g, :], func=AF.Square,
                accum_out=stat[:, 2 + g:3 + g]), sS)
        wait_tok("vector", xload[t])
        wait("vector", sV, P1_STATICS_V)
        for g in range(G):
            e_s1 = opi("vector", lambda v, g=g, p=par: v.reduce_sum(
                stat[:, 0 + g:1 + g], x_tok[p][:, g, :], axis=AX.X), sV)
        yield
        wait("vector", sV, e_s1)
        wait("vector", sS, e_sq)
        for g in range(G):
            op("vector", lambda v, g=g: v.tensor_scalar(out=stat[:, 4 + g:5 + g],
                in0=stat[:, 0 + g:1 + g], scalar1=1.0 / D, scalar2=None, op0=ALU.mult))
        for g in range(G):
            op("vector", lambda v, g=g: v.tensor_scalar(out=stat[:, 6 + g:7 + g],
                in0=stat[:, 2 + g:3 + g], scalar1=1.0 / D, scalar2=None, op0=ALU.mult))
        for g in range(G):
            op("vector", lambda v, g=g: v.tensor_mul(out=stat[:, 8 + g:9 + g],
                in0=stat[:, 4 + g:5 + g], in1=stat[:, 4 + g:5 + g]))
        for g in range(G):
            e_var = opi("vector", lambda v, g=g: v.tensor_sub(out=stat[:, 6 + g:7 + g],
                in0=stat[:, 6 + g:7 + g], in1=stat[:, 8 + g:9 + g]), sV)
        statV[t] = e_var
        wait("scalar", sV, e_var)
        for g in range(G):
            e_std = opi("scalar", lambda s, g=g: s.activation(out=stat[:, 8 + g:9 + g],
                in_=stat[:, 6 + g:7 + g], func=AF.Sqrt, bias=eps5[:]), sS)
        yield
        wait("vector", sS, e_std)
        for g in range(G):
            e_rst = opi("vector", lambda v, g=g: v.reciprocal(out=stat[:, 10 + g:11 + g],
                in_=stat[:, 8 + g:9 + g]), sV)
        wait("vector", sV, e_rst)
        for g in range(G):
            op("vector", lambda v, g=g, p=par: v.tensor_scalar(out=scrap[p][:, g, :],
                in0=x_tok[p][:, g, :], scalar1=stat[:, 4 + g:5 + g], scalar2=stat[:, 10 + g:11 + g],
                op0=ALU.subtract, op1=ALU.mult))
        yield
        if t >= 1:
            if hinBO[t - 1]:
                wait_tok("vector", hinBO[t - 1])
            wait("vector", sS, hinS[t - 1])
        for g in range(G):
            op("vector", lambda v, g=g, p=par: v.tensor_mul(out=hin[p][:, g, :],
                in0=scrap[p][:, g, :], in1=bc3[:, 0, :]))
        for g in range(G):
            e_hin = opi("vector", lambda v, g=g, p=par: v.tensor_add(out=hin[p][:, g, :],
                in0=hin[p][:, g, :], in1=bc3[:, 1, :]), sV)
        yield
        if t >= OWN0:
            oc = t - OWN0
            wait("sync", sV, e_hin)
            hinBO[t] = (sbop[par], opi("sync", lambda e, p=par, oc=oc: e.dma_start(
                out=hin_b[oc * Tc:(oc + 1) * Tc, :].rearrange("(g p) d -> p g d", p=128),
                in_=hin[p][:]), sbop[par], 16))
        wait("scalar", sV, e_hin)
        for g in range(G):
            e_rsq = opi("scalar", lambda s, g=g, p=par: s.activation(
                out=scrap[p][:, g, :], in_=hin[p][:, g, :], func=AF.Square,
                accum_out=stat[:, 12 + g:13 + g]), sS)
        hinS[t] = e_rsq
        wait("scalar", sS, e_rsq)
        for g in range(G):
            e_nrm = opi("scalar", lambda s, g=g: s.activation(out=stat[:, 14 + g:15 + g],
                in_=stat[:, 12 + g:13 + g], func=AF.Sqrt, scale=1.0 / D), sS)
        yield
        wait("vector", sS, e_nrm)
        for g in range(G):
            op("vector", lambda v, g=g: v.tensor_scalar(out=stat[:, 12 + g:13 + g],
                in0=stat[:, 14 + g:15 + g], scalar1=1e-8, scalar2=None, op0=ALU.add))
        for g in range(G):
            e_rr = opi("vector", lambda v, g=g: v.reciprocal(out=stat[:, 14 + g:15 + g],
                in_=stat[:, 12 + g:13 + g]), sV)
        wait("vector", sV, e_rr)
        for g in range(G):
            op("vector", lambda v, g=g, p=par: v.tensor_scalar(out=scrap[p][:, g, :],
                in0=hin[p][:, g, :], scalar1=stat[:, 14 + g:15 + g], scalar2=None, op0=ALU.mult))
        for g in range(G):
            e_xn = opi("vector", lambda v, g=g, p=par: v.tensor_mul(out=xg_tok[p][:, g, :],
                in0=scrap[p][:, g, :], in1=bc3[:, 2, :]), sV)
        LN[t] = dict(e_hin=e_hin, e_xn=e_xn)
        lnV[t] = sV.n
        lnS[t] = sS.n

    def ln_transpose(t):
        par = t % 2
        e_xn = LN[t]["e_xn"]
        wait("tensor", sV, e_xn)
        if t >= 2:
            if chBO[t - 2]:
                wait_tok("tensor", chBO[t - 2])
                wait_tok("vector", chBO[t - 2])
            wait("vector", sT, chT[t - 2])
        n_tp = G * DC
        e_xnT = None
        for b0 in range(0, n_tp, 4):
            for k in range(4):
                idx = b0 + k
                g, dc = idx // DC, idx % DC
                e_tp = opi("tensor", lambda tt, g=g, dc=dc, k=k, p=par: tt.transpose(
                    ps_t[:, k * 128:(k + 1) * 128], xg_tok[p][:, g, dc * 128:(dc + 1) * 128],
                    ident_b[:]), sT)
            wait("vector", sT, e_tp)
            g0, dc0 = b0 // DC, b0 % DC
            if dc0 + 4 <= DC:
                e_xnT = opi("vector", lambda v, g=g0, dc=dc0, p=par: v.tensor_copy(
                    out=xgT[p][:, dc:dc + 4, g * 128:(g + 1) * 128],
                    in_=ps_t[:, 0:512].rearrange("p (k t) -> p k t", k=4)), sV)
            else:
                for k in range(4):
                    idx = b0 + k
                    g, dc = idx // DC, idx % DC
                    e_xnT = opi("vector", lambda v, g=g, dc=dc, k=k, p=par: v.tensor_copy(
                        out=xgT[p][:, dc, g * 128:(g + 1) * 128],
                        in_=ps_t[:, k * 128:(k + 1) * 128]), sV)
            if b0 + 4 < n_tp:
                wait("tensor", sV, e_xnT)
        LN[t]["e_xnT"] = e_xnT
        if t >= OWN0:
            oc = t - OWN0
            wait("sync", sV, e_xnT)
            chBO[t] = (sbop[par], opi("sync", lambda e, p=par, oc=oc: e.dma_start(
                out=xnT_b[oc], in_=xgT[p][:]), sbop[par], 16))

    def cc_part(t):
        par = t % 2
        own = t >= OWN0
        e_xnT = LN[t]["e_xnT"]
        e_hin = LN[t]["e_hin"]
        wait_tok("tensor", e_ug)
        wait_tok("tensor", e_wk)
        for cc in range(CC):
            cp = cc % 2
            slot_k, slot_u, slot_gi = 4 * cp, 4 * cp + 1, 4 * cp + 2
            if (t, cc) > (0, 1):
                pse = psS.get(cp)
                if pse:
                    wait("tensor", sS, pse)
            wait("tensor", sV, e_xnT)
            def proj(tt, p=par, cc=cc, sk=slot_k, su=slot_u, sgi=slot_gi):
                mm = None
                for dc in range(DC):
                    mm = tt.matmul(ps_y[:, sk, :], wk_v[:, dc, cc * 128:(cc + 1) * 128],
                                   xgT[p][:, dc, :], start=(dc == 0), stop=(dc == DC - 1))
                for dc in range(DC):
                    mm = tt.matmul(ps_y[:, su, :], ugg_v[:, dc, cc * 128:(cc + 1) * 128],
                                   xgT[p][:, dc, :], start=(dc == 0), stop=(dc == DC - 1))
                for dc in range(DC):
                    mm = tt.matmul(ps_y[:, sgi, :], ugg_v[:, dc, H + cc * 128:H + (cc + 1) * 128],
                                   xgT[p][:, dc, :], start=(dc == 0), stop=(dc == DC - 1))
                return mm
            e_mm = opi("tensor", proj, sT)
            wait("scalar", sT, e_mm)
            if cp in ccV:
                wait("scalar", sV, ccV[cp])
            if cp in ccG:
                wait("scalar", sG, ccG[cp])
            e_sk = opi("scalar", lambda s, cp=cp, sk=slot_k: s.activation(
                out=Kbuf[cp][:, 1:Tc + 1], in_=ps_y[:, sk, :], func=AF.Sigmoid), sS)
            e_um = opi("scalar", lambda s, cp=cp, su=slot_u, t=t: s.activation(
                out=T2buf[cp][:], in_=ps_y[:, su, :], func=AF.Copy,
                scale=mask_sb[:, t:t + 1]), sS)
            e_sg = opi("scalar", lambda s, cp=cp, sgi=slot_gi: s.activation(
                out=Gibuf[cp][:], in_=ps_y[:, sgi, :], func=AF.Sigmoid), sS)
            e_km = opi("scalar", lambda s, cp=cp, sk=slot_k: s.activation(
                out=Ubuf[cp][:], in_=ps_y[:, sk, :], func=AF.Sigmoid,
                scale=-1.0), sS)
            psS[cp] = e_km
            wait("gpsimd", sS, e_km)
            op("gpsimd", lambda g, cp=cp: g.tensor_tensor(out=T2buf[cp][:],
                in0=T2buf[cp][:], in1=Gibuf[cp][:], op=ALU.mult))
            e_ut = opi("gpsimd", lambda g, cp=cp: g.tensor_tensor(out=T2buf[cp][:],
                in0=T2buf[cp][:], in1=Ubuf[cp][:], op=ALU.mult), sG)
            ccG[cp] = e_ut
            wait("vector", sS, e_sk)
            op("vector", lambda v, cp=cp, cc=cc: v.tensor_copy(out=Kbuf[cp][:, 0:1],
                in_=tailK[:, cc:cc + 1]))
            e_tl = opi("vector", lambda v, cp=cp, cc=cc: v.tensor_copy(
                out=tailK[:, cc:cc + 1], in_=Kbuf[cp][:, Tc:Tc + 1]), sV)
            wait("vector", sG, e_ut)
            e_scan = opi("vector", lambda v, cp=cp, cc=cc: v.tensor_tensor_scan(
                out=Sbuf[cp][:], data0=Kbuf[cp][:, 0:Tc], data1=T2buf[cp][:],
                initial=carry[:, cc:cc + 1], op0=ALU.mult, op1=ALU.add), sV)
            wait("vector", sV, e_scan)
            e_cr = opi("vector", lambda v, cp=cp, cc=cc: v.tensor_copy(
                out=carry[:, cc:cc + 1], in_=Sbuf[cp][:, Tc - 1:Tc]), sV)
            if own:
                if cc == 0 and t - 1 >= OWN0 and chBO[t - 1]:
                    wait_tok("vector", chBO[t - 1])
                e_sS = opi("vector", lambda v, cp=cp, cc=cc, p=par: v.tensor_copy(
                    out=S_chunk[p][:, cc, :], in_=Sbuf[cp][:]), sV)
            ccV[cp] = sV.n
            yield
        chV[t] = sV.n
        chS[t] = sS.n
        chT[t] = sT.n
        if own:
            oc = t - OWN0
            wait("sync", sV, sV.n)
            bo = opi("sync", lambda e, p=par, oc=oc: e.dma_start(
                out=S_b[oc], in_=S_chunk[p][:]), sbop[par], 16)
            chBO[t] = (sbop[par], bo)

    for _ in ln_part(0):
        pass
    ln_transpose(0)
    for _ in ln_part(1):
        pass
    for t in range(NCH):
        gc = cc_part(t)
        gl = ln_part(t + 2) if t + 2 < NCH else None
        for _ in gc:
            if gl is not None:
                try:
                    next(gl)
                except StopIteration:
                    gl = None
        while gl is not None:
            try:
                next(gl)
            except StopIteration:
                gl = None
        if t + 1 < NCH:
            ln_transpose(t + 1)

    P1_T_END = sT.n
    P1_V_END = sV.n
    P1_S_END = sS.n

    wait("gpsimd", sT, P1_T_END)
    e_go = gq.dma(lambda g: g.dma_start(
        out=gout_v, in_=wugg[:, 2 * H:3 * H].rearrange("(c p) m -> p c m", p=128)))
    e_wo = gq.dma(lambda g: g.dma_start(
        out=wout_v, in_=wout.rearrange("(c p) d -> p c d", p=128)))
    e_wg = gq.dma(lambda g: g.dma_start(
        out=wg_bf[:], in_=wg.rearrange("(c p) e -> p c e", p=128)))
    e_cb = bcast(bc3[:, 0, :], mod_d, 4 * D, D)
    e_sb2 = bcast(bc3[:, 1, :], mod_d, 3 * D, D)
    e_ga = bcast(bc3[:, 2, :], mod_d, 2 * D, D)
    wait("vector", sV, P1_V_END)
    wait("vector", sS, P1_S_END)
    wait("vector", sT, P1_T_END)
    for tok in (e_cb, e_sb2, e_ga):
        wait_tok("vector", tok)
    e_cb1 = opi("vector", lambda v: v.tensor_scalar(out=bc3[:, 0, :], in0=bc3[:, 0, :],
        scalar1=1.0, scalar2=None, op0=ALU.add), sV)
    op("vector", lambda v: v.tensor_scalar(out=scrap[0][:, 0, :], in0=bc3[:, 2, :],
        scalar1=0.0, scalar2=None, op0=ALU.is_ge))
    op("vector", lambda v: v.tensor_scalar(out=scrap[0][:, 0, :], in0=scrap[0][:, 0, :],
        scalar1=2.0, scalar2=None, op0=ALU.mult))
    op("vector", lambda v: v.tensor_scalar(out=scrap[0][:, 0, :], in0=scrap[0][:, 0, :],
        scalar1=-1.0, scalar2=None, op0=ALU.add))
    op("vector", lambda v: v.tensor_mul(out=bc3[:, 2, :], in0=bc3[:, 2, :],
        in1=scrap[0][:, 0, :]))
    op("vector", lambda v: v.tensor_scalar_max(bc3[:, 2, :], bc3[:, 2, :], 0.25))
    e_gap = opi("vector", lambda v: v.tensor_mul(out=bc3[:, 2, :], in0=bc3[:, 2, :],
        in1=scrap[0][:, 0, :]), sV)
    P2_STATICS_V = e_gap

    stkP1.close()
    y_sb = sbA("y_sb", (128, DC, Tc))
    x1_t = sbA("x1_t", (128, G, D))
    x1 = [x1_t, x1_t]
    s_all = sbA("s_all", (128, CC, Tc), BF16)

    p2Sdone = {}
    p2hin = {}
    p2sT = {}
    aT = {}
    bV = {}
    gateZ = {}
    yeV = {}
    x1BO = {}
    GB = {}

    wait_tok("tensor", e_wo)
    wait_tok("tensor", e_go)
    wait_tok("tensor", e_wg)

    def genA(oc):
        par = oc % 2
        if oc >= 2:
            wait("sync", sV, bV[oc - 2])
            wait("sync", sT, aT[oc - 2])
        else:
            wait("sync", sV, P1_V_END)
            wait("sync", sT, P1_T_END)
            wait("sync", sS, P1_S_END)
            for tok in (chBO[NCH - 1], chBO[NCH - 2]):
                if tok:
                    wait_tok("sync", tok)
        opi("sync", lambda e, p=par, oc=oc: e.dma_start(out=xgT[p][:], in_=xnT_b[oc]), sxp[par], 16)
        if oc >= 1:
            wait("sync", sV, p2Sdone[oc - 1])
            wait("sync", sV, p2hin[oc - 1])
            wait("sync", sS, GB.get(oc - 1, {}).get("bS", 0))
        opi("sync", lambda e, p=par, oc=oc: e.dma_start(out=S_chunk[p][:], in_=S_b[oc]), sxp[par], 16)
        opi("sync", lambda e, p=par, oc=oc: e.dma_start(
            out=hin[p][:], in_=hin_b[oc * Tc:(oc + 1) * Tc, :].rearrange("(g p) d -> p g d", p=128)), sxp[par], 16)
        ln = opi("sync", lambda e, p=par, oc=oc: e.dma_start(
            out=x_tok[p][:], in_=x_seq[(OWN0 + oc) * Tc:(OWN0 + oc + 1) * Tc, :]
            .rearrange("(g p) d -> p g d", p=128)), sxp[par], 16)
        yield
        wait("tensor", sxp[par], ln)
        if oc >= 1:
            wait("vector", sT, p2sT[oc - 1])
        e_go_evac, e_s_done = {}, {}
        for cc in range(CC):
            cp = cc % 2
            if cc >= 2:
                wait("tensor", sS, e_go_evac[cc - 2])
            def gproj(tt, p=par, cc=cc, cp=cp):
                mm = None
                for dc in range(DC):
                    mm = tt.matmul(ps_h[:, cp, 0:Tc], gout_v[:, dc, cc * 128:(cc + 1) * 128],
                                   xgT[p][:, dc, :], start=(dc == 0), stop=(dc == DC - 1))
                return mm
            e_gmm = opi("tensor", gproj, sT)
            wait("scalar", sT, e_gmm)
            if cc >= 2:
                wait("scalar", sV, e_s_done[cc - 2])
            e_go_evac[cc] = opi("scalar", lambda s, cp=cp: s.activation(
                out=Gibuf[cp][:], in_=ps_h[:, cp, 0:Tc], func=AF.Sigmoid), sS)
            wait("vector", sS, e_go_evac[cc])
            e_s_done[cc] = opi("vector", lambda v, cp=cp, cc=cc, p=par: v.tensor_tensor(
                out=s_all[:, cc, :], in0=S_chunk[p][:, cc, :], in1=Gibuf[cp][:],
                op=ALU.mult), sV)
            yield
        p2Sdone[oc] = e_s_done[CC - 1]
        wait("tensor", sV, e_s_done[CC - 1])
        if oc >= 1:
            wait("tensor", sV, yeV[oc - 1])
        def ymm(tt):
            mm = None
            for dd in range(DC):
                for cc in range(CC):
                    mm = tt.matmul(ps_y[:, dd, :], wout_v[:, cc, dd * 128:(dd + 1) * 128],
                                   s_all[:, cc, :], start=(cc == 0), stop=(cc == CC - 1))
            return mm
        e_ymm = opi("tensor", ymm, sT)
        p2sT[oc] = e_ymm
        aT[oc] = sT.n

    def genB(oc):
        par = oc % 2
        e_ymm = p2sT[oc]
        wait("vector", sT, e_ymm)
        for dd in range(DC):
            e_ye = opi("vector", lambda v, dd=dd: v.tensor_copy(
                out=y_sb[:, dd, :], in_=ps_y[:, dd, :]), sV)
        yeV[oc] = e_ye
        wait("tensor", sV, e_ye)
        wait("vector", sV, P2_STATICS_V)
        if oc >= 1:
            wait_tok("vector", x1BO[oc - 1])
        e_x1 = None
        BW = min(4, DC)
        W = BW * 128
        for g in range(G):
            for hdc in range(0, DC, BW):
                for k in range(BW):
                    dd = hdc + k
                    e_yt = opi("tensor", lambda tt, dd=dd, g=g, k=k: tt.transpose(
                        ps_tf[:, k * 128:(k + 1) * 128], y_sb[:, dd, g * 128:(g + 1) * 128],
                        ident_f[:]), sT)
                wait("vector", sT, e_yt)
                c0 = hdc * 128
                op("vector", lambda v, g=g, c0=c0, W=W, p=par: v.tensor_add(
                    out=scrap[p][:, g, c0:c0 + W], in0=ps_tf[:, 0:W],
                    in1=hin[p][:, g, c0:c0 + W]))
                op("vector", lambda v, g=g, c0=c0, W=W, p=par: v.tensor_mul(
                    out=scrap[p][:, g, c0:c0 + W], in0=scrap[p][:, g, c0:c0 + W],
                    in1=bc3[:, 2, c0:c0 + W]))
                e_x1 = opi("vector", lambda v, g=g, c0=c0, W=W, p=par: v.tensor_add(
                    out=x1[p][:, g, c0:c0 + W], in0=scrap[p][:, g, c0:c0 + W],
                    in1=x_tok[p][:, g, c0:c0 + W]), sV)
                if g * DC * 128 + c0 + W < G * D:
                    wait("tensor", sV, e_x1)
        p2hin[oc] = e_x1
        wait("sync", sV, e_x1)
        x1BO[oc] = (sbop[par], opi("sync", lambda e, p=par, oc=oc: e.dma_start(
            out=x1_b[oc * Tc:(oc + 1) * Tc, :].rearrange("(g p) d -> p g d", p=128),
            in_=x1[p][:]), sbop[par], 16))
        yield "x1"
        wait("scalar", sV, e_x1)
        if oc >= 1:
            wait("scalar", sV, GB[oc - 1]["statV"])
        for g in range(G):
            e_sq = opi("scalar", lambda s, g=g, p=par: s.activation(
                out=scrap[p][:, g, :], in_=x1[p][:, g, :], func=AF.Square,
                accum_out=stat[:, 2 + g:3 + g]), sS)
        for g in range(G):
            e_s1 = opi("vector", lambda v, g=g, p=par: v.reduce_sum(
                stat[:, 0 + g:1 + g], x1[p][:, g, :], axis=AX.X), sV)
        yield "ln"
        wait("vector", sV, e_s1)
        wait("vector", sS, e_sq)
        for g in range(G):
            op("vector", lambda v, g=g: v.tensor_scalar(out=stat[:, 4 + g:5 + g],
                in0=stat[:, 0 + g:1 + g], scalar1=1.0 / D, scalar2=None, op0=ALU.mult))
        for g in range(G):
            op("vector", lambda v, g=g: v.tensor_scalar(out=stat[:, 6 + g:7 + g],
                in0=stat[:, 2 + g:3 + g], scalar1=1.0 / D, scalar2=None, op0=ALU.mult))
        for g in range(G):
            op("vector", lambda v, g=g: v.tensor_mul(out=stat[:, 8 + g:9 + g],
                in0=stat[:, 4 + g:5 + g], in1=stat[:, 4 + g:5 + g]))
        for g in range(G):
            e_var = opi("vector", lambda v, g=g: v.tensor_sub(out=stat[:, 6 + g:7 + g],
                in0=stat[:, 6 + g:7 + g], in1=stat[:, 8 + g:9 + g]), sV)
        GB.setdefault(oc, {})["statV"] = e_var
        wait("scalar", sV, e_var)
        for g in range(G):
            e_std = opi("scalar", lambda s, g=g: s.activation(out=stat[:, 8 + g:9 + g],
                in_=stat[:, 6 + g:7 + g], func=AF.Sqrt, bias=eps5[:]), sS)
        yield "std"
        wait("vector", sS, e_std)
        for g in range(G):
            e_rst = opi("vector", lambda v, g=g: v.reciprocal(out=stat[:, 10 + g:11 + g],
                in_=stat[:, 8 + g:9 + g]), sV)
        wait("vector", sV, e_rst)
        for g in range(G):
            op("vector", lambda v, g=g, p=par: v.tensor_scalar(out=scrap[p][:, g, :],
                in0=x1[p][:, g, :], scalar1=stat[:, 4 + g:5 + g], scalar2=stat[:, 10 + g:11 + g],
                op0=ALU.subtract, op1=ALU.mult))
        for g in range(G):
            op("vector", lambda v, g=g, p=par: v.tensor_mul(out=scrap[p][:, g, :],
                in0=scrap[p][:, g, :], in1=bc3[:, 0, :]))
        for g in range(G):
            e_h2 = opi("vector", lambda v, g=g, p=par: v.tensor_add(out=xg_tok[p][:, g, :],
                in0=scrap[p][:, g, :], in1=bc3[:, 1, :]), sV)
        GB[oc]["bS"] = sS.n
        yield "h2"
        wait("tensor", sV, e_h2)
        for b0 in range(0, G * DC, 4):
            for k in range(4):
                idx = b0 + k
                g, dc = idx // DC, idx % DC
                e_tp = opi("tensor", lambda tt, g=g, dc=dc, k=k, p=par: tt.transpose(
                    ps_t[:, k * 128:(k + 1) * 128], xg_tok[p][:, g, dc * 128:(dc + 1) * 128],
                    ident_b[:]), sT)
            wait("vector", sT, e_tp)
            for k in range(4):
                idx = b0 + k
                g, dc = idx // DC, idx % DC
                col = oc * Tc + g * 128
                e_h2T = opi("vector", lambda v, g=g, dc=dc, k=k, col=col: v.tensor_copy(
                    out=h2T[:, dc, col:col + 128], in_=ps_t[:, k * 128:(k + 1) * 128]), sV)
            if b0 + 4 < G * DC:
                wait("tensor", sV, e_h2T)
            yield "tp"
        for g in range(G):
            col = oc * Tc + g * 128
            def gatemm(tt, col=col):
                mm = None
                for dc in range(DC):
                    mm = tt.matmul(ps_g, h2T[:, dc, col:col + 128], wg_bf[:, dc, :],
                                   start=(dc == 0), stop=(dc == DC - 1))
                return mm
            wait("tensor", sV, e_h2T)
            if e_wT_evac:
                wait("tensor", sV, e_wT_evac[-1])
            e_gl = opi("tensor", gatemm, sT)
            wait("vector", sT, e_gl)
            e_z = opi("vector", lambda v: v.tensor_add(out=gz[:], in0=ps_g, in1=bg_bc[:]), sV)
            gateZ[oc] = e_z
            wait("vector", sV, e_z)
            e_nm = opi("vector", lambda v: v.reduce_max(stat[:, 0:1], gz[:], axis=AX.X, negate=True), sV)
            wait("scalar", sV, e_nm)
            e_ex = opi("scalar", lambda s: s.activation(out=srt8[:], in_=gz[:], func=AF.Exp,
                bias=stat[:, 0:1], accum_out=stat[:, 1:2]), sS)
            wait("vector", sS, e_ex)
            e_rc = opi("vector", lambda v: v.reciprocal(out=stat[:, 2:3], in_=stat[:, 1:2]), sV)
            wait("vector", sV, e_rc)
            e_gt = opi("vector", lambda v: v.tensor_scalar(out=gz[:], in0=srt8[:],
                scalar1=stat[:, 2:3], scalar2=None, op0=ALU.mult), sV)
            wait("vector", sV, e_gt)
            e_m8 = opi("vector", lambda v: v.max(out=srt8[:], in_=gz[:]), sV)
            wait("vector", sV, e_m8)
            op("vector", lambda v: v.tensor_scalar(out=wfin[:], in0=gz[:],
                scalar1=srt8[:, 1:2], scalar2=None, op0=ALU.subtract))
            op("vector", lambda v: v.tensor_add(out=stat[:, 3:4], in0=srt8[:, 0:1], in1=srt8[:, 1:2]))
            op("vector", lambda v: v.tensor_scalar(out=wfin[:], in0=wfin[:],
                scalar1=0.0, scalar2=None, op0=ALU.is_ge))
            op("vector", lambda v: v.tensor_scalar(out=stat[:, 3:4], in0=stat[:, 3:4],
                scalar1=1e-8, scalar2=None, op0=ALU.add))
            op("vector", lambda v: v.tensor_mul(out=wfin[:], in0=wfin[:], in1=gz[:]))
            e_dn = opi("vector", lambda v: v.reciprocal(out=stat[:, 4:5], in_=stat[:, 3:4]), sV)
            wait("vector", sV, e_dn)
            e_wf = opi("vector", lambda v: v.tensor_scalar(out=wfin[:], in0=wfin[:],
                scalar1=stat[:, 4:5], scalar2=None, op0=ALU.mult), sV)
            wait("tensor", sV, e_wf)
            e_wt = opi("tensor", lambda tt: tt.transpose(ps_tf[0:E, 0:128], wfin[:], ident_f[:]), sT)
            wait("vector", sT, e_wt)
            e_wT_evac.append(opi("vector", lambda v, col=col: v.tensor_copy(
                out=wT_sb[:, col:col + 128], in_=ps_tf[0:E, 0:128]), sV))
            yield "gate"
        GB[oc]["wT"] = True
        bV[oc] = sV.n

    e_wT_evac = []
    import os
    OVL_STAGES = os.environ.get("P2_OVL_STAGES", "ln,std,h2").split(",")
    gB = None
    for oc in range(NOWN):
        gA = genA(oc)
        for _ in gA:
            if gB is not None:
                try:
                    nt = next(gB)
                    if nt not in OVL_STAGES:
                        while True:
                            next(gB)
                except StopIteration:
                    gB = None
        while gB is not None:
            try:
                next(gB)
            except StopIteration:
                gB = None
        gB = genB(oc)
        while True:
            tag = next(gB, None)
            if tag is None:
                gB = None
                break
            if tag == "x1":
                break
    while gB is not None:
        try:
            next(gB)
        except StopIteration:
            gB = None

    P2_V_END = sV.n
    P2_T_END = sT.n
    wait("sync", sV, P2_V_END)
    e_wd = (sod, opi("sync", lambda e: e.dma_start(out=w_d, in_=wT_sb[:]), sod, 16))

    stkA.close()
    acc = sb("acc", (128, DC, SH))
    p3stk = ExitStack()
    sb3 = lambda n, shp, dt=F32: p3stk.enter_context(nc.sbuf_tensor(n, list(shp), dt))
    w_bc = sb3("w_bc", (128, 2, NTT, Tc), BF16)
    H_all = sb3("H_all", (128, 2, FQC, Tc), BF16)
    tq_sb = [sb3("tq_sb%d" % i, (128, Tc)) for i in range(2)]
    b1_sb = [sb3("b1_sb%d" % i, (128, FC)) for i in range(2)]
    b2_bf = sb3("b2_bf", (E, D), BF16)
    wT_bf = sb3("wT_bf", (E, SH), BF16)
    x1p4_t = sb3("x1p4", (128, G, D))
    x1p4 = [x1p4_t, x1p4_t]
    p4tmp = sb3("p4tmp", (128, 512))

    wait_tok("gpsimd", e_wd)
    e_b2 = gq.dma(lambda g: g.dma_start(out=b2_bf[:], in_=b2))
    e_wtb = gq.dma(lambda g: g.dma_start(out=wT_bf[:], in_=w_d))

    wait_tok("tensor", e_wtb)
    wait_tok("tensor", e_b2)
    wait("tensor", sT, P2_T_END)
    wait("tensor", sV, P2_V_END)
    acc_init_v = 0
    for tt in range(NTT):
        def b2mm(ttn, tt=tt):
            mm = None
            for dd in range(DC):
                mm = ttn.matmul(ps_y[:, dd, :], b2_bf[0:E, dd * 128:(dd + 1) * 128],
                                wT_bf[0:E, tt * Tc:(tt + 1) * Tc],
                                start=True, stop=True)
            return mm
        e_b2mm = opi("tensor", b2mm, sT)
        wait("vector", sT, e_b2mm)
        for dd in range(DC):
            acc_init_v = opi("vector", lambda v, dd=dd, tt=tt: v.tensor_copy(
                out=acc[:, dd, tt * Tc:(tt + 1) * Tc], in_=ps_y[:, dd, :]), sV)
        if tt + 1 < NTT:
            wait("tensor", sV, acc_init_v)

    pairs = [(e, q) for e in range(E) for q in range(NQ)]
    loaded = {}
    w2T_hist = {}
    prev_accum_T = {}
    expV = {}
    b1_used = {}
    ebc = {}
    eb1t = {}
    wait("tensor", sV, acc_init_v)

    def issue_pair_loads(idx):
        e_i, q = pairs[idx]
        wpar = idx % 2
        pt = prev_accum_T.get(wpar)
        wait("gpsimd", sT, pt if pt else P2_T_END)
        lw1 = gq.dma(lambda g, e_i=e_i, q=q, wpar=wpar: g.dma_start(
            out=w1q_v[wpar], in_=w1[e_i][:, q * FQ:(q + 1) * FQ]
            .rearrange("(c p) f -> p c f", p=128)))
        lw2 = gq.dma(lambda g, e_i=e_i, q=q, wpar=wpar: g.dma_start(
            out=w2q_v[wpar], in_=w2[e_i][q * FQ:(q + 1) * FQ, :]
            .rearrange("(c p) d -> p c d", p=128)))
        loaded[idx] = (lw1, lw2)

    def issue_expert_smalls(e_i):
        ep = e_i % 2
        if e_i >= 2:
            wait("gpsimd", sV, expV[e_i - 2])
            wait("gpsimd", sS, b1_used[e_i - 2])
        ebc[e_i] = gq.dma(lambda g, e_i=e_i, ep=ep: g.dma_start(
            out=w_bc[:, ep].rearrange("p a t -> p (a t)"),
            in_=bass.AP(w_d.tensor, e_i * SH, [[0, 128], [1, SH]])))
        eb1t[e_i] = gq.dma(lambda g, e_i=e_i, ep=ep: g.dma_start(
            out=b1_sb[ep][:], in_=b1T[e_i]))

    issue_expert_smalls(0)
    issue_pair_loads(0)
    for idx, (e_i, q) in enumerate(pairs):
        wpar = idx % 2
        ep = e_i % 2
        if idx + 1 < len(pairs):
            if pairs[idx + 1][1] == 0:
                issue_expert_smalls(e_i + 1)
            issue_pair_loads(idx + 1)
        lw1, lw2 = loaded[idx]
        wait_tok("tensor", lw1)
        wait_tok("tensor", lw2)
        if q == 0:
            wait_tok("scalar", eb1t[e_i])
            wait_tok("vector", ebc[e_i])
        for tt in range(NTT):
            tpar = (idx * NTT + tt) % 2
            if idx * NTT + tt >= 2:
                wait("scalar", sT, w2T_hist[(idx * NTT + tt) - 2])
            for fc in range(FQC):
                hp = fc % 2
                if fc >= 2:
                    wait("tensor", sS, e_gelu[fc - 2])
                def w1mm(ttn, fc=fc, hp=hp, wpar=wpar, tt=tt, e_i=e_i):
                    mm = None
                    for dc in range(DC):
                        mm = ttn.matmul(ps_h[:, hp, 0:Tc], w1q_v[wpar][:, dc, fc * 128:(fc + 1) * 128],
                                        h2T[:, dc, tt * Tc:(tt + 1) * Tc],
                                        start=(dc == 0), stop=(dc == DC - 1))
                    return mm
                e_w1 = opi("tensor", w1mm, sT)
                wait("scalar", sT, e_w1)
                if fc == 0:
                    e_gelu = {}
                bcol = q * FQC + fc
                e_gelu[fc] = opi("scalar", lambda s, hp=hp, fc=fc, bcol=bcol, ep=ep, tpar=tpar: s.activation(
                    out=H_all[:, tpar, fc, :], in_=ps_h[:, hp, 0:Tc], func=AF.Gelu_apprx_tanh,
                    bias=b1_sb[ep][:, bcol:bcol + 1]), sS)
                b1_used[e_i] = e_gelu[fc]
            wait("tensor", sS, e_gelu[FQC - 1])
            if idx or tt:
                wait("tensor", sV, prev_tq)
            def w2mm(ttn, wpar=wpar, tpar=tpar):
                mm = None
                for dd in range(DC):
                    for fc in range(FQC):
                        mm = ttn.matmul(ps_y[:, dd, :], w2q_v[wpar][:, fc, dd * 128:(dd + 1) * 128],
                                        H_all[:, tpar, fc, :], start=(fc == 0), stop=(fc == FQC - 1))
                return mm
            e_w2mm = opi("tensor", w2mm, sT)
            w2T_hist[idx * NTT + tt] = e_w2mm
            wait("vector", sT, e_w2mm)
            wait("vector", sV, acc_init_v)
            for dd in range(DC):
                dp = dd % 2
                if dd >= 2:
                    wait("vector", sG, e_acc[dd - 2])
                if dd == 0:
                    e_acc = {}
                e_tq = opi("vector", lambda v, dd=dd, dp=dp, tt=tt, ep=ep: v.tensor_mul(
                    out=tq_sb[dp][:], in0=ps_y[:, dd, :], in1=w_bc[:, ep, tt, :]), sV)
                wait("gpsimd", sV, e_tq)
                e_acc[dd] = opi("gpsimd", lambda g, dd=dd, dp=dp, tt=tt: g.tensor_tensor(
                    out=acc[:, dd, tt * Tc:(tt + 1) * Tc], in0=acc[:, dd, tt * Tc:(tt + 1) * Tc],
                    in1=tq_sb[dp][:], op=ALU.add), sG)
            prev_tq = e_tq
        prev_accum_T[wpar] = sT.n
        if q == NQ - 1:
            expV[e_i] = sV.n
    P3_G_END = sG.n
    P3_V_END = sV.n

    e_gb = bcast(bc3[:, 0, :], mod_d, 5 * D, D)
    wait_tok("vector", e_gb)
    wait("vector", sV, P3_V_END)
    e_gbp = 0
    W4 = min(512, D)
    for c0 in range(0, D, W4):
        op("vector", lambda v, c0=c0: v.tensor_scalar(out=p4tmp[:, 0:W4], in0=bc3[:, 0, c0:c0 + W4],
            scalar1=0.0, scalar2=None, op0=ALU.is_ge))
        op("vector", lambda v, c0=c0: v.tensor_scalar(out=p4tmp[:, 0:W4], in0=p4tmp[:, 0:W4],
            scalar1=2.0, scalar2=None, op0=ALU.mult))
        op("vector", lambda v, c0=c0: v.tensor_scalar(out=p4tmp[:, 0:W4], in0=p4tmp[:, 0:W4],
            scalar1=-1.0, scalar2=None, op0=ALU.add))
        op("vector", lambda v, c0=c0: v.tensor_mul(out=bc3[:, 0, c0:c0 + W4],
            in0=bc3[:, 0, c0:c0 + W4], in1=p4tmp[:, 0:W4]))
        op("vector", lambda v, c0=c0: v.tensor_scalar_max(bc3[:, 0, c0:c0 + W4], bc3[:, 0, c0:c0 + W4], 0.25))
        e_gbp = opi("vector", lambda v, c0=c0: v.tensor_mul(out=bc3[:, 0, c0:c0 + W4],
            in0=bc3[:, 0, c0:c0 + W4], in1=p4tmp[:, 0:W4]), sV)

    p4ld = [0] * NTT
    p4st = [0] * NTT
    for tt in range(NTT):
        par = tt % 2
        if tt >= 1:
            wait_tok("sync", p4st[tt - 1])
            wait("sync", sV, p4V[tt - 1])
        elif tt == 0:
            wait("sync", sV, P3_V_END)
            for oc_ in range(max(0, NOWN - 2), NOWN):
                wait_tok("sync", x1BO[oc_])
        ln = opi("sync", lambda e, p=par, tt=tt: e.dma_start(
            out=x1p4[p][:], in_=x1_b[tt * Tc:(tt + 1) * Tc, :].rearrange("(g p) d -> p g d", p=128)), sxp[par], 16)
        p4ld[tt] = (sxp[par], ln)
        wait("tensor", sG, P3_G_END)
        wait_tok("vector", p4ld[tt])
        wait("vector", sV, e_gbp)
        wait("vector", sG, P3_G_END)
        e_cmb = None
        BW = min(4, DC)
        HW = BW * 64
        for g in range(G):
            for hdc in range(0, DC, BW):
                for k in range(BW):
                    dd = hdc + k
                    e_at = opi("tensor", lambda ttn, dd=dd, g=g, k=k, tt=tt: ttn.transpose(
                        ps_tf[:, k * 128:(k + 1) * 128],
                        acc[:, dd, tt * Tc + g * 128:tt * Tc + (g + 1) * 128], ident_f[:]), sT)
                wait("vector", sT, e_at)
                c0 = hdc * 128
                for h in range(2):
                    op("vector", lambda v, g=g, c0=c0, h=h, HW=HW, p=par: v.tensor_mul(
                        out=p4tmp[:, h * HW:(h + 1) * HW],
                        in0=ps_tf[:, h * HW:(h + 1) * HW],
                        in1=bc3[:, 0, c0 + h * HW:c0 + (h + 1) * HW]))
                for h in range(2):
                    e_cmb = opi("vector", lambda v, g=g, c0=c0, h=h, HW=HW, p=par: v.tensor_add(
                        out=x1p4[p][:, g, c0 + h * HW:c0 + (h + 1) * HW],
                        in0=x1p4[p][:, g, c0 + h * HW:c0 + (h + 1) * HW],
                        in1=p4tmp[:, h * HW:(h + 1) * HW]), sV)
                wait("tensor", sV, e_cmb)
        p4V_cur = sV.n
        if tt == 0:
            p4V = [0] * NTT
        p4V[tt] = p4V_cur
        wait("sync", sV, e_cmb)
        p4st[tt] = (soutp[par], opi("sync", lambda e, p=par, tt=tt: e.dma_start(
            out=out[tt * Tc:(tt + 1) * Tc, :].rearrange("(g p) d -> p g d", p=128),
            in_=x1p4[p][:]), soutp[par], 16))

    for sp in soutp:
        if sp.n:
            wait("gpsimd", sp, sp.n)

    with nc.Block() as block:
        def runner(lst):
            def run(e):
                for fn in lst:
                    fn(e)
            return run
        block.sync(runner(ctx.q["sync"]))
        block.tensor(runner(ctx.q["tensor"]))
        block.vector(runner(ctx.q["vector"]))
        block.scalar(runner(ctx.q["scalar"]))
        block.gpsimd(runner(ctx.q["gpsimd"]))

    p3stk.close()
    sstk.close()
    top.close()
    return nc


def make_in_maps(cfg, inputs):
    D, H, F, E, S, Tc = (cfg[k] for k in ("D", "H", "F", "E", "S", "Tc"))
    NCH = S // Tc
    FC = F // 128
    x = np.ascontiguousarray(inputs["L0_attn_x"], dtype=np.float32)
    B = x.shape[0]
    SH = S // 2
    c = np.ascontiguousarray(inputs["L0_c"], dtype=np.float32)
    shared = {
        "rms_row": np.ascontiguousarray(inputs["L0_attn_rms_scale"].reshape(1, D)),
        "split_row": np.ascontiguousarray(inputs["L0_attn_split_scale"].reshape(1, D)),
        "bg_row": np.ascontiguousarray(inputs["L0_ffn_bg"].reshape(1, E)),
        "wmod": np.ascontiguousarray(inputs["L0_wmod"]),
        "wk": np.ascontiguousarray(inputs["L0_attn_W_K"]),
        "wugg": np.ascontiguousarray(inputs["L0_attn_W_ugg"]),
        "wout": np.ascontiguousarray(inputs["L0_attn_W_out"]),
        "wg": np.ascontiguousarray(inputs["L0_ffn_Wg"]),
        "w1": np.ascontiguousarray(inputs["L0_ffn_W1"]),
        "b1T": np.ascontiguousarray(
            inputs["L0_ffn_b1"].reshape(E, FC, 128).transpose(0, 2, 1)),
        "w2": np.ascontiguousarray(inputs["L0_ffn_W2"]),
        "b2": np.ascontiguousarray(inputs["L0_ffn_b2"]),
    }
    ones = np.ones((128, NCH), np.float32)
    mask_j0 = ones.copy()
    mask_j0[:, :NCH // 2] = 0.0
    in_maps = []
    for core in range(2 * B):
        b, j = core // 2, core % 2
        if j == 0:
            xs = np.concatenate([x[b, 0:SH], x[b, 0:SH]], axis=0)
            mk = mask_j0
        else:
            xs = np.concatenate([x[b, 0:SH], x[b, SH:S]], axis=0)
            mk = ones
        m = dict(shared)
        m["x_seq"] = np.ascontiguousarray(xs)
        m["mask16"] = np.ascontiguousarray(mk)
        m["cvecT"] = np.ascontiguousarray(c[b].reshape(D // 128, 128).T)
        in_maps.append(m)
    return in_maps


def kernel(**inputs):
    cfg = FULL_CFG
    nc = build_program(cfg)
    in_maps = make_in_maps(cfg, inputs)
    res = run_bass_kernel_spmd(nc, in_maps, core_ids=list(range(len(in_maps))))
    B = inputs["L0_attn_x"].shape[0]
    S, D = cfg["S"], cfg["D"]
    out = np.empty((B, S, D), dtype=np.float32)
    for core in range(2 * B):
        b, j = core // 2, core % 2
        out[b, j * (S // 2):(j + 1) * (S // 2)] = res.results[core]["out"]
    return out


# version 15
# speedup vs baseline: 1.2608x; 1.2014x over previous
import numpy as np
from contextlib import ExitStack

import concourse.bass as bass
import concourse.mybir as mybir
from concourse.bass_utils import run_bass_kernel_spmd
IOA = bass.IndirectOffsetOnAxis

F32 = mybir.dt.float32
BF16 = mybir.dt.bfloat16
I32 = mybir.dt.int32
AF = mybir.ActivationFunctionType
ALU = mybir.AluOpType
AX = mybir.AxisListType

ENGINES = ("sync", "tensor", "vector", "scalar", "gpsimd")

FULL_CFG = dict(D=1024, H=2048, F=4096, E=8, S=4096, Tc=256, NQ=4)


class Sem:
    def __init__(self, stk, nc, name):
        self.h = stk.enter_context(nc.semaphore(name))
        self.n = 0
        self.name = name


class Ctx:

    def __init__(self, nc, cfg):
        self.nc = nc
        self.cfg = cfg
        self.q = {e: [] for e in ENGINES}
        self.last_wait = {e: {} for e in ENGINES}

    def op(self, eng, fn):
        self.q[eng].append(fn)

    def opi(self, eng, fn, sem, k=1):
        sem.n += k
        def run(e, fn=fn, s=sem.h, k=k):
            return fn(e).then_inc(s, k)
        self.q[eng].append(run)
        return sem.n

    def wait(self, eng, sem, val):
        if val <= 0:
            return
        lw = self.last_wait[eng]
        if lw.get(sem.name, -1) >= val:
            return
        lw[sem.name] = val
        self.q[eng].append(lambda e, s=sem.h, v=val: e.wait_ge(s, v))


def build_program(cfg):
    nc = bass.Bass("TRN2", target_bir_lowering=False, detect_race_conditions=False)
    D, H, F, E, S, Tc, NQ = (cfg[k] for k in ("D", "H", "F", "E", "S", "Tc", "NQ"))
    DC, CC, FC = D // 128, H // 128, F // 128
    G = Tc // 128
    NCH = S // Tc
    OWN0 = NCH // 2
    NOWN = NCH - OWN0
    SH = S // 2
    FQ = F // NQ
    FQC = FQ // 128
    NTT = SH // Tc
    assert NTT == NOWN
    CAP = cfg.get("CAP", SH // 2)
    CSUB = CAP // 128
    CTG = CAP // Tc
    NGRP = SH // 128
    assert G * D == SH and DC * Tc == SH and CAP % Tc == 0

    dt_in = lambda n, shp: nc.dram_tensor(n, list(shp), F32, kind="ExternalInput").ap()
    x_seq = dt_in("x_seq", (S, D))
    mask16 = dt_in("mask16", (128, NCH))
    cvecT = dt_in("cvecT", (128, DC))
    rms_row = dt_in("rms_row", (1, D))
    split_row = dt_in("split_row", (1, D))
    bg_row = dt_in("bg_row", (1, E))
    wmod = dt_in("wmod", (D, 6 * D))
    wk = dt_in("wk", (D, H))
    wugg = dt_in("wugg", (D, 3 * H))
    wout = dt_in("wout", (H, D))
    wg = dt_in("wg", (D, E))
    w1 = dt_in("w1", (E, D, F))
    b1T = dt_in("b1T", (E, 128, FC))
    w2 = dt_in("w2", (E, F, D))
    b2 = dt_in("b2", (E, D))

    out = nc.dram_tensor("out", [SH, D], F32, kind="ExternalOutput").ap()

    hin_b = nc.dram_tensor("hin_b", [SH, D], F32, kind="Internal").ap()
    x1_b = nc.dram_tensor("x1_b", [SH, D], F32, kind="Internal").ap()
    xnT_b = nc.dram_tensor("xnT_b", [NOWN, 128, DC, Tc], BF16, kind="Internal").ap()
    S_b = nc.dram_tensor("S_b", [NOWN, 128, CC, Tc], BF16, kind="Internal").ap()
    mod_d = nc.dram_tensor("mod_d", [1, 6 * D], F32, kind="Internal").ap()
    w_d = nc.dram_tensor("w_d", [E, SH], F32, kind="Internal").ap()
    h2tok_d = nc.dram_tensor("h2tok_d", [SH + 128, D], BF16, kind="Internal").ap()
    acc_d = nc.dram_tensor("acc_d", [SH + 128, D], F32, kind="Internal").ap()
    idx_d = nc.dram_tensor("idx_d", [E, 2 * SH], I32, kind="Internal").ap()
    wpad_d = nc.dram_tensor("wpad_d", [E, SH + 128], F32, kind="Internal").ap()

    ctx = Ctx(nc, cfg)
    top = ExitStack()
    sb = lambda n, shp, dt=F32: top.enter_context(nc.sbuf_tensor(n, list(shp), dt))
    ps = lambda n, shp, dt=F32: top.enter_context(nc.psum_tensor(n, list(shp), dt))

    ident_f = sb("ident_f", (128, 128))
    ident_b = sb("ident_b", (128, 128), BF16)
    wa = sb("wa", (128, DC * H), BF16)
    wb = sb("wb", (128, 2 * DC * H), BF16)
    bc3 = sb("bc3", (128, 3, D))
    bg_bc = sb("bg_bc", (128, E))
    mask_sb = sb("mask_sb", (128, NCH))
    cvec_sb = sb("cvec_sb", (128, DC))
    siluc_bf = sb("siluc_bf", (128, DC), BF16)
    eps5 = sb("eps5", (128, 1))
    stat = sb("stat", (128, 16))
    tailK = sb("tailK", (128, CC))
    carry = sb("carry", (128, CC))

    wk_v = wa[:].rearrange("p (c h) -> p c h", c=DC)
    h2T = wa[:].rearrange("p (c t) -> p c t", c=DC)
    wb_flat = wb[:]
    wmod_v = wb_flat.rearrange("p (c f) -> p c f", c=DC)
    ugg_v = wb_flat.rearrange("p (c f) -> p c f", c=DC)
    gout_v = wb_flat[:, 0:DC * H].rearrange("p (c h) -> p c h", c=DC)
    wout_v = wb_flat[:, DC * H:2 * DC * H].rearrange("p (c d) -> p c d", c=CC)
    QW = DC * FQ
    assert QW == FQC * D
    w1q_v = [wb_flat[:, i * QW:(i + 1) * QW].rearrange("p (c f) -> p c f", c=DC) for i in range(2)]
    w2q_v = [wb_flat[:, (2 + i) * QW:(3 + i) * QW].rearrange("p (c d) -> p c d", c=FQC) for i in range(2)]

    ps_y = ps("ps_y", (128, 8, Tc))
    ps_h = ps("ps_h", (128, 2, 512))
    ps_tf = ps("ps_tf", (128, 512))
    ps_t = ps("ps_t", (128, 1024), BF16)
    ps_g = ps_tf[:, 256:256 + E]

    sstk = ExitStack()
    sems = {}
    for name in ("sT", "sV", "sS", "sG", "sod"):
        sems[name] = Sem(sstk, nc, name)
    sT, sV, sS, sG, sod = (sems[k] for k in ("sT", "sV", "sS", "sG", "sod"))
    sxp = [Sem(sstk, nc, "sx%d" % i) for i in range(2)]
    sbop = [Sem(sstk, nc, "sbo%d" % i) for i in range(2)]
    soutp = [Sem(sstk, nc, "sout%d" % i) for i in range(2)]

    op, opi, wait = ctx.op, ctx.opi, ctx.wait

    class GQ:
        def __init__(self, n=4):
            self.sems = [Sem(sstk, nc, "sgd%d" % i) for i in range(n)]
            self.i = 0

        def dma(self, fn):
            s = self.sems[self.i % len(self.sems)]
            self.i += 1
            if s.n > 0:
                wait("gpsimd", s, s.n)
            val = opi("gpsimd", fn, s, 16)
            return (s, val)

    gq = GQ()

    def wait_tok(eng, tok):
        wait(eng, tok[0], tok[1])

    op("gpsimd", lambda g: g.memset(ident_f[:], 0.0))
    e_ms = opi("gpsimd", lambda g: g.memset(ident_b[:], 0.0), sG)
    wait("gpsimd", sG, e_ms)
    op("gpsimd", lambda g: g.affine_select(out=ident_f[:], in_=ident_f[:],
        compare_op=ALU.not_equal, fill=1.0, base=0, pattern=[[-1, 128]], channel_multiplier=1))
    op("gpsimd", lambda g: g.affine_select(out=ident_b[:], in_=ident_b[:],
        compare_op=ALU.not_equal, fill=1.0, base=0, pattern=[[-1, 128]], channel_multiplier=1))
    op("gpsimd", lambda g: g.memset(eps5[:], 1e-5))
    op("gpsimd", lambda g: g.memset(tailK[:], 0.0))
    op("gpsimd", lambda g: g.memset(carry[:], 0.0))
    e_const = opi("gpsimd", lambda g: g.engine_nop(), sG)

    opi("sync", lambda e: e.dma_start(out=mask_sb[:], in_=mask16), sxp[0], 16)
    ld0 = (sxp[0], opi("sync", lambda e: e.dma_start(out=cvec_sb[:], in_=cvecT), sxp[0], 16))
    e_wk = gq.dma(lambda g: g.dma_start(
        out=wk_v, in_=wk.rearrange("(c p) h -> p c h", p=128)))
    HF = 3 * D
    e_wm0 = gq.dma(lambda g: g.dma_start(
        out=wmod_v[:, :, 0:HF], in_=wmod[:, 0:HF].rearrange("(c p) m -> p c m", p=128)))

    wait_tok("scalar", ld0)
    e_silu = opi("scalar", lambda s: s.activation(out=cvec_sb[:], in_=cvec_sb[:], func=AF.Silu), sS)
    wait("vector", sS, e_silu)
    e_silub = opi("vector", lambda v: v.tensor_copy(out=siluc_bf[:], in_=cvec_sb[:]), sV)

    NGH = (HF + 511) // 512
    GW = HF // NGH
    assert GW * NGH == HF and GW <= 512
    wait_tok("tensor", e_wm0)
    wait("tensor", sV, e_silub)
    mm_done = []
    mod_grp_tok = []
    for half in range(2):
        if half == 1:
            wait("gpsimd", sT, mm_done[-1])
            e_wm1 = gq.dma(lambda g: g.dma_start(
                out=wmod_v[:, :, 0:HF],
                in_=wmod[:, HF:6 * D].rearrange("(c p) m -> p c m", p=128)))
            wait_tok("tensor", e_wm1)
        for j in range(NGH):
            if mm_done:
                wait("tensor", sV, e_row)
            def mmrow(t, j=j):
                mm = None
                for dc in range(DC):
                    mm = t.matmul(ps_tf[0:1, 0:GW], siluc_bf[:, dc:dc + 1],
                                  wmod_v[:, dc, j * GW:(j + 1) * GW],
                                  start=(dc == 0), stop=(dc == DC - 1))
                return mm
            ev = opi("tensor", mmrow, sT)
            mm_done.append(ev)
            col0 = half * HF + j * GW
            wait("vector", sT, ev)
            if sod.n:
                wait("vector", sod, sod.n)
            e_row = opi("vector", lambda v: v.tensor_copy(
                out=mod_sb[0:1, 0:GW], in_=ps_tf[0:1, 0:GW]), sV)
            wait("sync", sV, e_row)
            e_dd = opi("sync", lambda e, c0=col0: e.dma_start(
                out=mod_d[0:1, c0:c0 + GW], in_=mod_sb[0:1, 0:GW]), sod, 16)
            mod_grp_tok.append(e_dd)
    e_modd = (sod, e_dd)

    def bcast(dst_ap, src_t, src_off, n):
        src = bass.AP(src_t.tensor, src_t.offset + src_off, [[0, 128], [1, n]])
        return gq.dma(lambda g, d=dst_ap, s=src: g.dma_start(out=d, in_=s))

    grp_needed = (2 * D + GW - 1) // GW
    wait("gpsimd", sod, mod_grp_tok[grp_needed - 1])
    e_ca = bcast(bc3[:, 0, :], mod_d, 1 * D, D)
    e_sa = bcast(bc3[:, 1, :], mod_d, 0 * D, D)
    e_rm = bcast(bc3[:, 2, :], rms_row, 0, D)
    e_bg = bcast(bg_bc[:], bg_row, 0, E)

    stkA = ExitStack()
    sbA = lambda n, shp, dt=F32: stkA.enter_context(nc.sbuf_tensor(n, list(shp), dt))
    x_tok = [sbA("x_tok%d" % i, (128, G, D)) for i in range(2)]
    scrap = [sbA("scrap%d" % i, (128, G, D)) for i in range(2)]
    hin_t = sbA("hin_t", (128, G, D))
    hin = [hin_t, hin_t]
    xg_tok = [sbA("xg_tok%d" % i, (128, G, D), BF16) for i in range(2)]
    xgT = [sbA("xgT%d" % i, (128, DC, Tc), BF16) for i in range(2)]
    Gibuf = [sbA("Gibuf%d" % i, (128, Tc)) for i in range(2)]
    S_chunk_t = sbA("S_chunk", (128, CC, Tc), BF16)
    S_chunk = [S_chunk_t, S_chunk_t]
    wg_bf = sbA("wg_bf", (128, DC, E), BF16)
    wfin = sbA("wfin", (128, E))
    srt8 = sbA("srt8", (128, E))
    gz = sbA("gz", (128, E))
    wT_sb = sbA("wT_sb", (E, SH))
    stkP1 = ExitStack()
    sbP1 = lambda n, shp, dt=F32: stkP1.enter_context(nc.sbuf_tensor(n, list(shp), dt))
    mod_sb = sbP1("mod_sb", (1, 512))
    Kbuf = [sbP1("Kbuf%d" % i, (128, Tc + 1)) for i in range(2)]
    T2buf = [sbP1("T2buf%d" % i, (128, Tc)) for i in range(2)]
    Ubuf = [sbP1("Ubuf%d" % i, (128, Tc)) for i in range(2)]
    Sbuf = [sbP1("Sbuf%d" % i, (128, Tc)) for i in range(2)]

    e_sp = bcast(x_tok[0][:, 0, :], split_row, 0, D)
    for tok in (e_ca, e_sa, e_rm, e_sp):
        wait_tok("vector", tok)
    e_rsss = opi("vector", lambda v: v.tensor_mul(out=bc3[:, 2, :], in0=bc3[:, 2, :], in1=x_tok[0][:, 0, :]), sV)
    e_ca1 = opi("vector", lambda v: v.tensor_scalar(out=bc3[:, 0, :], in0=bc3[:, 0, :],
        scalar1=1.0, scalar2=None, op0=ALU.add), sV)
    P1_STATICS_V = e_ca1

    wait("gpsimd", sT, mm_done[-1])
    UGW = (2 * H) // 4
    e_ug_k = []
    for k in range(4):
        e_ug_k.append(gq.dma(lambda g, k=k: g.dma_start(
            out=ugg_v[:, :, k * UGW:(k + 1) * UGW],
            in_=wugg[:, k * UGW:(k + 1) * UGW].rearrange("(c p) m -> p c m", p=128))))
    e_ug = e_ug_k[-1]

    chV = [0] * NCH
    chS = [0] * NCH
    chT = [0] * NCH
    chBO = [None] * (NCH + 1)
    hinBO = [None] * (NCH + 1)
    statV = [0] * NCH
    hinS = [0] * NCH
    lnV = [0] * NCH
    lnS = [0] * NCH
    psV = {}
    psS = {}
    ccV = {}
    ccG = {}
    xload = [0] * NCH
    LN = {}

    def ln_part(t):
        par = t % 2
        if t < 2:
            wait("sync", sV, P1_STATICS_V)
        if t >= 2:
            wait("sync", sV, lnV[t - 2])
            wait("sync", sS, lnS[t - 2])
        xs = x_seq[t * Tc:(t + 1) * Tc, :].rearrange("(g p) d -> p g d", p=128)
        xload[t] = (sxp[par], opi("sync", lambda e, d=x_tok[par], s=xs: e.dma_start(out=d[:], in_=s), sxp[par], 16))
        yield
        wait_tok("scalar", xload[t])
        if t >= 1:
            wait("scalar", sV, statV[t - 1])
        if t >= 2:
            wait("scalar", sV, lnV[t - 2])
        for g in range(G):
            e_sq = opi("scalar", lambda s, g=g, p=par: s.activation(
                out=scrap[p][:, g, :], in_=x_tok[p][:, g, :], func=AF.Square,
                accum_out=stat[:, 2 + g:3 + g]), sS)
        wait_tok("vector", xload[t])
        wait("vector", sV, P1_STATICS_V)
        for g in range(G):
            e_s1 = opi("vector", lambda v, g=g, p=par: v.reduce_sum(
                stat[:, 0 + g:1 + g], x_tok[p][:, g, :], axis=AX.X), sV)
        yield
        wait("vector", sV, e_s1)
        wait("vector", sS, e_sq)
        for g in range(G):
            op("vector", lambda v, g=g: v.tensor_scalar(out=stat[:, 4 + g:5 + g],
                in0=stat[:, 0 + g:1 + g], scalar1=1.0 / D, scalar2=None, op0=ALU.mult))
        for g in range(G):
            op("vector", lambda v, g=g: v.tensor_scalar(out=stat[:, 6 + g:7 + g],
                in0=stat[:, 2 + g:3 + g], scalar1=1.0 / D, scalar2=None, op0=ALU.mult))
        for g in range(G):
            op("vector", lambda v, g=g: v.tensor_mul(out=stat[:, 8 + g:9 + g],
                in0=stat[:, 4 + g:5 + g], in1=stat[:, 4 + g:5 + g]))
        for g in range(G):
            e_var = opi("vector", lambda v, g=g: v.tensor_sub(out=stat[:, 6 + g:7 + g],
                in0=stat[:, 6 + g:7 + g], in1=stat[:, 8 + g:9 + g]), sV)
        statV[t] = e_var
        wait("scalar", sV, e_var)
        for g in range(G):
            e_std = opi("scalar", lambda s, g=g: s.activation(out=stat[:, 8 + g:9 + g],
                in_=stat[:, 6 + g:7 + g], func=AF.Sqrt, bias=eps5[:]), sS)
        yield
        wait("vector", sS, e_std)
        for g in range(G):
            e_rst = opi("vector", lambda v, g=g: v.reciprocal(out=stat[:, 10 + g:11 + g],
                in_=stat[:, 8 + g:9 + g]), sV)
        wait("vector", sV, e_rst)
        for g in range(G):
            op("vector", lambda v, g=g, p=par: v.tensor_scalar(out=scrap[p][:, g, :],
                in0=x_tok[p][:, g, :], scalar1=stat[:, 4 + g:5 + g], scalar2=stat[:, 10 + g:11 + g],
                op0=ALU.subtract, op1=ALU.mult))
        yield
        if t >= 1:
            if hinBO[t - 1]:
                wait_tok("vector", hinBO[t - 1])
            wait("vector", sS, hinS[t - 1])
        for g in range(G):
            op("vector", lambda v, g=g, p=par: v.tensor_mul(out=hin[p][:, g, :],
                in0=scrap[p][:, g, :], in1=bc3[:, 0, :]))
        for g in range(G):
            e_hin = opi("vector", lambda v, g=g, p=par: v.tensor_add(out=hin[p][:, g, :],
                in0=hin[p][:, g, :], in1=bc3[:, 1, :]), sV)
        yield
        if t >= OWN0:
            oc = t - OWN0
            wait("sync", sV, e_hin)
            hinBO[t] = (sbop[par], opi("sync", lambda e, p=par, oc=oc: e.dma_start(
                out=hin_b[oc * Tc:(oc + 1) * Tc, :].rearrange("(g p) d -> p g d", p=128),
                in_=hin[p][:]), sbop[par], 16))
        wait("scalar", sV, e_hin)
        for g in range(G):
            e_rsq = opi("scalar", lambda s, g=g, p=par: s.activation(
                out=scrap[p][:, g, :], in_=hin[p][:, g, :], func=AF.Square,
                accum_out=stat[:, 12 + g:13 + g]), sS)
        hinS[t] = e_rsq
        wait("scalar", sS, e_rsq)
        for g in range(G):
            e_nrm = opi("scalar", lambda s, g=g: s.activation(out=stat[:, 14 + g:15 + g],
                in_=stat[:, 12 + g:13 + g], func=AF.Sqrt, scale=1.0 / D), sS)
        yield
        wait("vector", sS, e_nrm)
        for g in range(G):
            op("vector", lambda v, g=g: v.tensor_scalar(out=stat[:, 12 + g:13 + g],
                in0=stat[:, 14 + g:15 + g], scalar1=1e-8, scalar2=None, op0=ALU.add))
        for g in range(G):
            e_rr = opi("vector", lambda v, g=g: v.reciprocal(out=stat[:, 14 + g:15 + g],
                in_=stat[:, 12 + g:13 + g]), sV)
        wait("vector", sV, e_rr)
        for g in range(G):
            op("vector", lambda v, g=g, p=par: v.tensor_scalar(out=scrap[p][:, g, :],
                in0=hin[p][:, g, :], scalar1=stat[:, 14 + g:15 + g], scalar2=None, op0=ALU.mult))
        for g in range(G):
            e_xn = opi("vector", lambda v, g=g, p=par: v.tensor_mul(out=xg_tok[p][:, g, :],
                in0=scrap[p][:, g, :], in1=bc3[:, 2, :]), sV)
        LN[t] = dict(e_hin=e_hin, e_xn=e_xn)
        lnV[t] = sV.n
        lnS[t] = sS.n

    def ln_transpose(t):
        par = t % 2
        e_xn = LN[t]["e_xn"]
        wait("tensor", sV, e_xn)
        if t >= 2:
            if chBO[t - 2]:
                wait_tok("tensor", chBO[t - 2])
                wait_tok("vector", chBO[t - 2])
            wait("vector", sT, chT[t - 2])
        n_tp = G * DC
        e_xnT = None
        for b0 in range(0, n_tp, 4):
            for k in range(4):
                idx = b0 + k
                g, dc = idx // DC, idx % DC
                e_tp = opi("tensor", lambda tt, g=g, dc=dc, k=k, p=par: tt.transpose(
                    ps_t[:, k * 128:(k + 1) * 128], xg_tok[p][:, g, dc * 128:(dc + 1) * 128],
                    ident_b[:]), sT)
            wait("vector", sT, e_tp)
            g0, dc0 = b0 // DC, b0 % DC
            if dc0 + 4 <= DC:
                e_xnT = opi("vector", lambda v, g=g0, dc=dc0, p=par: v.tensor_copy(
                    out=xgT[p][:, dc:dc + 4, g * 128:(g + 1) * 128],
                    in_=ps_t[:, 0:512].rearrange("p (k t) -> p k t", k=4)), sV)
            else:
                for k in range(4):
                    idx = b0 + k
                    g, dc = idx // DC, idx % DC
                    e_xnT = opi("vector", lambda v, g=g, dc=dc, k=k, p=par: v.tensor_copy(
                        out=xgT[p][:, dc, g * 128:(g + 1) * 128],
                        in_=ps_t[:, k * 128:(k + 1) * 128]), sV)
            if b0 + 4 < n_tp:
                wait("tensor", sV, e_xnT)
        LN[t]["e_xnT"] = e_xnT
        if t >= OWN0:
            oc = t - OWN0
            wait("sync", sV, e_xnT)
            chBO[t] = (sbop[par], opi("sync", lambda e, p=par, oc=oc: e.dma_start(
                out=xnT_b[oc], in_=xgT[p][:]), sbop[par], 16))

    def cc_part(t):
        par = t % 2
        own = t >= OWN0
        e_xnT = LN[t]["e_xnT"]
        e_hin = LN[t]["e_hin"]
        wait_tok("tensor", e_wk)
        for cc in range(CC):
            if t == 0:
                wait_tok("tensor", e_ug_k[min(3, (cc * 128) // UGW)])
                wait_tok("tensor", e_ug_k[min(3, (H + cc * 128) // UGW)])
            cp = cc % 2
            slot_k, slot_u, slot_gi = 4 * cp, 4 * cp + 1, 4 * cp + 2
            if (t, cc) > (0, 1):
                pse = psS.get(cp)
                if pse:
                    wait("tensor", sS, pse)
            wait("tensor", sV, e_xnT)
            def proj(tt, p=par, cc=cc, sk=slot_k, su=slot_u, sgi=slot_gi):
                mm = None
                for dc in range(DC):
                    mm = tt.matmul(ps_y[:, sk, :], wk_v[:, dc, cc * 128:(cc + 1) * 128],
                                   xgT[p][:, dc, :], start=(dc == 0), stop=(dc == DC - 1))
                for dc in range(DC):
                    mm = tt.matmul(ps_y[:, su, :], ugg_v[:, dc, cc * 128:(cc + 1) * 128],
                                   xgT[p][:, dc, :], start=(dc == 0), stop=(dc == DC - 1))
                for dc in range(DC):
                    mm = tt.matmul(ps_y[:, sgi, :], ugg_v[:, dc, H + cc * 128:H + (cc + 1) * 128],
                                   xgT[p][:, dc, :], start=(dc == 0), stop=(dc == DC - 1))
                return mm
            e_mm = opi("tensor", proj, sT)
            wait("scalar", sT, e_mm)
            if cp in ccV:
                wait("scalar", sV, ccV[cp])
            if cp in ccG:
                wait("scalar", sG, ccG[cp])
            e_sk = opi("scalar", lambda s, cp=cp, sk=slot_k: s.activation(
                out=Kbuf[cp][:, 1:Tc + 1], in_=ps_y[:, sk, :], func=AF.Sigmoid), sS)
            e_um = opi("scalar", lambda s, cp=cp, su=slot_u, t=t: s.activation(
                out=T2buf[cp][:], in_=ps_y[:, su, :], func=AF.Copy,
                scale=mask_sb[:, t:t + 1]), sS)
            e_sg = opi("scalar", lambda s, cp=cp, sgi=slot_gi: s.activation(
                out=Gibuf[cp][:], in_=ps_y[:, sgi, :], func=AF.Sigmoid), sS)
            e_km = opi("scalar", lambda s, cp=cp, sk=slot_k: s.activation(
                out=Ubuf[cp][:], in_=ps_y[:, sk, :], func=AF.Sigmoid,
                scale=-1.0), sS)
            psS[cp] = e_km
            wait("gpsimd", sS, e_km)
            op("gpsimd", lambda g, cp=cp: g.tensor_tensor(out=T2buf[cp][:],
                in0=T2buf[cp][:], in1=Gibuf[cp][:], op=ALU.mult))
            e_ut = opi("gpsimd", lambda g, cp=cp: g.tensor_tensor(out=T2buf[cp][:],
                in0=T2buf[cp][:], in1=Ubuf[cp][:], op=ALU.mult), sG)
            ccG[cp] = e_ut
            wait("vector", sS, e_sk)
            op("vector", lambda v, cp=cp, cc=cc: v.tensor_copy(out=Kbuf[cp][:, 0:1],
                in_=tailK[:, cc:cc + 1]))
            e_tl = opi("vector", lambda v, cp=cp, cc=cc: v.tensor_copy(
                out=tailK[:, cc:cc + 1], in_=Kbuf[cp][:, Tc:Tc + 1]), sV)
            wait("vector", sG, e_ut)
            e_scan = opi("vector", lambda v, cp=cp, cc=cc: v.tensor_tensor_scan(
                out=Sbuf[cp][:], data0=Kbuf[cp][:, 0:Tc], data1=T2buf[cp][:],
                initial=carry[:, cc:cc + 1], op0=ALU.mult, op1=ALU.add), sV)
            wait("vector", sV, e_scan)
            e_cr = opi("vector", lambda v, cp=cp, cc=cc: v.tensor_copy(
                out=carry[:, cc:cc + 1], in_=Sbuf[cp][:, Tc - 1:Tc]), sV)
            if own:
                if cc == 0 and t - 1 >= OWN0 and chBO[t - 1]:
                    wait_tok("vector", chBO[t - 1])
                e_sS = opi("vector", lambda v, cp=cp, cc=cc, p=par: v.tensor_copy(
                    out=S_chunk[p][:, cc, :], in_=Sbuf[cp][:]), sV)
            ccV[cp] = sV.n
            yield
        chV[t] = sV.n
        chS[t] = sS.n
        chT[t] = sT.n
        if own:
            oc = t - OWN0
            wait("sync", sV, sV.n)
            bo = opi("sync", lambda e, p=par, oc=oc: e.dma_start(
                out=S_b[oc], in_=S_chunk[p][:]), sbop[par], 16)
            chBO[t] = (sbop[par], bo)

    for _ in ln_part(0):
        pass
    ln_transpose(0)
    for _ in ln_part(1):
        pass
    for t in range(NCH):
        gc = cc_part(t)
        gl = ln_part(t + 2) if t + 2 < NCH else None
        for _ in gc:
            if gl is not None:
                try:
                    next(gl)
                except StopIteration:
                    gl = None
        while gl is not None:
            try:
                next(gl)
            except StopIteration:
                gl = None
        if t + 1 < NCH:
            ln_transpose(t + 1)

    P1_T_END = sT.n
    P1_V_END = sV.n
    P1_S_END = sS.n

    wait("gpsimd", sT, P1_T_END)
    e_go_k = []
    for k in range(2):
        e_go_k.append(gq.dma(lambda g, k=k: g.dma_start(
            out=gout_v[:, :, k * (H // 2):(k + 1) * (H // 2)],
            in_=wugg[:, 2 * H + k * (H // 2):2 * H + (k + 1) * (H // 2)]
            .rearrange("(c p) m -> p c m", p=128))))
    e_go = e_go_k[-1]
    e_wo = gq.dma(lambda g: g.dma_start(
        out=wout_v, in_=wout.rearrange("(c p) d -> p c d", p=128)))
    e_wg = gq.dma(lambda g: g.dma_start(
        out=wg_bf[:], in_=wg.rearrange("(c p) e -> p c e", p=128)))
    e_cb = bcast(bc3[:, 0, :], mod_d, 4 * D, D)
    e_sb2 = bcast(bc3[:, 1, :], mod_d, 3 * D, D)
    e_ga = bcast(bc3[:, 2, :], mod_d, 2 * D, D)
    wait("vector", sV, P1_V_END)
    wait("vector", sS, P1_S_END)
    wait("vector", sT, P1_T_END)
    for tok in (e_cb, e_sb2, e_ga):
        wait_tok("vector", tok)
    e_cb1 = opi("vector", lambda v: v.tensor_scalar(out=bc3[:, 0, :], in0=bc3[:, 0, :],
        scalar1=1.0, scalar2=None, op0=ALU.add), sV)
    op("vector", lambda v: v.tensor_scalar(out=scrap[0][:, 0, :], in0=bc3[:, 2, :],
        scalar1=0.0, scalar2=None, op0=ALU.is_ge))
    op("vector", lambda v: v.tensor_scalar(out=scrap[0][:, 0, :], in0=scrap[0][:, 0, :],
        scalar1=2.0, scalar2=None, op0=ALU.mult))
    op("vector", lambda v: v.tensor_scalar(out=scrap[0][:, 0, :], in0=scrap[0][:, 0, :],
        scalar1=-1.0, scalar2=None, op0=ALU.add))
    op("vector", lambda v: v.tensor_mul(out=bc3[:, 2, :], in0=bc3[:, 2, :],
        in1=scrap[0][:, 0, :]))
    op("vector", lambda v: v.tensor_scalar_max(bc3[:, 2, :], bc3[:, 2, :], 0.25))
    e_gap = opi("vector", lambda v: v.tensor_mul(out=bc3[:, 2, :], in0=bc3[:, 2, :],
        in1=scrap[0][:, 0, :]), sV)
    P2_STATICS_V = e_gap

    stkP1.close()
    y_sb = sbA("y_sb", (128, DC, Tc))
    x1_t = sbA("x1_t", (128, G, D))
    x1 = [x1_t, x1_t]
    stkS = ExitStack()
    s_all = stkS.enter_context(nc.sbuf_tensor("s_all", [128, CC, Tc], BF16))

    p2Sdone = {}
    p2hin = {}
    p2sT = {}
    aT = {}
    bV = {}
    gateZ = {}
    yeV = {}
    x1BO = {}
    h2BO = {}
    GB = {}

    wait_tok("tensor", e_wg)

    def genA(oc):
        par = oc % 2
        if oc >= 2:
            wait("sync", sV, bV[oc - 2])
            wait("sync", sT, aT[oc - 2])
        else:
            wait("sync", sV, P1_V_END)
            wait("sync", sT, P1_T_END)
            wait("sync", sS, P1_S_END)
            for tok in (chBO[NCH - 1], chBO[NCH - 2]):
                if tok:
                    wait_tok("sync", tok)
        opi("sync", lambda e, p=par, oc=oc: e.dma_start(out=xgT[p][:], in_=xnT_b[oc]), sxp[par], 16)
        if oc >= 1:
            wait("sync", sV, p2Sdone[oc - 1])
            wait("sync", sV, p2hin[oc - 1])
            wait("sync", sS, GB.get(oc - 1, {}).get("bS", 0))
        opi("sync", lambda e, p=par, oc=oc: e.dma_start(out=S_chunk[p][:], in_=S_b[oc]), sxp[par], 16)
        opi("sync", lambda e, p=par, oc=oc: e.dma_start(
            out=hin[p][:], in_=hin_b[oc * Tc:(oc + 1) * Tc, :].rearrange("(g p) d -> p g d", p=128)), sxp[par], 16)
        ln = opi("sync", lambda e, p=par, oc=oc: e.dma_start(
            out=x_tok[p][:], in_=x_seq[(OWN0 + oc) * Tc:(OWN0 + oc + 1) * Tc, :]
            .rearrange("(g p) d -> p g d", p=128)), sxp[par], 16)
        yield
        wait("tensor", sxp[par], ln)
        if oc >= 1:
            wait("vector", sT, p2sT[oc - 1])
        e_go_evac, e_s_done = {}, {}
        for cc in range(CC):
            if oc == 0:
                wait_tok("tensor", e_go_k[min(1, (cc * 128) // (H // 2))])
            cp = cc % 2
            if cc >= 2:
                wait("tensor", sS, e_go_evac[cc - 2])
            def gproj(tt, p=par, cc=cc, cp=cp):
                mm = None
                for dc in range(DC):
                    mm = tt.matmul(ps_h[:, cp, 0:Tc], gout_v[:, dc, cc * 128:(cc + 1) * 128],
                                   xgT[p][:, dc, :], start=(dc == 0), stop=(dc == DC - 1))
                return mm
            e_gmm = opi("tensor", gproj, sT)
            wait("scalar", sT, e_gmm)
            if cc >= 2:
                wait("scalar", sV, e_s_done[cc - 2])
            e_go_evac[cc] = opi("scalar", lambda s, cp=cp: s.activation(
                out=Gibuf[cp][:], in_=ps_h[:, cp, 0:Tc], func=AF.Sigmoid), sS)
            wait("vector", sS, e_go_evac[cc])
            e_s_done[cc] = opi("vector", lambda v, cp=cp, cc=cc, p=par: v.tensor_tensor(
                out=s_all[:, cc, :], in0=S_chunk[p][:, cc, :], in1=Gibuf[cp][:],
                op=ALU.mult), sV)
            yield
        p2Sdone[oc] = e_s_done[CC - 1]
        wait("tensor", sV, e_s_done[CC - 1])
        if oc >= 1:
            wait("tensor", sV, yeV[oc - 1])
        if oc == 0:
            wait_tok("tensor", e_wo)
        def ymm(tt):
            mm = None
            for dd in range(DC):
                for cc in range(CC):
                    mm = tt.matmul(ps_y[:, dd, :], wout_v[:, cc, dd * 128:(dd + 1) * 128],
                                   s_all[:, cc, :], start=(cc == 0), stop=(cc == CC - 1))
            return mm
        e_ymm = opi("tensor", ymm, sT)
        p2sT[oc] = e_ymm
        aT[oc] = sT.n

    def genB(oc):
        par = oc % 2
        e_ymm = p2sT[oc]
        wait("vector", sT, e_ymm)
        for dd in range(DC):
            e_ye = opi("vector", lambda v, dd=dd: v.tensor_copy(
                out=y_sb[:, dd, :], in_=ps_y[:, dd, :]), sV)
        yeV[oc] = e_ye
        wait("tensor", sV, e_ye)
        wait("vector", sV, P2_STATICS_V)
        if oc >= 1:
            wait_tok("vector", x1BO[oc - 1])
        e_x1 = None
        BW = min(4, DC)
        W = BW * 128
        for g in range(G):
            for hdc in range(0, DC, BW):
                for k in range(BW):
                    dd = hdc + k
                    e_yt = opi("tensor", lambda tt, dd=dd, g=g, k=k: tt.transpose(
                        ps_tf[:, k * 128:(k + 1) * 128], y_sb[:, dd, g * 128:(g + 1) * 128],
                        ident_f[:]), sT)
                wait("vector", sT, e_yt)
                c0 = hdc * 128
                op("vector", lambda v, g=g, c0=c0, W=W, p=par: v.tensor_add(
                    out=scrap[p][:, g, c0:c0 + W], in0=ps_tf[:, 0:W],
                    in1=hin[p][:, g, c0:c0 + W]))
                op("vector", lambda v, g=g, c0=c0, W=W, p=par: v.tensor_mul(
                    out=scrap[p][:, g, c0:c0 + W], in0=scrap[p][:, g, c0:c0 + W],
                    in1=bc3[:, 2, c0:c0 + W]))
                e_x1 = opi("vector", lambda v, g=g, c0=c0, W=W, p=par: v.tensor_add(
                    out=x1[p][:, g, c0:c0 + W], in0=scrap[p][:, g, c0:c0 + W],
                    in1=x_tok[p][:, g, c0:c0 + W]), sV)
                if g * DC * 128 + c0 + W < G * D:
                    wait("tensor", sV, e_x1)
        p2hin[oc] = e_x1
        wait("sync", sV, e_x1)
        x1BO[oc] = (sbop[par], opi("sync", lambda e, p=par, oc=oc: e.dma_start(
            out=x1_b[oc * Tc:(oc + 1) * Tc, :].rearrange("(g p) d -> p g d", p=128),
            in_=x1[p][:]), sbop[par], 16))
        yield "x1"
        wait("scalar", sV, e_x1)
        if oc >= 1:
            wait("scalar", sV, GB[oc - 1]["statV"])
        for g in range(G):
            e_sq = opi("scalar", lambda s, g=g, p=par: s.activation(
                out=scrap[p][:, g, :], in_=x1[p][:, g, :], func=AF.Square,
                accum_out=stat[:, 2 + g:3 + g]), sS)
        for g in range(G):
            e_s1 = opi("vector", lambda v, g=g, p=par: v.reduce_sum(
                stat[:, 0 + g:1 + g], x1[p][:, g, :], axis=AX.X), sV)
        yield "ln"
        wait("vector", sV, e_s1)
        wait("vector", sS, e_sq)
        for g in range(G):
            op("vector", lambda v, g=g: v.tensor_scalar(out=stat[:, 4 + g:5 + g],
                in0=stat[:, 0 + g:1 + g], scalar1=1.0 / D, scalar2=None, op0=ALU.mult))
        for g in range(G):
            op("vector", lambda v, g=g: v.tensor_scalar(out=stat[:, 6 + g:7 + g],
                in0=stat[:, 2 + g:3 + g], scalar1=1.0 / D, scalar2=None, op0=ALU.mult))
        for g in range(G):
            op("vector", lambda v, g=g: v.tensor_mul(out=stat[:, 8 + g:9 + g],
                in0=stat[:, 4 + g:5 + g], in1=stat[:, 4 + g:5 + g]))
        for g in range(G):
            e_var = opi("vector", lambda v, g=g: v.tensor_sub(out=stat[:, 6 + g:7 + g],
                in0=stat[:, 6 + g:7 + g], in1=stat[:, 8 + g:9 + g]), sV)
        GB.setdefault(oc, {})["statV"] = e_var
        wait("scalar", sV, e_var)
        for g in range(G):
            e_std = opi("scalar", lambda s, g=g: s.activation(out=stat[:, 8 + g:9 + g],
                in_=stat[:, 6 + g:7 + g], func=AF.Sqrt, bias=eps5[:]), sS)
        yield "std"
        wait("vector", sS, e_std)
        for g in range(G):
            e_rst = opi("vector", lambda v, g=g: v.reciprocal(out=stat[:, 10 + g:11 + g],
                in_=stat[:, 8 + g:9 + g]), sV)
        wait("vector", sV, e_rst)
        if oc >= 2 and h2BO.get(oc - 2):
            wait_tok("vector", h2BO[oc - 2])
        for g in range(G):
            op("vector", lambda v, g=g, p=par: v.tensor_scalar(out=scrap[p][:, g, :],
                in0=x1[p][:, g, :], scalar1=stat[:, 4 + g:5 + g], scalar2=stat[:, 10 + g:11 + g],
                op0=ALU.subtract, op1=ALU.mult))
        for g in range(G):
            op("vector", lambda v, g=g, p=par: v.tensor_mul(out=scrap[p][:, g, :],
                in0=scrap[p][:, g, :], in1=bc3[:, 0, :]))
        for g in range(G):
            e_h2 = opi("vector", lambda v, g=g, p=par: v.tensor_add(out=xg_tok[p][:, g, :],
                in0=scrap[p][:, g, :], in1=bc3[:, 1, :]), sV)
        GB[oc]["bS"] = sS.n
        wait("sync", sV, e_h2)
        h2BO[oc] = (sbop[par], opi("sync", lambda e, p=par, oc=oc: e.dma_start(
            out=h2tok_d[oc * Tc:(oc + 1) * Tc, :].rearrange("(g p) d -> p g d", p=128),
            in_=xg_tok[p][:]), sbop[par], 16))
        yield "h2"
        wait("tensor", sV, e_h2)
        for b0 in range(0, G * DC, 4):
            for k in range(4):
                idx = b0 + k
                g, dc = idx // DC, idx % DC
                e_tp = opi("tensor", lambda tt, g=g, dc=dc, k=k, p=par: tt.transpose(
                    ps_t[:, k * 128:(k + 1) * 128], xg_tok[p][:, g, dc * 128:(dc + 1) * 128],
                    ident_b[:]), sT)
            wait("vector", sT, e_tp)
            for k in range(4):
                idx = b0 + k
                g, dc = idx // DC, idx % DC
                col = oc * Tc + g * 128
                e_h2T = opi("vector", lambda v, g=g, dc=dc, k=k, col=col: v.tensor_copy(
                    out=h2T[:, dc, col:col + 128], in_=ps_t[:, k * 128:(k + 1) * 128]), sV)
            if b0 + 4 < G * DC:
                wait("tensor", sV, e_h2T)
            yield "tp"
        for g in range(G):
            col = oc * Tc + g * 128
            def gatemm(tt, col=col):
                mm = None
                for dc in range(DC):
                    mm = tt.matmul(ps_g, h2T[:, dc, col:col + 128], wg_bf[:, dc, :],
                                   start=(dc == 0), stop=(dc == DC - 1))
                return mm
            wait("tensor", sV, e_h2T)
            if e_wT_evac:
                wait("tensor", sV, e_wT_evac[-1])
            e_gl = opi("tensor", gatemm, sT)
            wait("vector", sT, e_gl)
            e_z = opi("vector", lambda v: v.tensor_add(out=gz[:], in0=ps_g, in1=bg_bc[:]), sV)
            gateZ[oc] = e_z
            wait("vector", sV, e_z)
            e_nm = opi("vector", lambda v: v.reduce_max(stat[:, 0:1], gz[:], axis=AX.X, negate=True), sV)
            wait("scalar", sV, e_nm)
            e_ex = opi("scalar", lambda s: s.activation(out=srt8[:], in_=gz[:], func=AF.Exp,
                bias=stat[:, 0:1], accum_out=stat[:, 1:2]), sS)
            wait("vector", sS, e_ex)
            e_rc = opi("vector", lambda v: v.reciprocal(out=stat[:, 2:3], in_=stat[:, 1:2]), sV)
            wait("vector", sV, e_rc)
            e_gt = opi("vector", lambda v: v.tensor_scalar(out=gz[:], in0=srt8[:],
                scalar1=stat[:, 2:3], scalar2=None, op0=ALU.mult), sV)
            wait("vector", sV, e_gt)
            e_m8 = opi("vector", lambda v: v.max(out=srt8[:], in_=gz[:]), sV)
            wait("vector", sV, e_m8)
            op("vector", lambda v: v.tensor_scalar(out=wfin[:], in0=gz[:],
                scalar1=srt8[:, 1:2], scalar2=None, op0=ALU.subtract))
            op("vector", lambda v: v.tensor_add(out=stat[:, 3:4], in0=srt8[:, 0:1], in1=srt8[:, 1:2]))
            op("vector", lambda v: v.tensor_scalar(out=wfin[:], in0=wfin[:],
                scalar1=0.0, scalar2=None, op0=ALU.is_ge))
            op("vector", lambda v: v.tensor_scalar(out=stat[:, 3:4], in0=stat[:, 3:4],
                scalar1=1e-8, scalar2=None, op0=ALU.add))
            op("vector", lambda v: v.tensor_mul(out=wfin[:], in0=wfin[:], in1=gz[:]))
            e_dn = opi("vector", lambda v: v.reciprocal(out=stat[:, 4:5], in_=stat[:, 3:4]), sV)
            wait("vector", sV, e_dn)
            e_wf = opi("vector", lambda v: v.tensor_scalar(out=wfin[:], in0=wfin[:],
                scalar1=stat[:, 4:5], scalar2=None, op0=ALU.mult), sV)
            wait("tensor", sV, e_wf)
            e_wt = opi("tensor", lambda tt: tt.transpose(ps_tf[0:E, 0:128], wfin[:], ident_f[:]), sT)
            wait("vector", sT, e_wt)
            e_wT_evac.append(opi("vector", lambda v, col=col: v.tensor_copy(
                out=wT_sb[:, col:col + 128], in_=ps_tf[0:E, 0:128]), sV))
            yield "gate"
        GB[oc]["wT"] = True
        bV[oc] = sV.n

    e_wT_evac = []
    import os
    OVL_STAGES = os.environ.get("P2_OVL_STAGES", "ln,std,h2").split(",")
    gB = None
    for oc in range(NOWN):
        gA = genA(oc)
        for _ in gA:
            if gB is not None:
                try:
                    nt = next(gB)
                    if nt not in OVL_STAGES:
                        while True:
                            next(gB)
                except StopIteration:
                    gB = None
        while gB is not None:
            try:
                next(gB)
            except StopIteration:
                gB = None
        gB = genB(oc)
        while True:
            tag = next(gB, None)
            if tag is None:
                gB = None
                break
            if tag == "x1":
                break
    while gB is not None:
        try:
            next(gB)
        except StopIteration:
            gB = None

    P2_V_END = sV.n
    P2_T_END = sT.n
    wait("sync", sV, P2_V_END)
    e_wd = (sod, opi("sync", lambda e: e.dma_start(out=w_d, in_=wT_sb[:]), sod, 16))

    stkS.close()
    r8 = lambda tt_: tt_[0:8, :, :].rearrange("p a b -> p (a b)")
    ones8 = r8(scrap[0])
    mask8 = r8(scrap[1])
    pref8 = r8(x_tok[0])
    slotf = r8(x_tok[1])
    tmpA = r8(hin_t)
    tmpB = r8(y_sb)
    trrow = r8(x1_t)
    stk25 = ExitStack()
    sb25 = lambda n, shp, dt=F32: stk25.enter_context(nc.sbuf_tensor(n, list(shp), dt))
    slot_tok_f = sb25("slot_tok_f", (128, NGRP, E))
    slot_tok_i = sb25("slot_tok_i", (128, NGRP, E), I32)
    idsg = sb25("idsg", (128, NGRP), I32)
    izero = sb25("izero", (1, 512), I32)
    bzero = sb25("bzero", (1, D), BF16)
    fzero = sb25("fzero", (8, 128))

    wait("gpsimd", sV, P2_V_END)
    wait("gpsimd", sS, sS.n)
    for oc_ in range(max(0, NOWN - 2), NOWN):
        wait_tok("gpsimd", x1BO[oc_])
        if h2BO.get(oc_):
            wait_tok("gpsimd", h2BO[oc_])
    op("gpsimd", lambda g: g.memset(ones8, 1.0))
    op("gpsimd", lambda g: g.iota(trrow, [[1, SH]], base=0, channel_multiplier=0,
                                  allow_small_or_imprecise_dtypes=True))
    op("gpsimd", lambda g: g.iota(idsg[:], [[128, NGRP]], base=0, channel_multiplier=1))
    op("gpsimd", lambda g: g.memset(izero[:], SH))
    op("gpsimd", lambda g: g.memset(bzero[:], 0.0))
    e_prep = opi("gpsimd", lambda g: g.memset(fzero[:], 0.0), sG)
    assert (2 * SH) % 512 == 0
    wait("gpsimd", sT, P2_T_END)
    e_zi = gq.dma(lambda g: g.dma_start(
        out=idx_d, in_=bass.AP(izero[:].tensor, 0, [[1, 1], [0, E * ((2 * SH) // 512)], [1, 512]])))
    e_zh = gq.dma(lambda g: g.dma_start(
        out=h2tok_d[SH:SH + 128, :], in_=bass.AP(bzero[:].tensor, 0, [[1, 1], [0, 128], [1, D]])))
    wait_tok("gpsimd", e_wd)
    e_wp = gq.dma(lambda g: g.dma_start(out=wpad_d[:, 0:SH], in_=w_d))
    e_wz = gq.dma(lambda g: g.dma_start(out=wpad_d[:, SH:SH + 128], in_=fzero[:]))
    e_za = gq.dma(lambda g: g.dma_start(
        out=acc_d[SH:SH + 128, :],
        in_=bass.AP(bzero[:].tensor, 0, [[1, 1], [0, 128], [1, D]])))

    wait("vector", sG, e_prep)
    e_msk = opi("vector", lambda v: v.tensor_scalar(out=mask8, in0=wT_sb[:],
        scalar1=0.0, scalar2=None, op0=ALU.is_gt), sV)
    wait("vector", sV, e_msk)
    e_pre = opi("vector", lambda v: v.tensor_tensor_scan(out=pref8, data0=ones8,
        data1=mask8, initial=0.0, op0=ALU.mult, op1=ALU.add), sV)
    wait("vector", sV, e_pre)
    op("vector", lambda v: v.tensor_scalar(out=slotf, in0=pref8, scalar1=-1.0,
        scalar2=None, op0=ALU.add))
    op("vector", lambda v: v.tensor_scalar(out=tmpA, in0=mask8, scalar1=-1.0,
        scalar2=None, op0=ALU.mult))
    op("vector", lambda v: v.tensor_mul(out=slotf, in0=slotf, in1=mask8))
    op("vector", lambda v: v.tensor_scalar(out=tmpA, in0=tmpA, scalar1=1.0,
        scalar2=None, op0=ALU.add))
    op("vector", lambda v: v.tensor_scalar(out=tmpB, in0=trrow, scalar1=float(SH),
        scalar2=None, op0=ALU.add))
    op("vector", lambda v: v.tensor_mul(out=tmpB, in0=tmpB, in1=tmpA))
    e_slot = opi("vector", lambda v: v.tensor_add(out=slotf, in0=slotf, in1=tmpB), sV)
    wait("tensor", sV, e_slot)
    for k in range(NGRP):
        kp = k % 2
        if k >= 2:
            wait("tensor", sV, e_ste[k - 2])
        e_stp = opi("tensor", lambda tt_, k=k, kp=kp: tt_.transpose(
            ps_tf[:, kp * 8:kp * 8 + 8], slotf[0:8, k * 128:(k + 1) * 128],
            ident_f[0:8, 0:8]), sT)
        wait("vector", sT, e_stp)
        if k == 0:
            e_ste = {}
        e_ste[k] = opi("vector", lambda v, k=k, kp=kp: v.tensor_copy(
            out=slot_tok_f[:, k, :], in_=ps_tf[:, kp * 8:kp * 8 + 8]), sV)
    wait("vector", sV, e_ste[NGRP - 1])
    e_sti = opi("vector", lambda v: v.tensor_copy(out=slot_tok_i[:], in_=slot_tok_f[:]), sV)
    wait("gpsimd", sV, e_sti)
    wait_tok("gpsimd", e_zi)
    idsc_tok = {}
    for e_i in range(E):
        for k in range(NGRP):
            tok = gq.dma(lambda g, e_i=e_i, k=k: g.indirect_dma_start(
                out=idx_d.rearrange("e (a o) -> (e a) o", o=1),
                out_offset=IOA(slot_tok_i[:, k, e_i:e_i + 1], 0),
                in_=idsg[:, k:k + 1], in_offset=None,
                element_offset=e_i * 2 * SH))
        idsc_tok[e_i] = tok

    stk25.close()
    stkA.close()
    p3stk = ExitStack()
    sb3 = lambda n, shp, dt=F32: p3stk.enter_context(nc.sbuf_tensor(n, list(shp), dt))
    b1_sb = [sb3("b1_sb%d" % i, (128, FC)) for i in range(2)]
    idxe = [sb3("idxe%d" % i, (128, CSUB), I32) for i in range(2)]
    wq = [sb3("wq%d" % i, (128, CSUB)) for i in range(2)]
    yacc = sb3("yacc", (128, CSUB, D))
    yout = [sb3("yout%d" % i, (128, D)) for i in range(2)]
    stkB2 = ExitStack()
    b2_bf = stkB2.enter_context(nc.sbuf_tensor("b2_bf", [E, D], BF16))
    wT_bf = stkB2.enter_context(nc.sbuf_tensor("wT_bf", [E, SH], BF16))
    p3istk = ExitStack()
    sb3i = lambda n, shp, dt=F32: p3istk.enter_context(nc.sbuf_tensor(n, list(shp), dt))

    for sm in gq.sems:
        if sm.n:
            wait("gpsimd", sm, sm.n)
    wait("gpsimd", sV, sV.n)
    e_b2 = gq.dma(lambda g: g.dma_start(out=b2_bf[:], in_=b2))
    e_wtb = gq.dma(lambda g: g.dma_start(out=wT_bf[:], in_=w_d))

    wait_tok("tensor", e_wtb)
    wait_tok("tensor", e_b2)
    wait("tensor", sT, P2_T_END)
    binit_tok = []
    for tg in range(NGRP):
        bp = tg % 2
        pw = ps_y[:, bp * 4:(bp + 1) * 4, :].rearrange("p a b -> p (a b)")[:, 0:D]
        if tg >= 2:
            wait("tensor", sV, e_bev[tg - 2])
        def bmm(tt_, tg=tg, pw=pw):
            mm = None
            for dh in range(0, D, 512):
                dw = min(512, D - dh)
                mm = tt_.matmul(pw[:, dh:dh + dw], wT_bf[0:E, tg * 128:(tg + 1) * 128],
                                b2_bf[0:E, dh:dh + dw], start=True, stop=True)
            return mm
        e_bmm = opi("tensor", bmm, sT)
        wait("vector", sT, e_bmm)
        if tg >= 2:
            wait_tok("vector", binit_tok[tg - 2])
        if tg == 0:
            e_bev = {}
        e_bev[tg] = opi("vector", lambda v, tg=tg, bp=bp, pw=pw: v.tensor_copy(
            out=yout[bp][:], in_=pw), sV)
        wait("sync", sV, e_bev[tg])
        binit_tok.append((sxp[bp], opi("sync", lambda e, tg=tg, bp=bp: e.dma_start(
            out=acc_d[tg * 128:(tg + 1) * 128, :], in_=yout[bp][:]), sxp[bp], 16)))

    BINIT_T = sT.n
    stkB2.close()
    Xtok = [sb3i("Xtok%d" % i, (128, CSUB, D), BF16) for i in range(2)]
    X_T = sb3i("X_T", (128, DC, CAP), BF16)
    H_all = sb3i("H_all", (128, 2, FQC, Tc), BF16)

    gather_tok = {}
    trspV = {}

    def issue_gathers(e_i):
        ep = e_i % 2
        wait_tok("gpsimd", idsc_tok[e_i])
        for tok in (e_zh, e_wp, e_wz):
            wait_tok("gpsimd", tok)
        for oc_ in range(max(0, NOWN - 2), NOWN):
            if h2BO.get(oc_):
                wait_tok("gpsimd", h2BO[oc_])
        wait("gpsimd", sT, BINIT_T)
        if e_i >= 2:
            wait("gpsimd", sT, trspV[e_i - 2])
            wait("gpsimd", sV, wqread[e_i - 2])
        li = gq.dma(lambda g, e_i=e_i, ep=ep: g.dma_start(
            out=idxe[ep][:], in_=idx_d[e_i][0:CAP].rearrange("(p c) -> p c", p=128)))
        wait_tok("gpsimd", li)
        last = None
        for c in range(CSUB):
            last = gq.dma(lambda g, e_i=e_i, ep=ep, c=c: g.indirect_dma_start(
                out=Xtok[ep][:, c, :], out_offset=None,
                in_=h2tok_d, in_offset=IOA(idxe[ep][:, c:c + 1], 0)))
            gq.dma(lambda g, e_i=e_i, ep=ep, c=c: g.indirect_dma_start(
                out=wq[ep][:, c:c + 1], out_offset=None,
                in_=wpad_d.rearrange("e (a o) -> (e a) o", o=1),
                in_offset=IOA(idxe[ep][:, c:c + 1], 0),
                element_offset=e_i * (SH + 128)))
        gather_tok[e_i] = (gq.sems[(gq.i - 1) % len(gq.sems)],
                           gq.sems[(gq.i - 1) % len(gq.sems)].n)
        gather_tok[e_i] = [(sm, sm.n) for sm in gq.sems]

    wqread = {}
    issue_gathers(0)

    pairs = [(e, q) for e in range(E) for q in range(NQ)]
    loaded = {}
    w2T_hist = {}
    prev_accum_T = {}
    b1_used = {}
    sadd = Sem(sstk, nc, "sadd")
    sadd_tokens = []

    def issue_pair_loads(idx):
        e_i, q = pairs[idx]
        wpar = idx % 2
        pt = prev_accum_T.get(wpar)
        wait("gpsimd", sT, pt if pt else P2_T_END)
        lw1 = gq.dma(lambda g, e_i=e_i, q=q, wpar=wpar: g.dma_start(
            out=w1q_v[wpar], in_=w1[e_i][:, q * FQ:(q + 1) * FQ]
            .rearrange("(c p) f -> p c f", p=128)))
        lw2 = gq.dma(lambda g, e_i=e_i, q=q, wpar=wpar: g.dma_start(
            out=w2q_v[wpar], in_=w2[e_i][q * FQ:(q + 1) * FQ, :]
            .rearrange("(c p) d -> p c d", p=128)))
        loaded[idx] = (lw1, lw2)

    def issue_expert_smalls(e_i):
        ep = e_i % 2
        if e_i >= 2:
            wait("gpsimd", sS, b1_used[e_i - 2])
        eb1 = gq.dma(lambda g, e_i=e_i, ep=ep: g.dma_start(
            out=b1_sb[ep][:], in_=b1T[e_i]))
        return eb1

    eb1t = {0: issue_expert_smalls(0)}
    issue_pair_loads(0)
    prev_w2T = None
    prev_tq = None
    for idx, (e_i, q) in enumerate(pairs):
        wpar = idx % 2
        ep = e_i % 2
        if q == 0:
            for tok in gather_tok[e_i]:
                wait("tensor", tok[0], tok[1])
            e_xte = None
            for b0 in range(0, CSUB * DC, 4):
                for k in range(4):
                    i2 = b0 + k
                    c, dc = i2 // DC, i2 % DC
                    e_xtp = opi("tensor", lambda tt_, c=c, dc=dc, k=k, ep=ep: tt_.transpose(
                        ps_t[:, k * 128:(k + 1) * 128], Xtok[ep][:, c, dc * 128:(dc + 1) * 128],
                        ident_b[:]), sT)
                wait("vector", sT, e_xtp)
                for k in range(4):
                    i2 = b0 + k
                    c, dc = i2 // DC, i2 % DC
                    e_xte = opi("vector", lambda v, c=c, dc=dc, k=k: v.tensor_copy(
                        out=X_T[:, dc, c * 128:(c + 1) * 128],
                        in_=ps_t[:, k * 128:(k + 1) * 128]), sV)
                if b0 + 4 < CSUB * DC:
                    wait("tensor", sV, e_xte)
            trspV[e_i] = sT.n
            XTready = e_xte
            if e_i + 1 < E:
                eb1t[e_i + 1] = issue_expert_smalls(e_i + 1)
                issue_gathers(e_i + 1)
        if idx + 1 < len(pairs):
            issue_pair_loads(idx + 1)
        lw1, lw2 = loaded[idx]
        wait_tok("tensor", lw1)
        wait_tok("tensor", lw2)
        wait("tensor", sV, XTready)
        if q == 0:
            wait_tok("scalar", eb1t[e_i])
        for tg in range(CTG):
            tpar = (idx * CTG + tg) % 2
            if idx * CTG + tg >= 2:
                wait("scalar", sT, w2T_hist[(idx * CTG + tg) - 2])
            for fc in range(FQC):
                hp = fc % 2
                if fc >= 2:
                    wait("tensor", sS, e_gelu[fc - 2])
                def w1mm(ttn, fc=fc, hp=hp, wpar=wpar, tg=tg, ep=ep):
                    mm = None
                    for dc in range(DC):
                        mm = ttn.matmul(ps_h[:, hp, 0:Tc], w1q_v[wpar][:, dc, fc * 128:(fc + 1) * 128],
                                        X_T[:, dc, tg * Tc:(tg + 1) * Tc],
                                        start=(dc == 0), stop=(dc == DC - 1))
                    return mm
                e_w1 = opi("tensor", w1mm, sT)
                wait("scalar", sT, e_w1)
                if fc == 0:
                    e_gelu = {}
                bcol = q * FQC + fc
                e_gelu[fc] = opi("scalar", lambda sc, hp=hp, fc=fc, bcol=bcol, ep=ep, tpar=tpar: sc.activation(
                    out=H_all[:, tpar, fc, :], in_=ps_h[:, hp, 0:Tc], func=AF.Gelu_apprx_tanh,
                    bias=b1_sb[ep][:, bcol:bcol + 1]), sS)
                b1_used[e_i] = e_gelu[fc]
            wait("tensor", sS, e_gelu[FQC - 1])
            for tsub in range(2):
                ypar = (idx * CTG * 2 + tg * 2 + tsub) % 2
                pw = ps_y[:, ypar * 4:(ypar + 1) * 4, :].rearrange("p a b -> p (a b)")[:, 0:D]
                if prev_tq is not None and idx * CTG * 2 + tg * 2 + tsub >= 2:
                    wait("tensor", sV, ev_hist[(idx * CTG * 2 + tg * 2 + tsub) - 2])
                def w2mm(ttn, wpar=wpar, tpar=tpar, tsub=tsub, pw=pw):
                    mm = None
                    for dh in range(0, D, 512):
                        dw = min(512, D - dh)
                        for fc in range(FQC):
                            mm = ttn.matmul(pw[:, dh:dh + dw],
                                            H_all[:, tpar, fc, tsub * 128:(tsub + 1) * 128],
                                            w2q_v[wpar][:, fc, dh:dh + dw],
                                            start=(fc == 0), stop=(fc == FQC - 1))
                    return mm
                e_w2 = opi("tensor", w2mm, sT)
                w2T_hist[idx * CTG + tg] = e_w2
                wait("vector", sT, e_w2)
                csub = tg * 2 + tsub
                if q == 0:
                    ev = opi("vector", lambda v, csub=csub, pw=pw: v.tensor_copy(
                        out=yacc[:, csub, :], in_=pw), sV)
                else:
                    ev = opi("vector", lambda v, csub=csub, pw=pw: v.tensor_add(
                        out=yacc[:, csub, :], in0=yacc[:, csub, :], in1=pw), sV)
                if prev_tq is None:
                    ev_hist = {}
                    prev_tq = 1
                ev_hist[idx * CTG * 2 + tg * 2 + tsub] = ev
        if q == NQ - 1:
            for tok in gather_tok[e_i]:
                wait("vector", tok[0], tok[1])
            for csub in range(CSUB):
                yp = csub % 2
                if csub >= 2:
                    wait("vector", sadd, sadd_val[csub - 2])
                e_yt = opi("vector", lambda v, csub=csub, yp=yp, ep=ep: v.tensor_scalar(
                    out=yout[yp][:], in0=yacc[:, csub, :], scalar1=wq[ep][:, csub:csub + 1],
                    scalar2=None, op0=ALU.mult), sV)
                wait("gpsimd", sV, e_yt)
                if sadd.n:
                    wait("gpsimd", sadd, sadd.n)
                else:
                    for btok in binit_tok:
                        wait_tok("gpsimd", btok)
                    wait_tok("gpsimd", e_za)
                sadd.n += 16
                sv_ = sadd.n

                def do_sadd(g, csub=csub, yp=yp, ep=ep, s=sadd.h):
                    return g.indirect_dma_start(
                        out=acc_d, out_offset=IOA(idxe[ep][:, csub:csub + 1], 0),
                        in_=yout[yp][:], in_offset=None,
                        compute_op=ALU.add).then_inc(s, 16)
                ctx.q["gpsimd"].append(do_sadd)
                if csub == 0:
                    sadd_val = {}
                sadd_val[csub] = sv_
            wqread[e_i] = sV.n
        prev_accum_T[wpar] = sT.n
    P3_G_END = sG.n
    P3_V_END = sV.n
    SADD_END = sadd.n

    p3istk.close()
    accP4 = [p3stk.enter_context(nc.sbuf_tensor("accP4_%d" % i, [128, G, D], F32)) for i in range(2)]
    x1p4 = [p3stk.enter_context(nc.sbuf_tensor("x1p4_%d" % i, [128, G, D], F32)) for i in range(2)]
    p4tmp = p3stk.enter_context(nc.sbuf_tensor("p4tmp", [128, 512], F32))

    e_gb = bcast(bc3[:, 0, :], mod_d, 5 * D, D)
    wait_tok("vector", e_gb)
    wait("vector", sV, P3_V_END)
    e_gbp = 0
    W4 = min(512, D)
    for c0 in range(0, D, W4):
        op("vector", lambda v, c0=c0: v.tensor_scalar(out=p4tmp[:, 0:W4], in0=bc3[:, 0, c0:c0 + W4],
            scalar1=0.0, scalar2=None, op0=ALU.is_ge))
        op("vector", lambda v, c0=c0: v.tensor_scalar(out=p4tmp[:, 0:W4], in0=p4tmp[:, 0:W4],
            scalar1=2.0, scalar2=None, op0=ALU.mult))
        op("vector", lambda v, c0=c0: v.tensor_scalar(out=p4tmp[:, 0:W4], in0=p4tmp[:, 0:W4],
            scalar1=-1.0, scalar2=None, op0=ALU.add))
        op("vector", lambda v, c0=c0: v.tensor_mul(out=bc3[:, 0, c0:c0 + W4],
            in0=bc3[:, 0, c0:c0 + W4], in1=p4tmp[:, 0:W4]))
        op("vector", lambda v, c0=c0: v.tensor_scalar_max(bc3[:, 0, c0:c0 + W4], bc3[:, 0, c0:c0 + W4], 0.25))
        e_gbp = opi("vector", lambda v, c0=c0: v.tensor_mul(out=bc3[:, 0, c0:c0 + W4],
            in0=bc3[:, 0, c0:c0 + W4], in1=p4tmp[:, 0:W4]), sV)

    p4st = [None] * NTT
    p4V = [0] * NTT
    for tt in range(NTT):
        par = tt % 2
        if tt >= 2:
            wait_tok("sync", p4st[tt - 2])
            wait("sync", sV, p4V[tt - 2])
        elif tt == 0:
            wait("sync", sV, P3_V_END)
            wait("sync", sadd, SADD_END)
        ln1_ = opi("sync", lambda e, p=par, tt=tt: e.dma_start(
            out=x1p4[p][:], in_=x1_b[tt * Tc:(tt + 1) * Tc, :].rearrange("(g p) d -> p g d", p=128)), sxp[par], 16)
        ln2_ = opi("sync", lambda e, p=par, tt=tt: e.dma_start(
            out=accP4[p][:], in_=acc_d[tt * Tc:(tt + 1) * Tc, :].rearrange("(g p) d -> p g d", p=128)), sxp[par], 16)
        wait("vector", sxp[par], ln2_)
        wait("vector", sV, e_gbp)
        wait("vector", sG, P3_G_END)
        e_cmb = None
        for g in range(G):
            op("vector", lambda v, g=g, p=par: v.tensor_mul(
                out=accP4[p][:, g, :], in0=accP4[p][:, g, :], in1=bc3[:, 0, :]))
        for g in range(G):
            e_cmb = opi("vector", lambda v, g=g, p=par: v.tensor_add(
                out=x1p4[p][:, g, :], in0=x1p4[p][:, g, :], in1=accP4[p][:, g, :]), sV)
        p4V[tt] = e_cmb
        wait("sync", sV, e_cmb)
        p4st[tt] = (soutp[par], opi("sync", lambda e, p=par, tt=tt: e.dma_start(
            out=out[tt * Tc:(tt + 1) * Tc, :].rearrange("(g p) d -> p g d", p=128),
            in_=x1p4[p][:]), soutp[par], 16))

    for sp in soutp:
        if sp.n:
            wait("gpsimd", sp, sp.n)

    with nc.Block() as block:
        def runner(lst):
            def run(e):
                for fn in lst:
                    fn(e)
            return run
        block.sync(runner(ctx.q["sync"]))
        block.tensor(runner(ctx.q["tensor"]))
        block.vector(runner(ctx.q["vector"]))
        block.scalar(runner(ctx.q["scalar"]))
        block.gpsimd(runner(ctx.q["gpsimd"]))

    p3stk.close()
    sstk.close()
    top.close()
    return nc


def make_in_maps(cfg, inputs):
    D, H, F, E, S, Tc = (cfg[k] for k in ("D", "H", "F", "E", "S", "Tc"))
    NCH = S // Tc
    FC = F // 128
    x = np.ascontiguousarray(inputs["L0_attn_x"], dtype=np.float32)
    B = x.shape[0]
    SH = S // 2
    c = np.ascontiguousarray(inputs["L0_c"], dtype=np.float32)
    shared = {
        "rms_row": np.ascontiguousarray(inputs["L0_attn_rms_scale"].reshape(1, D)),
        "split_row": np.ascontiguousarray(inputs["L0_attn_split_scale"].reshape(1, D)),
        "bg_row": np.ascontiguousarray(inputs["L0_ffn_bg"].reshape(1, E)),
        "wmod": np.ascontiguousarray(inputs["L0_wmod"]),
        "wk": np.ascontiguousarray(inputs["L0_attn_W_K"]),
        "wugg": np.ascontiguousarray(inputs["L0_attn_W_ugg"]),
        "wout": np.ascontiguousarray(inputs["L0_attn_W_out"]),
        "wg": np.ascontiguousarray(inputs["L0_ffn_Wg"]),
        "w1": np.ascontiguousarray(inputs["L0_ffn_W1"]),
        "b1T": np.ascontiguousarray(
            inputs["L0_ffn_b1"].reshape(E, FC, 128).transpose(0, 2, 1)),
        "w2": np.ascontiguousarray(inputs["L0_ffn_W2"]),
        "b2": np.ascontiguousarray(inputs["L0_ffn_b2"]),
    }
    ones = np.ones((128, NCH), np.float32)
    mask_j0 = ones.copy()
    mask_j0[:, :NCH // 2] = 0.0
    in_maps = []
    for core in range(2 * B):
        b, j = core // 2, core % 2
        if j == 0:
            xs = np.concatenate([x[b, 0:SH], x[b, 0:SH]], axis=0)
            mk = mask_j0
        else:
            xs = np.concatenate([x[b, 0:SH], x[b, SH:S]], axis=0)
            mk = ones
        m = dict(shared)
        m["x_seq"] = np.ascontiguousarray(xs)
        m["mask16"] = np.ascontiguousarray(mk)
        m["cvecT"] = np.ascontiguousarray(c[b].reshape(D // 128, 128).T)
        in_maps.append(m)
    return in_maps


def kernel(**inputs):
    cfg = FULL_CFG
    nc = build_program(cfg)
    in_maps = make_in_maps(cfg, inputs)
    res = run_bass_kernel_spmd(nc, in_maps, core_ids=list(range(len(in_maps))))
    B = inputs["L0_attn_x"].shape[0]
    S, D = cfg["S"], cfg["D"]
    out = np.empty((B, S, D), dtype=np.float32)
    for core in range(2 * B):
        b, j = core // 2, core % 2
        out[b, j * (S // 2):(j + 1) * (S // 2)] = res.results[core]["out"]
    return out
